# version 04
# baseline (speedup 1.0000x reference)
import numpy as np
import ml_dtypes
import contextlib
import concourse.bass as bass
import concourse.mybir as mybir
from concourse.bass_utils import run_bass_kernel_spmd

F32 = mybir.dt.float32
BF16 = mybir.dt.bfloat16
U8 = mybir.dt.uint8
AF = mybir.ActivationFunctionType
ALU = mybir.AluOpType
AX = mybir.AxisListType

NCORES = 8
NT = 4096
TBN = 32
D = 1024
DCH = 8
FF = 2048
FCH = 16
E = 8
H = 16
DH = 64
NW = 8
WTB = 4
TWIN = 512
NSUB = 4
CAP = 4096
NBIS = 26

STAGES = ("ffn1", "attn", "ffn2")
DEBUG = False
DBG_CALL = [0]


class EM:
    ENGS = ("t", "s", "v", "g", "y")

    def __init__(self, nc):
        self.nc = nc
        self.q = {e: [] for e in self.ENGS}
        self.sems = {}
        self.cnt = {}
        self._es = contextlib.ExitStack()
        self.last_wait = {}

    def sem(self, name):
        if name not in self.sems:
            self.sems[name] = self._es.enter_context(self.nc.semaphore(name))
            self.cnt[name] = 0
        return self.sems[name]

    def wait(self, eng, name, val):
        if val is None or val <= 0:
            return
        self.sem(name)
        assert val <= self.cnt[name], f"wait on future count {name} {val} > {self.cnt[name]}"
        key = (eng, name)
        if self.last_wait.get(key, 0) >= val:
            return
        self.last_wait[key] = val
        s = self.sems[name]
        self.q[eng].append(lambda h, s=s, v=val: h.wait_ge(s, v))

    def waitd(self, eng, dep):
        if dep is not None:
            self.wait(eng, dep[0], dep[1])

    ESEM = {"t": "T", "s": "S", "v": "V", "g": "G", "y": "Y"}

    def chain(self, eng, fn):
        nm = self.ESEM[eng]
        self.sem(nm)
        self.wait(eng, nm, self.cnt[nm])
        return self.op(eng, fn, inc=nm)

    def op(self, eng, fn, inc=None, dmainc=None):
        if inc is not None:
            s = self.sem(inc)
            self.cnt[inc] += 1
            c = self.cnt[inc]
            self.q[eng].append(lambda h, fn=fn, s=s: fn(h).then_inc(s, 1))
            return c
        if dmainc is not None:
            s = self.sem(dmainc)
            self.cnt[dmainc] += 16
            c = self.cnt[dmainc]
            self.q[eng].append(lambda h, fn=fn, s=s: fn(h).then_inc(s, 16))
            return c
        self.q[eng].append(fn)
        return None


def build_nc():
    nc = bass.Bass("TRN2", target_bir_lowering=False)

    xin = nc.dram_tensor("xin", [NT, D], F32, kind="ExternalInput")
    w1 = nc.dram_tensor("w1", [E, D, FF], F32, kind="ExternalInput")
    w2 = nc.dram_tensor("w2", [E, FF, D], F32, kind="ExternalInput")
    wg = nc.dram_tensor("wg", [D, E], F32, kind="ExternalInput")
    wq = nc.dram_tensor("wq", [D, D], F32, kind="ExternalInput")
    wk = nc.dram_tensor("wk", [D, D], F32, kind="ExternalInput")
    wv = nc.dram_tensor("wv", [D, D], F32, kind="ExternalInput")
    wo = nc.dram_tensor("wo", [D, D], F32, kind="ExternalInput")
    bq = nc.dram_tensor("bq", [1, D], F32, kind="ExternalInput")
    bk = nc.dram_tensor("bk", [1, D], F32, kind="ExternalInput")
    bv = nc.dram_tensor("bv", [1, D], F32, kind="ExternalInput")
    bo = nc.dram_tensor("bo", [1, D], F32, kind="ExternalInput")
    identb = nc.dram_tensor("identb", [128, 128], BF16, kind="ExternalInput")
    identf = nc.dram_tensor("identf", [128, 128], F32, kind="ExternalInput")
    onesb = nc.dram_tensor("onesb", [1, 128], BF16, kind="ExternalInput")
    onesf = nc.dram_tensor("onesf", [1, 128], F32, kind="ExternalInput")
    ones128 = nc.dram_tensor("ones128", [128, 128], BF16, kind="ExternalInput")
    onescol = nc.dram_tensor("onescol", [128, 1], BF16, kind="ExternalInput")
    out = nc.dram_tensor("out", [NT, D], F32, kind="ExternalOutput")
    dbg = {}
    if DEBUG:
        dbg["mmean"] = nc.dram_tensor("d_mmean", [128, TBN], F32, kind="ExternalOutput")
        dbg["rstd"] = nc.dram_tensor("d_rstd", [128, TBN], F32, kind="ExternalOutput")
        dbg["probs"] = nc.dram_tensor("d_probs", [128, TBN, E], F32, kind="ExternalOutput")
        dbg["theta"] = nc.dram_tensor("d_theta", [128, E], F32, kind="ExternalOutput")
        dbg["gates"] = nc.dram_tensor("d_gates", [128, TBN, E], F32, kind="ExternalOutput")
        dbg["xnfm"] = nc.dram_tensor("d_xnfm", [128, DCH, TWIN], F32, kind="ExternalOutput")
        dbg["h1g"] = nc.dram_tensor("d_h1g", [128, FCH, TWIN], F32, kind="ExternalOutput")
        dbg["cnt"] = nc.dram_tensor("d_cnt", [128, E], F32, kind="ExternalOutput")
        dbg["sum1"] = nc.dram_tensor("d_sum1", [128, TBN], F32, kind="ExternalOutput")
        dbg["sum2"] = nc.dram_tensor("d_sum2", [128, TBN], F32, kind="ExternalOutput")
        dbg["var"] = nc.dram_tensor("d_var", [128, TBN], F32, kind="ExternalOutput")

    ag_in = nc.dram_tensor("ag_in", [128, 256], F32)
    ag_out = nc.dram_tensor("ag_out", [8 * 128, 256], F32)
    ar_in = nc.dram_tensor("ar_in", [64, 1040], F32)
    kst = nc.dram_tensor("kst", [16, 64], F32)
    ar_out = nc.dram_tensor("ar_out", [64, 1040], F32)

    em = EM(nc)
    es = em._es

    x = es.enter_context(nc.sbuf_tensor("x", [128, TBN, D], F32))
    w1ring = es.enter_context(nc.sbuf_tensor("w1ring", [128, 2, 4096], BF16))
    h1g = es.enter_context(nc.sbuf_tensor("h1g", [128, FCH, TWIN], BF16))
    xn_fm = es.enter_context(nc.sbuf_tensor("xn_fm", [128, DCH, TWIN], BF16))
    w2ring = es.enter_context(nc.sbuf_tensor("w2ring", [128, 3, D], BF16))
    xn_tm = es.enter_context(nc.sbuf_tensor("xn_tm", [128, 2, D], BF16))
    probs_g = es.enter_context(nc.sbuf_tensor("probs_g", [128, 2048], F32))
    s12 = es.enter_context(nc.sbuf_tensor("s12", [128, D], F32))
    s3 = es.enter_context(nc.sbuf_tensor("s3", [128, 512], F32))
    probs_l = es.enter_context(nc.sbuf_tensor("probs_l", [128, TBN, E], F32))
    gates = es.enter_context(nc.sbuf_tensor("gates", [128, TBN, E], F32))
    kv_bf = es.enter_context(nc.sbuf_tensor("kv_bf", [128, 8, DH], BF16))
    sum1 = es.enter_context(nc.sbuf_tensor("sum1", [128, TBN], F32))
    sum2 = es.enter_context(nc.sbuf_tensor("sum2", [128, TBN], F32))
    mmean = es.enter_context(nc.sbuf_tensor("mmean", [128, TBN], F32))
    rstd = es.enter_context(nc.sbuf_tensor("rstd", [128, TBN], F32))
    lo_a = es.enter_context(nc.sbuf_tensor("lo_a", [128, E], F32))
    lo_b = es.enter_context(nc.sbuf_tensor("lo_b", [128, E], F32))
    hi_a = es.enter_context(nc.sbuf_tensor("hi_a", [128, E], F32))
    hi_b = es.enter_context(nc.sbuf_tensor("hi_b", [128, E], F32))
    mid = es.enter_context(nc.sbuf_tensor("mid", [128, E], F32))
    cntf = es.enter_context(nc.sbuf_tensor("cntf", [128, E], F32))
    cntb = es.enter_context(nc.sbuf_tensor("cntb", [128, E], BF16))
    msku = es.enter_context(nc.sbuf_tensor("msku", [128, E], U8))
    rmsc = es.enter_context(nc.sbuf_tensor("rmsc", [128, TBN, E], F32))
    rm1 = es.enter_context(nc.sbuf_tensor("rm1", [128, TBN], F32))
    rm0 = es.enter_context(nc.sbuf_tensor("rm0", [128, TBN], F32))
    rw = es.enter_context(nc.sbuf_tensor("rw", [128, TBN], F32))
    ram = es.enter_context(nc.sbuf_tensor("ram", [128, TBN], F32))
    zdot = es.enter_context(nc.sbuf_tensor("zdot", [128, H], F32))
    zval = es.enter_context(nc.sbuf_tensor("zval", [128, H], F32))
    wg_sb = es.enter_context(nc.sbuf_tensor("wg_sb", [128, DCH, E], BF16))
    bias_sb = es.enter_context(nc.sbuf_tensor("bias_sb", [1, 2, D], BF16))
    id_bf = es.enter_context(nc.sbuf_tensor("id_bf", [128, 128], BF16))
    id_f = es.enter_context(nc.sbuf_tensor("id_f", [128, 128], F32))
    ones1b = es.enter_context(nc.sbuf_tensor("ones1b", [1, 128], BF16))
    ones1f = es.enter_context(nc.sbuf_tensor("ones1f", [1, 128], F32))
    ones_sq = es.enter_context(nc.sbuf_tensor("ones_sq", [128, 128], BF16))
    ones_c = es.enter_context(nc.sbuf_tensor("ones_c", [128, 1], BF16))

    ksum_bc = probs_g[:, 0:1024]
    kv_acc = probs_g[0:64, 1008:2048].rearrange("p (h c) -> p h c", h=16)
    k_tm = h1g[:, 0:8, :].rearrange("p a b -> p (a b)").rearrange("p (tb d) -> p tb d", tb=WTB)
    v_tm = h1g[:, 8:16, :].rearrange("p a b -> p (a b)").rearrange("p (tb d) -> p tb d", tb=WTB)
    attn_fm = h1g[:, 0:8, :]
    q_tm = h1g[:, 8:16, :].rearrange("p a b -> p (a b)").rearrange("p (tb d) -> p tb d", tb=WTB)
    qz_fm = [xn_fm[:, 4, :], xn_fm[:, 5, :]]
    stat_dmy = h1g[:, 0:4, :].rearrange("p a b -> p (a b)").bitcast(F32)

    pb = [es.enter_context(nc.psum_tensor(f"pb{i}", [128, 512], F32)) for i in range(8)]
    bank_fill = [None] * 8
    bank_free = [None] * 8

    def pe_bank_acquire(b):
        if bank_free[b] is not None:
            em.waitd("t", bank_free[b])

    def evac_bank_wait(eng, b):
        assert bank_fill[b] is not None
        em.waitd(eng, bank_fill[b])

    em.op("y", lambda h: h.dma_start(out=x[:, :, :],
          in_=xin.ap().rearrange("(tb tp) d -> tp tb d", tp=128)), dmainc="XL")

    for dst, src in ((id_bf, identb), (id_f, identf), (ones1b, onesb),
                     (ones1f, onesf), (ones_sq, ones128), (ones_c, onescol)):
        em.op("g", lambda h, dst=dst, src=src: h.dma_start(out=dst[:, :], in_=src[:, :]), dmainc="MI")
    em.op("g", lambda h: h.dma_start(out=wg_sb[:, :, :],
          in_=wg.ap().rearrange("(dc dp) e -> dp dc e", dp=128)), dmainc="MI")
    mi_consts = em.cnt["MI"]

    def load_bias_pair(na, nb, tdep):
        src = {"bq": bq, "bk": bk, "bv": bv, "bo": bo}
        em.waitd("g", tdep)
        em.wait("g", "MI", em.cnt["MI"])
        em.op("g", lambda h, s=src[na]: h.dma_start(out=bias_sb[:, 0, :], in_=s[:, :]), dmainc="MI")
        em.wait("g", "MI", em.cnt["MI"])
        return em.op("g", lambda h, s=src[nb]: h.dma_start(out=bias_sb[:, 1, :], in_=s[:, :]), dmainc="MI")

    def ln_stats(xdep):
        em.waitd("s", xdep)
        sc = None
        for tb in range(TBN):
            em.op("s", lambda h, tb=tb: h.activation(
                stat_dmy[:, :], x[:, tb, :], AF.Square, accum_out=sum2[:, tb:tb + 1]), inc="S")
            sc = em.op("s", lambda h, tb=tb: h.activation(
                stat_dmy[:, :], x[:, tb, :], AF.Copy, accum_out=sum1[:, tb:tb + 1]), inc="S")
        em.wait("v", "S", sc)
        em.op("v", lambda h: h.tensor_scalar(mmean[:, :], sum1[:, :], 1.0 / D, None, ALU.mult), inc="V")
        em.chain("v", lambda h: h.tensor_mul(ram[:, :], mmean[:, :], mmean[:, :]))
        em.chain("v", lambda h: h.scalar_tensor_tensor(
            rw[:, :], sum2[:, :], 1.0 / D, ram[:, :], ALU.mult, ALU.subtract))
        vc = em.chain("v", lambda h: h.tensor_scalar(rm0[:, :], rw[:, :], 1e-5, None, ALU.add))
        if DEBUG and DBG_CALL[0] == 0:
            em.wait("y", "V", vc)
            em.op("y", lambda h: h.dma_start(out=dbg["sum1"][:, :], in_=sum1[:, :]), dmainc="DBG")
            em.op("y", lambda h: h.dma_start(out=dbg["sum2"][:, :], in_=sum2[:, :]), dmainc="DBG")
            em.op("y", lambda h: h.dma_start(out=dbg["var"][:, :], in_=rm0[:, :]), dmainc="DBG")
        em.wait("s", "V", vc)
        sc = em.op("s", lambda h: h.activation(rm1[:, :], rm0[:, :], AF.Sqrt), inc="S")
        em.wait("v", "S", sc)
        c = em.op("v", lambda h: h.reciprocal(rstd[:, :], rm1[:, :]), inc="V")
        return ("V", c)

    tp_state = {}

    def xn_window(w, stats_dep):
        for tb in range(WTB):
            gtb = w * WTB + tb
            slot = gtb % 2
            em.waitd("v", stats_dep)
            prev = tp_state.get(slot)
            if prev is not None:
                em.waitd("v", prev)
            em.op("v", lambda h, gtb=gtb, slot=slot: h.tensor_scalar(
                xn_tm[:, slot, :], x[:, gtb, :], mmean[:, gtb:gtb + 1],
                rstd[:, gtb:gtb + 1], ALU.subtract, ALU.mult), inc="V")
            nv = em.cnt["V"]
            em.wait("t", "V", nv)
            for bi in range(2):
                b = bi
                pe_bank_acquire(b)
                tc = None
                for k in range(4):
                    dc = bi * 4 + k
                    tc = em.op("t", lambda h, slot=slot, dc=dc, b=b, k=k: h.matmul(
                        pb[b][:, :].rearrange("p (a c) -> p a c", a=4)[:, k, :],
                        xn_tm[:, slot, dc * 128:(dc + 1) * 128], id_bf[:, :],
                        start=True, stop=True),
                        inc="T" if k == 3 else None)
                bank_fill[b] = ("T", tc)
                evac_bank_wait("v", b)
                ev = em.op("v", lambda h, b=b, bi=bi, tb=tb: h.tensor_copy(
                    xn_fm[:, bi * 4:(bi + 1) * 4, tb * 128:(tb + 1) * 128],
                    pb[b][:, :].rearrange("p (a c) -> p a c", a=4)), inc="V")
                bank_free[b] = ("V", ev)
            tp_state[slot] = ("T", tc)
        return ("V", em.cnt["V"])

    w1r_free = [None, None]
    w1r_sem = ["W1R0", "W1R1"]
    aq = [0]

    def w1q_load(slot, dst_view, src_ap):
        em.waitd("g", w1r_free[slot])
        c = em.op("g", lambda h, dst_view=dst_view, src_ap=src_ap: h.dma_start(
            out=dst_view, in_=src_ap), dmainc=w1r_sem[slot])
        return (w1r_sem[slot], c)

    w2r_free = [None, None, None]
    w2r_sem = ["W2R0", "W2R1", "W2R2"]

    def ffn_call(xdep):
        stats_dep = ln_stats(xdep)

        last_probs = None
        for w in range(NW):
            xn_dep = xn_window(w, stats_dep)
            b = 2
            pe_bank_acquire(b)
            em.wait("t", "MI", mi_consts)
            em.waitd("t", xn_dep)
            lg = pb[b][:, 0:32].rearrange("p (s e) -> p s e", s=NSUB)
            tc = None
            for sub in range(NSUB):
                for dc in range(DCH):
                    tc = em.op("t", lambda h, sub=sub, dc=dc, lg=lg: h.matmul(
                        lg[:, sub, :], xn_fm[:, dc, sub * 128:(sub + 1) * 128],
                        wg_sb[:, dc, :], start=(sub == 0 and dc == 0),
                        stop=(sub == NSUB - 1 and dc == DCH - 1)),
                        inc="T" if (dc == DCH - 1) else None)
            bank_fill[b] = ("T", tc)
            evac_bank_wait("v", b)
            evac_bank_wait("s", b)
            for sub in range(NSUB):
                gtb = w * WTB + sub
                em.op("v", lambda h, sub=sub, lg=lg: h.tensor_reduce(
                    s3[:, 0:1], lg[:, sub, :], AX.X, ALU.max, negate=True), inc="V")
                vc = em.cnt["V"]
                em.wait("s", "V", vc)
                sc = em.op("s", lambda h, sub=sub, lg=lg: h.activation(
                    s3[:, 8:16], lg[:, sub, :], AF.Exp, bias=s3[:, 0:1]), inc="S")
                em.wait("v", "S", sc)
                em.op("v", lambda h: h.tensor_reduce(
                    s3[:, 1:2], s3[:, 8:16], AX.X, ALU.add), inc="V")
                em.chain("v", lambda h: h.reciprocal(s3[:, 2:3], s3[:, 1:2]))
                last_probs = em.chain("v", lambda h, gtb=gtb: h.tensor_scalar(
                    probs_l[:, gtb, :], s3[:, 8:16], s3[:, 2:3], None, ALU.mult))
            bank_free[b] = ("V", last_probs)

        em.wait("g", "V", last_probs)
        em.wait("g", "MI", em.cnt["MI"])
        dmp = em.op("g", lambda h: h.dma_start(
            out=ag_in[:, :], in_=probs_l[:, :, :].rearrange("p a e -> p (a e)")), dmainc="MI")
        em.wait("g", "MI", dmp)
        em.op("g", lambda h: h.collective_compute(
            "AllGather", ALU.bypass, replica_groups=[list(range(8))],
            ins=[ag_in.ap().opt()], outs=[ag_out.ap().opt()]), inc="CC")
        em.wait("g", "CC", em.cnt["CC"])
        pgl = em.op("g", lambda h: h.dma_start(
            out=probs_g[:, :].rearrange("p (r c) -> p r c", r=8),
            in_=ag_out.ap().rearrange("(r p) c -> p r c", p=128)), dmainc="MI")

        em.wait("v", "MI", pgl)
        em.op("v", lambda h: h.memset(lo_a[:, :], 0.0), inc="V")
        em.op("v", lambda h: h.memset(hi_a[:, :], 1.0), inc="V")
        pg4 = probs_g[:, :].rearrange("p (r tb e) -> p r tb e", r=8, e=8)
        s3v = s3[:, :].rearrange("p (r tb) -> p r tb", r=8)[:, :, 0:32]
        lo, hi, lon, hin = lo_a, hi_a, lo_b, hi_b
        for it in range(NBIS):
            em.chain("v", lambda h, lo=lo, hi=hi: h.tensor_add(mid[:, :], lo[:, :], hi[:, :]))
            em.chain("v", lambda h: h.tensor_scalar(mid[:, :], mid[:, :], 0.5, None, ALU.mult))
            first = True
            for e in range(E):
                if first:
                    em.chain("v", lambda h, e=e: h.tensor_scalar(
                        s3v, pg4[:, :, :, e], mid[:, e:e + 1], 0.0, ALU.is_gt, ALU.add,
                        accum_out=cntf[:, e:e + 1]))
                    first = False
                else:
                    em.op("v", lambda h, e=e: h.tensor_scalar(
                        s3v, pg4[:, :, :, e], mid[:, e:e + 1], 0.0, ALU.is_gt, ALU.add,
                        accum_out=cntf[:, e:e + 1]), inc="V")
            vc = em.chain("v", lambda h: h.tensor_copy(cntb[:, :], cntf[:, :]))
            b = 3
            pe_bank_acquire(b)
            em.wait("t", "V", vc)
            tc = em.op("t", lambda h, b=b: h.matmul(
                pb[b][:, 0:E], ones_sq[:, :], cntb[:, :], start=True, stop=True), inc="T")
            bank_fill[b] = ("T", tc)
            evac_bank_wait("v", b)
            em.op("v", lambda h, b=b: h.tensor_scalar(
                msku[:, :], pb[b][:, 0:E], float(CAP) - 0.5, None, ALU.is_ge), inc="V")
            em.chain("v", lambda h, lo=lo, lon=lon: h.select(
                lon[:, :], msku[:, :], mid[:, :], lo[:, :]))
            ev = em.chain("v", lambda h, hi=hi, hin=hin: h.select(
                hin[:, :], msku[:, :], hi[:, :], mid[:, :]))
            bank_free[b] = ("V", ev)
            lo, lon = lon, lo
            hi, hin = hin, hi
        theta = lo

        th_b = theta[:, :].rearrange("p (o e) -> p o e", o=1).broadcast_to([128, TBN, E])
        em.chain("v", lambda h, th_b=th_b: h.tensor_tensor(
            rmsc[:, :, :], probs_l[:, :, :], th_b, ALU.is_gt))
        em.chain("v", lambda h: h.tensor_mul(gates[:, :, :], probs_l[:, :, :], rmsc[:, :, :]))
        em.chain("v", lambda h: h.scalar_tensor_tensor(
            rmsc[:, :, :], rmsc[:, :, :], 1.0, gates[:, :, :], ALU.subtract, ALU.add))
        em.chain("v", lambda h: h.tensor_reduce(rm1[:, :], rmsc[:, :, :], AX.X, ALU.max))
        em.chain("v", lambda h: h.tensor_reduce(rm0[:, :], probs_l[:, :, :], AX.X, ALU.max))
        em.chain("v", lambda h: h.tensor_scalar(ram[:, :], rm1[:, :], -0.5, None, ALU.is_gt))
        em.chain("v", lambda h: h.tensor_mul(rw[:, :], ram[:, :], rm1[:, :]))
        em.chain("v", lambda h: h.tensor_scalar(
            ram[:, :], ram[:, :], -1.0, 1.0, ALU.mult, ALU.add))
        em.chain("v", lambda h: h.tensor_mul(ram[:, :], ram[:, :], rm0[:, :]))
        em.chain("v", lambda h: h.tensor_add(rw[:, :], rw[:, :], ram[:, :]))
        w_b = rw[:, :].rearrange("p (a o) -> p a o", o=1).broadcast_to([128, TBN, E])
        em.chain("v", lambda h, w_b=w_b: h.tensor_tensor(
            rmsc[:, :, :], probs_l[:, :, :], w_b, ALU.is_equal))
        em.chain("v", lambda h: h.tensor_scalar(rw[:, :], rw[:, :], 0.5, None, ALU.mult))
        gts = None
        first = True
        for tb in range(TBN):
            if first:
                gts = em.chain("v", lambda h, tb=tb: h.tensor_scalar(
                    gates[:, tb, :], rmsc[:, tb, :], rw[:, tb:tb + 1], None, ALU.mult))
                first = False
            else:
                gts = em.op("v", lambda h, tb=tb: h.tensor_scalar(
                    gates[:, tb, :], rmsc[:, tb, :], rw[:, tb:tb + 1], None, ALU.mult), inc="V")
        gates_dep = ("V", gts)
        if DEBUG and DBG_CALL[0] == 0:
            em.wait("y", "V", gts)
            em.op("y", lambda h: h.dma_start(out=dbg["mmean"][:, :], in_=mmean[:, :]), dmainc="DBG")
            em.op("y", lambda h: h.dma_start(out=dbg["rstd"][:, :], in_=rstd[:, :]), dmainc="DBG")
            em.op("y", lambda h: h.dma_start(out=dbg["probs"][:, :, :], in_=probs_l[:, :, :]), dmainc="DBG")
            em.op("y", lambda h: h.dma_start(out=dbg["theta"][:, :], in_=theta[:, :]), dmainc="DBG")
            em.op("y", lambda h: h.dma_start(out=dbg["gates"][:, :, :], in_=gates[:, :, :]), dmainc="DBG")
            em.op("y", lambda h: h.dma_start(out=dbg["cnt"][:, :], in_=cntf[:, :]), dmainc="DBG")

        h1_ready = [None] * FCH
        h1_freed = [None] * FCH
        for w in range(NW):
            xn_dep = xn_window(w, stats_dep)
            for e in range(E):
                for qt in range(4):
                    slot = aq[0] % 2
                    aq[0] += 1
                    src = w1[e, :, qt * 512:(qt + 1) * 512].rearrange(
                        "(dc dp) f -> dp dc f", dp=128)
                    dstv = w1ring[:, slot, :].rearrange("p (dc f) -> p dc f", dc=8)
                    ldep = w1q_load(slot, dstv, src)
                    em.waitd("t", ldep)
                    em.waitd("t", xn_dep)
                    w1v = dstv
                    tc = None
                    for fi in range(4):
                        f = qt * 4 + fi
                        b = f % 2
                        pe_bank_acquire(b)
                        if h1_freed[f] is not None:
                            em.waitd("s", h1_freed[f])
                        for dc in range(DCH):
                            tc = em.op("t", lambda h, w1v=w1v, fi=fi, dc=dc, b=b: h.matmul(
                                pb[b][:, :], w1v[:, dc, fi * 128:(fi + 1) * 128],
                                xn_fm[:, dc, :], start=(dc == 0), stop=(dc == DCH - 1)),
                                inc="T" if dc == DCH - 1 else None)
                        bank_fill[b] = ("T", tc)
                        evac_bank_wait("s", b)
                        sc = em.op("s", lambda h, b=b, f=f: h.activation(
                            h1g[:, f, :], pb[b][:, :], AF.Gelu), inc="S")
                        bank_free[b] = ("S", sc)
                        h1_ready[f] = ("S", sc)
                    w1r_free[slot] = ("T", tc)
                if DEBUG and DBG_CALL[0] == 0 and w == 0 and e == 0:
                    em.wait("g", "S", em.cnt["S"])
                    em.op("g", lambda h: h.dma_start(out=dbg["xnfm"][:, :, :], in_=xn_fm[:, :, :]), dmainc="DBG")
                    em.op("g", lambda h: h.dma_start(out=dbg["h1g"][:, :, :], in_=h1g[:, :, :]), dmainc="DBG")
                tc = None
                for f in range(FCH):
                    slot = f % 3
                    em.waitd("g", w2r_free[slot])
                    lc = em.op("g", lambda h, slot=slot, e=e, f=f: h.dma_start(
                        out=w2ring[:, slot, :], in_=w2[e, f * 128:(f + 1) * 128, :]),
                        dmainc=w2r_sem[slot])
                    em.wait("t", w2r_sem[slot], lc)
                    em.waitd("t", h1_ready[f])
                    for sub in range(NSUB):
                        for jh in range(2):
                            b = sub * 2 + jh
                            if f == 0:
                                pe_bank_acquire(b)
                            tc = em.op("t", lambda h, f=f, sub=sub, jh=jh, b=b, slot=slot: h.matmul(
                                pb[b][:, :], h1g[:, f, sub * 128:(sub + 1) * 128],
                                w2ring[:, slot, jh * 512:(jh + 1) * 512],
                                start=(f == 0), stop=(f == FCH - 1)),
                                inc="T" if (f == FCH - 1 or (sub == NSUB - 1 and jh == 1)) else None)
                    h1_freed[f] = ("T", tc)
                    w2r_free[slot] = ("T", tc)
                for sub in range(NSUB):
                    for jh in range(2):
                        bank_fill[sub * 2 + jh] = ("T", tc)
                em.waitd("v", gates_dep)
                for sub in range(NSUB):
                    gtb = w * WTB + sub
                    for jh in range(2):
                        b = sub * 2 + jh
                        evac_bank_wait("v", b)
                        ev = em.op("v", lambda h, b=b, gtb=gtb, jh=jh, e=e: h.scalar_tensor_tensor(
                            x[:, gtb, jh * 512:(jh + 1) * 512], pb[b][:, :],
                            gates[:, gtb, e:e + 1], x[:, gtb, jh * 512:(jh + 1) * 512],
                            ALU.mult, ALU.add), inc="V")
                        bank_free[b] = ("V", ev)
        DBG_CALL[0] += 1
        return ("V", em.cnt["V"])

    def attn_call(xdep):
        stats_dep = ln_stats(xdep)
        bias_c = load_bias_pair("bk", "bv", None)

        em.waitd("v", xdep)
        kvz = em.op("v", lambda h: h.memset(
            kv_acc[:, :, :].rearrange("p a c -> p (a c)"), 0.0), inc="V")

        for w in range(NW):
            xn_dep = xn_window(w, stats_dep)
            for proj in ("k", "v"):
                wsrc = wk if proj == "k" else wv
                btile = 0 if proj == "k" else 1
                dst = k_tm if proj == "k" else v_tm
                for qt in range(4):
                    slot = aq[0] % 2
                    aq[0] += 1
                    src = wsrc[qt * 256:(qt + 1) * 256, :].rearrange(
                        "(dc dp) j -> dp dc j", dp=128)
                    dstv = w1ring[:, slot, 0:2048].rearrange("p (dc j) -> p dc j", dc=2)
                    ldep = w1q_load(slot, dstv, src)
                    em.waitd("t", ldep)
                    em.waitd("t", xn_dep)
                    tc = None
                    for dcr in range(2):
                        dc = qt * 2 + dcr
                        for sub in range(NSUB):
                            for jh in range(2):
                                b = sub * 2 + jh
                                if dc == 0:
                                    pe_bank_acquire(b)
                                tc = em.op("t", lambda h, dstv=dstv, dcr=dcr, dc=dc, sub=sub, jh=jh, b=b: h.matmul(
                                    pb[b][:, :], xn_fm[:, dc, sub * 128:(sub + 1) * 128],
                                    dstv[:, dcr, jh * 512:(jh + 1) * 512],
                                    start=(dc == 0), stop=False),
                                    inc="T" if (dcr == 1 and sub == NSUB - 1 and jh == 1) else None)
                    w1r_free[slot] = ("T", tc)
                em.wait("t", "MI", bias_c)
                for sub in range(NSUB):
                    for jh in range(2):
                        b = sub * 2 + jh
                        tc = em.op("t", lambda h, btile=btile, jh=jh, b=b: h.matmul(
                            pb[b][:, :], ones1b[:, :], bias_sb[:, btile, jh * 512:(jh + 1) * 512],
                            start=False, stop=True), inc="T")
                        bank_fill[b] = ("T", tc)
                for sub in range(NSUB):
                    for jh in range(2):
                        b = sub * 2 + jh
                        evac_bank_wait("v", b)
                        if proj == "k":
                            em.chain("v", lambda h, b=b: h.tensor_scalar(
                                s12[:, 0:512], pb[b][:, :], 0.0, None, ALU.min))
                            vc = em.op("v", lambda h, b=b: h.tensor_scalar(
                                s12[:, 512:1024], pb[b][:, :], 0.0, None, ALU.max), inc="V")
                            em.wait("s", "V", vc)
                            sc = em.op("s", lambda h: h.activation(
                                s12[:, 0:512], s12[:, 0:512], AF.Exp), inc="S")
                            em.wait("v", "S", sc)
                            ev = em.op("v", lambda h, dst=dst, sub=sub, jh=jh: h.tensor_add(
                                dst[:, sub, jh * 512:(jh + 1) * 512],
                                s12[:, 0:512], s12[:, 512:1024]), inc="V")
                        else:
                            ev = em.op("v", lambda h, dst=dst, sub=sub, jh=jh, b=b: h.tensor_copy(
                                dst[:, sub, jh * 512:(jh + 1) * 512], pb[b][:, :]), inc="V")
                        bank_free[b] = ("V", ev)
            kv_dep = ("V", em.cnt["V"])
            for b in (0, 1, 2):
                pe_bank_acquire(b)
            em.waitd("t", kv_dep)
            em.wait("t", "V", kvz)
            em.wait("t", "MI", mi_consts)
            tc = None
            first = True
            for tb in range(WTB):
                for hh in range(H):
                    b = hh // 8
                    reg = pb[b][:, :].rearrange("p (a c) -> p a c", a=8)[0:64, hh % 8, :]
                    em.op("t", lambda h, tb=tb, hh=hh, reg=reg, first=first: h.matmul(
                        reg, k_tm[:, tb, hh * 64:(hh + 1) * 64],
                        v_tm[:, tb, hh * 64:(hh + 1) * 64],
                        start=first and (hh % 8 == 0) and tb == 0,
                        stop=(tb == WTB - 1)), inc=None)
                    ksreg = pb[2][0:64, hh:hh + 1]
                    tc = em.op("t", lambda h, tb=tb, hh=hh, ksreg=ksreg: h.matmul(
                        ksreg, k_tm[:, tb, hh * 64:(hh + 1) * 64], ones_c[:, :],
                        start=(tb == 0 and hh == 0), stop=(tb == WTB - 1)),
                        inc="T" if (tb == WTB - 1 and hh == H - 1) else None)
            for b in (0, 1, 2):
                bank_fill[b] = ("T", tc)
            for b in (0, 1):
                evac_bank_wait("v", b)
                ev = em.op("v", lambda h, b=b: h.tensor_add(
                    kv_acc[:, b * 8:(b + 1) * 8, 0:64],
                    pb[b][:, :].rearrange("p (a c) -> p a c", a=8)[0:64, :, :],
                    kv_acc[:, b * 8:(b + 1) * 8, 0:64]), inc="V")
                bank_free[b] = ("V", ev)
            evac_bank_wait("v", 2)
            ev = em.op("v", lambda h: h.tensor_add(
                kv_acc[:, :, 64:65].rearrange("p a c -> p (a c)"),
                pb[2][0:64, 0:16],
                kv_acc[:, :, 64:65].rearrange("p a c -> p (a c)")), inc="V")
            bank_free[2] = ("V", ev)
        kv_all = em.cnt["V"]

        em.wait("g", "V", kv_all)
        em.wait("g", "MI", em.cnt["MI"])
        d1 = em.op("g", lambda h: h.dma_start(
            out=ar_in[:, :], in_=kv_acc[:, :, :].rearrange("p a c -> p (a c)")), dmainc="MI")
        em.wait("g", "MI", d1)
        em.op("g", lambda h: h.collective_compute(
            "AllReduce", ALU.add, replica_groups=[[0, 1], [2, 3], [4, 5], [6, 7]],
            ins=[ar_in.ap().opt()], outs=[ar_out.ap().opt()]), inc="CC")
        em.wait("g", "CC", em.cnt["CC"])
        d2 = em.op("g", lambda h: h.dma_start(
            out=kv_acc[:, :, :].rearrange("p a c -> p (a c)"), in_=ar_out[:, :]), dmainc="MI")
        kvbf_d = None
        for two in range(2):
            em.wait("g", "MI", em.cnt["MI"])
            kvbf_d = em.op("g", lambda h, two=two: h.dma_start(
                out=kv_bf[two * 64:(two + 1) * 64, :, :],
                in_=ar_out.ap().rearrange("d (hh two c) -> two d hh c", two=2, c=65)[two, :, :, 0:64]), dmainc="MI")
        b = 4
        pe_bank_acquire(b)
        em.wait("t", "MI", d2)
        tc = em.op("t", lambda h, b=b: h.matmul(
            pb[b][0:16, 0:64], kv_acc[:, :, 64:65].rearrange("p a c -> p (a c)"),
            id_f[0:64, 0:64], start=True, stop=True), inc="T")
        bank_fill[b] = ("T", tc)
        evac_bank_wait("v", b)
        ks_ev = em.op("v", lambda h, b=b: h.tensor_copy(
            s12[0:16, 64:128], pb[b][0:16, 0:64]), inc="V")
        bank_free[b] = ("V", ks_ev)
        em.wait("g", "V", ks_ev)
        em.wait("g", "MI", em.cnt["MI"])
        em.op("g", lambda h: h.dma_start(
            out=kst[:, :], in_=s12[0:16, 64:128]), dmainc="MI")
        em.wait("g", "MI", em.cnt["MI"])
        d3 = em.op("g", lambda h: h.dma_start(
            out=s12[0:1, 0:1024],
            in_=kst.ap().rearrange("h d -> (h d)").rearrange("(o c) -> o c", o=1)), dmainc="MI")
        ks_fill = []
        for half in range(2):
            b = 4 + half
            pe_bank_acquire(b)
            em.wait("t", "MI", d3)
            tc = em.op("t", lambda h, b=b, half=half: h.matmul(
                pb[b][:, :], ones1f[:, :], s12[0:1, half * 512:(half + 1) * 512],
                start=True, stop=True), inc="T")
            bank_fill[b] = ("T", tc)
            evac_bank_wait("v", b)
            ev = em.op("v", lambda h, b=b, half=half: h.tensor_copy(
                ksum_bc[:, half * 512:(half + 1) * 512], pb[b][:, :]), inc="V")
            bank_free[b] = ("V", ev)
        ksbc_c = em.cnt["V"]

        bias2_c = load_bias_pair("bq", "bo", ("T", em.cnt["T"]))

        for w in range(NW):
            xn_dep = xn_window(w, stats_dep)
            for qt in range(4):
                slot = aq[0] % 2
                aq[0] += 1
                src = wq[qt * 256:(qt + 1) * 256, :].rearrange("(dc dp) j -> dp dc j", dp=128)
                dstv = w1ring[:, slot, 0:2048].rearrange("p (dc j) -> p dc j", dc=2)
                ldep = w1q_load(slot, dstv, src)
                em.waitd("t", ldep)
                em.waitd("t", xn_dep)
                tc = None
                for dcr in range(2):
                    dc = qt * 2 + dcr
                    for sub in range(NSUB):
                        for jh in range(2):
                            b = sub * 2 + jh
                            if dc == 0:
                                pe_bank_acquire(b)
                            tc = em.op("t", lambda h, dstv=dstv, dcr=dcr, dc=dc, sub=sub, jh=jh, b=b: h.matmul(
                                pb[b][:, :], xn_fm[:, dc, sub * 128:(sub + 1) * 128],
                                dstv[:, dcr, jh * 512:(jh + 1) * 512],
                                start=(dc == 0), stop=False),
                                inc="T" if (dcr == 1 and sub == NSUB - 1 and jh == 1) else None)
                w1r_free[slot] = ("T", tc)
            em.wait("t", "MI", bias2_c)
            for sub in range(NSUB):
                for jh in range(2):
                    b = sub * 2 + jh
                    tc = em.op("t", lambda h, jh=jh, b=b: h.matmul(
                        pb[b][:, :], ones1b[:, :], bias_sb[:, 0, jh * 512:(jh + 1) * 512],
                        start=False, stop=True), inc="T")
                    bank_fill[b] = ("T", tc)
            for sub in range(NSUB):
                for jh in range(2):
                    b = sub * 2 + jh
                    evac_bank_wait("v", b)
                    em.chain("v", lambda h, b=b: h.tensor_scalar(
                        s12[:, 0:512], pb[b][:, :], 0.0, None, ALU.min))
                    vc = em.op("v", lambda h, b=b: h.tensor_scalar(
                        s12[:, 512:1024], pb[b][:, :], 0.0, None, ALU.max), inc="V")
                    em.wait("s", "V", vc)
                    sc = em.op("s", lambda h: h.activation(
                        s12[:, 0:512], s12[:, 0:512], AF.Exp), inc="S")
                    em.wait("v", "S", sc)
                    ev = em.op("v", lambda h, sub=sub, jh=jh: h.tensor_add(
                        q_tm[:, sub, jh * 512:(jh + 1) * 512],
                        s12[:, 0:512], s12[:, 512:1024]), inc="V")
                    bank_free[b] = ("V", ev)
            em.wait("v", "V", ksbc_c)
            qz_c = None
            for tb in range(WTB):
                em.chain("v", lambda h, tb=tb: h.tensor_tensor(
                    s12[:, :].rearrange("p (a c) -> p a c", a=16),
                    q_tm[:, tb, :].rearrange("p (a c) -> p a c", a=16),
                    ksum_bc[:, :].rearrange("p (a c) -> p a c", a=16), ALU.mult))
                em.chain("v", lambda h: h.tensor_reduce(
                    zdot[:, :], s12[:, :].rearrange("p (a c) -> p a c", a=16),
                    AX.X, ALU.add))
                em.chain("v", lambda h: h.tensor_scalar(
                    zdot[:, :], zdot[:, :], 1e-6, None, ALU.add))
                em.chain("v", lambda h: h.reciprocal(zval[:, :], zdot[:, :]))
                zb = zval[:, :].rearrange("p (a o) -> p a o", o=1).broadcast_to([128, H, DH])
                qz_c = em.chain("v", lambda h, tb=tb, zb=zb: h.tensor_tensor(
                    q_tm[:, tb, :].rearrange("p (a c) -> p a c", a=16),
                    q_tm[:, tb, :].rearrange("p (a c) -> p a c", a=16), zb, ALU.mult))
            em.wait("t", "MI", kvbf_d)
            sc_banks = [2, 3, 4, 5, 6, 7]
            sc_i = [0]
            for jch in range(8):
                ring = jch % 2
                em.wait("t", "V", qz_c)
                b = ring
                pe_bank_acquire(b)
                tc = None
                for tb in range(WTB):
                    tc = em.op("t", lambda h, tb=tb, jch=jch, b=b: h.matmul(
                        pb[b][:, :].rearrange("p (a c) -> p a c", a=4)[:, tb, :],
                        q_tm[:, tb, jch * 128:(jch + 1) * 128], id_bf[:, :],
                        start=True, stop=True),
                        inc="T" if tb == WTB - 1 else None)
                bank_fill[b] = ("T", tc)
                evac_bank_wait("v", b)
                ev = em.op("v", lambda h, b=b, ring=ring: h.tensor_copy(
                    qz_fm[ring][:, :].rearrange("p (a c) -> p a c", a=4),
                    pb[b][:, :].rearrange("p (a c) -> p a c", a=4)), inc="V")
                bank_free[b] = ("V", ev)
                for hr in range(2):
                    hh = jch * 2 + hr
                    sb = sc_banks[sc_i[0] % 6]
                    sc_i[0] += 1
                    pe_bank_acquire(sb)
                    em.wait("t", "V", ev)
                    tp = (64, 64) if hr == 1 else None
                    tc = em.op("t", lambda h, hh=hh, hr=hr, ring=ring, sb=sb, tp=tp: h.matmul(
                        pb[sb][hr * 64:(hr + 1) * 64, :],
                        kv_bf[hr * 64:(hr + 1) * 64, hh // 2, :],
                        qz_fm[ring][hr * 64:(hr + 1) * 64, :], start=True, stop=True,
                        tile_position=tp), inc="T")
                    bank_fill[sb] = ("T", tc)
                    evac_bank_wait("v", sb)
                    ev2 = em.op("v", lambda h, hh=hh, hr=hr, jch=jch, sb=sb: h.tensor_copy(
                        attn_fm[hr * 64:(hr + 1) * 64, jch, :],
                        pb[sb][hr * 64:(hr + 1) * 64, :]), inc="V")
                    bank_free[sb] = ("V", ev2)
            af_dep = ("V", em.cnt["V"])
            for qt in range(4):
                slot = aq[0] % 2
                aq[0] += 1
                src = wo[qt * 256:(qt + 1) * 256, :].rearrange("(dc dp) j -> dp dc j", dp=128)
                dstv = w1ring[:, slot, 0:2048].rearrange("p (dc j) -> p dc j", dc=2)
                ldep = w1q_load(slot, dstv, src)
                em.waitd("t", ldep)
                em.waitd("t", af_dep)
                tc = None
                for dcr in range(2):
                    dc = qt * 2 + dcr
                    for sub in range(NSUB):
                        for jh in range(2):
                            b = sub * 2 + jh
                            if dc == 0:
                                pe_bank_acquire(b)
                            tc = em.op("t", lambda h, dstv=dstv, dcr=dcr, dc=dc, sub=sub, jh=jh, b=b: h.matmul(
                                pb[b][:, :], attn_fm[:, dc, sub * 128:(sub + 1) * 128],
                                dstv[:, dcr, jh * 512:(jh + 1) * 512],
                                start=(dc == 0), stop=False),
                                inc="T" if (dcr == 1 and sub == NSUB - 1 and jh == 1) else None)
                w1r_free[slot] = ("T", tc)
            for sub in range(NSUB):
                for jh in range(2):
                    b = sub * 2 + jh
                    tc = em.op("t", lambda h, jh=jh, b=b: h.matmul(
                        pb[b][:, :], ones1b[:, :], bias_sb[:, 1, jh * 512:(jh + 1) * 512],
                        start=False, stop=True), inc="T")
                    bank_fill[b] = ("T", tc)
            for sub in range(NSUB):
                gtb = w * WTB + sub
                for jh in range(2):
                    b = sub * 2 + jh
                    evac_bank_wait("v", b)
                    ev = em.op("v", lambda h, b=b, gtb=gtb, jh=jh: h.tensor_add(
                        x[:, gtb, jh * 512:(jh + 1) * 512], pb[b][:, :],
                        x[:, gtb, jh * 512:(jh + 1) * 512]), inc="V")
                    bank_free[b] = ("V", ev)
        return ("V", em.cnt["V"])

    dep = ("XL", 16)
    if "ffn1" in STAGES:
        dep = ffn_call(dep)
    if "attn" in STAGES:
        dep = attn_call(dep)
    if "ffn2" in STAGES:
        dep = ffn_call(dep)

    for w in range(NW):
        em.waitd("y", dep)
        em.op("y", lambda h, w=w: h.dma_start(
            out=out[w * TWIN:(w + 1) * TWIN, :].rearrange("(tb tp) d -> tp tb d", tp=128),
            in_=x[:, w * WTB:(w + 1) * WTB, :]), dmainc="ST")
    em.wait("y", "ST", em.cnt["ST"])
    if DEBUG:
        em.wait("y", "DBG", em.cnt.get("DBG", 0))

    with em._es:
        with nc.Block() as block:
            @block.tensor
            def _(h):
                for fn in em.q["t"]:
                    fn(h)

            @block.scalar
            def _(h):
                for fn in em.q["s"]:
                    fn(h)

            @block.vector
            def _(h):
                for fn in em.q["v"]:
                    fn(h)

            @block.gpsimd
            def _(h):
                for fn in em.q["g"]:
                    fn(h)

            @block.sync
            def _(h):
                for fn in em.q["y"]:
                    fn(h)
    return nc


_NC_CACHE = {}


def kernel(L0_attn_x, L0_attn_wq, L0_attn_bq, L0_attn_wk, L0_attn_bk,
           L0_attn_wv, L0_attn_bv, L0_attn_wo, L0_attn_bo,
           L0_ffn_Wg, L0_ffn_W1, L0_ffn_W2):
    xfull = np.ascontiguousarray(np.asarray(L0_attn_x, dtype=np.float32))
    shared = dict(
        w1=np.ascontiguousarray(np.asarray(L0_ffn_W1, dtype=np.float32)),
        w2=np.ascontiguousarray(np.asarray(L0_ffn_W2, dtype=np.float32)),
        wg=np.ascontiguousarray(np.asarray(L0_ffn_Wg, dtype=np.float32)),
        wq=np.ascontiguousarray(np.asarray(L0_attn_wq, dtype=np.float32)),
        wk=np.ascontiguousarray(np.asarray(L0_attn_wk, dtype=np.float32)),
        wv=np.ascontiguousarray(np.asarray(L0_attn_wv, dtype=np.float32)),
        wo=np.ascontiguousarray(np.asarray(L0_attn_wo, dtype=np.float32)),
        bq=np.ascontiguousarray(np.asarray(L0_attn_bq, dtype=np.float32)).reshape(1, D),
        bk=np.ascontiguousarray(np.asarray(L0_attn_bk, dtype=np.float32)).reshape(1, D),
        bv=np.ascontiguousarray(np.asarray(L0_attn_bv, dtype=np.float32)).reshape(1, D),
        bo=np.ascontiguousarray(np.asarray(L0_attn_bo, dtype=np.float32)).reshape(1, D),
        identb=np.eye(128, dtype=ml_dtypes.bfloat16),
        identf=np.eye(128, dtype=np.float32),
        onesb=np.ones((1, 128), dtype=ml_dtypes.bfloat16),
        onesf=np.ones((1, 128), dtype=np.float32),
        ones128=np.ones((128, 128), dtype=ml_dtypes.bfloat16),
        onescol=np.ones((128, 1), dtype=ml_dtypes.bfloat16),
    )
    in_maps = []
    for c in range(NCORES):
        b, half = c // 2, c % 2
        shard = np.ascontiguousarray(xfull[b, half * NT:(half + 1) * NT, :])
        in_maps.append(dict(xin=shard, **shared))

    key = STAGES
    if key not in _NC_CACHE:
        _NC_CACHE[key] = build_nc()
    nc = _NC_CACHE[key]
    res = run_bass_kernel_spmd(nc, in_maps, core_ids=list(range(NCORES)))
    outp = np.empty((4, 8192, D), dtype=np.float32)
    for c in range(NCORES):
        b, half = c // 2, c % 2
        outp[b, half * NT:(half + 1) * NT, :] = np.asarray(res.results[c]["out"])
    if DEBUG:
        global DEBUG_RES
        DEBUG_RES = [{k: np.asarray(v) for k, v in r.items() if k.startswith("d_")}
                     for r in res.results]
    return outp


# version 05
# speedup vs baseline: 1.0660x; 1.0660x over previous
import numpy as np
import ml_dtypes
import contextlib
import concourse.bass as bass
import concourse.mybir as mybir
from concourse.bass_utils import run_bass_kernel_spmd

F32 = mybir.dt.float32
BF16 = mybir.dt.bfloat16
U8 = mybir.dt.uint8
AF = mybir.ActivationFunctionType
ALU = mybir.AluOpType
AX = mybir.AxisListType

NCORES = 8
NT = 4096
TBN = 32
D = 1024
DCH = 8
FF = 2048
FCH = 16
E = 8
H = 16
DH = 64
NW = 8
WTB = 4
TWIN = 512
NSUB = 4
CAP = 4096
NBIS = 26

STAGES = ("ffn1", "attn", "ffn2")
DEBUG = False
DBG_CALL = [0]


class EM:
    ENGS = ("t", "s", "v", "g", "y")

    def __init__(self, nc):
        self.nc = nc
        self.q = {e: [] for e in self.ENGS}
        self.sems = {}
        self.cnt = {}
        self._es = contextlib.ExitStack()
        self.last_wait = {}

    def sem(self, name):
        if name not in self.sems:
            self.sems[name] = self._es.enter_context(self.nc.semaphore(name))
            self.cnt[name] = 0
        return self.sems[name]

    def wait(self, eng, name, val):
        if val is None or val <= 0:
            return
        self.sem(name)
        assert val <= self.cnt[name], f"wait on future count {name} {val} > {self.cnt[name]}"
        key = (eng, name)
        if self.last_wait.get(key, 0) >= val:
            return
        self.last_wait[key] = val
        s = self.sems[name]
        self.q[eng].append(lambda h, s=s, v=val: h.wait_ge(s, v))

    def waitd(self, eng, dep):
        if dep is not None:
            self.wait(eng, dep[0], dep[1])

    ESEM = {"t": "T", "s": "S", "v": "V", "g": "G", "y": "Y"}

    def chain(self, eng, fn):
        nm = self.ESEM[eng]
        self.sem(nm)
        self.wait(eng, nm, self.cnt[nm])
        return self.op(eng, fn, inc=nm)

    def op(self, eng, fn, inc=None, dmainc=None):
        if inc is not None:
            s = self.sem(inc)
            self.cnt[inc] += 1
            c = self.cnt[inc]
            self.q[eng].append(lambda h, fn=fn, s=s: fn(h).then_inc(s, 1))
            return c
        if dmainc is not None:
            s = self.sem(dmainc)
            self.cnt[dmainc] += 16
            c = self.cnt[dmainc]
            self.q[eng].append(lambda h, fn=fn, s=s: fn(h).then_inc(s, 16))
            return c
        self.q[eng].append(fn)
        return None


def build_nc():
    nc = bass.Bass("TRN2", target_bir_lowering=False)

    xin = nc.dram_tensor("xin", [NT, D], F32, kind="ExternalInput")
    w1 = nc.dram_tensor("w1", [E, D, FF], F32, kind="ExternalInput")
    w2 = nc.dram_tensor("w2", [E, FF, D], F32, kind="ExternalInput")
    wg = nc.dram_tensor("wg", [D, E], F32, kind="ExternalInput")
    wq = nc.dram_tensor("wq", [D, D], F32, kind="ExternalInput")
    wk = nc.dram_tensor("wk", [D, D], F32, kind="ExternalInput")
    wv = nc.dram_tensor("wv", [D, D], F32, kind="ExternalInput")
    wo = nc.dram_tensor("wo", [D, D], F32, kind="ExternalInput")
    bq = nc.dram_tensor("bq", [1, D], F32, kind="ExternalInput")
    bk = nc.dram_tensor("bk", [1, D], F32, kind="ExternalInput")
    bv = nc.dram_tensor("bv", [1, D], F32, kind="ExternalInput")
    bo = nc.dram_tensor("bo", [1, D], F32, kind="ExternalInput")
    identb = nc.dram_tensor("identb", [128, 128], BF16, kind="ExternalInput")
    identf = nc.dram_tensor("identf", [128, 128], F32, kind="ExternalInput")
    onesb = nc.dram_tensor("onesb", [1, 128], BF16, kind="ExternalInput")
    onesf = nc.dram_tensor("onesf", [1, 128], F32, kind="ExternalInput")
    ones128 = nc.dram_tensor("ones128", [128, 128], BF16, kind="ExternalInput")
    onescol = nc.dram_tensor("onescol", [128, 1], BF16, kind="ExternalInput")
    out = nc.dram_tensor("out", [NT, D], F32, kind="ExternalOutput")
    dbg = {}
    if DEBUG:
        dbg["mmean"] = nc.dram_tensor("d_mmean", [128, TBN], F32, kind="ExternalOutput")
        dbg["rstd"] = nc.dram_tensor("d_rstd", [128, TBN], F32, kind="ExternalOutput")
        dbg["probs"] = nc.dram_tensor("d_probs", [128, TBN, E], F32, kind="ExternalOutput")
        dbg["theta"] = nc.dram_tensor("d_theta", [128, E], F32, kind="ExternalOutput")
        dbg["gates"] = nc.dram_tensor("d_gates", [128, TBN, E], F32, kind="ExternalOutput")
        dbg["xnfm"] = nc.dram_tensor("d_xnfm", [128, DCH, TWIN], F32, kind="ExternalOutput")
        dbg["h1g"] = nc.dram_tensor("d_h1g", [128, FCH, TWIN], F32, kind="ExternalOutput")
        dbg["cnt"] = nc.dram_tensor("d_cnt", [128, E], F32, kind="ExternalOutput")
        dbg["sum1"] = nc.dram_tensor("d_sum1", [128, TBN], F32, kind="ExternalOutput")
        dbg["sum2"] = nc.dram_tensor("d_sum2", [128, TBN], F32, kind="ExternalOutput")
        dbg["var"] = nc.dram_tensor("d_var", [128, TBN], F32, kind="ExternalOutput")

    w1s = nc.dram_tensor("w1s", [E, D, FF], BF16)
    w2s = nc.dram_tensor("w2s", [E, FF, D], BF16)
    wqs = nc.dram_tensor("wqs", [D, D], BF16)
    wks = nc.dram_tensor("wks", [D, D], BF16)
    wvs = nc.dram_tensor("wvs", [D, D], BF16)
    wos = nc.dram_tensor("wos", [D, D], BF16)
    ag_in = nc.dram_tensor("ag_in", [128, 256], F32)
    ag_out = nc.dram_tensor("ag_out", [8 * 128, 256], F32)
    ar_in = nc.dram_tensor("ar_in", [64, 1040], F32)
    kst = nc.dram_tensor("kst", [16, 64], F32)
    ar_out = nc.dram_tensor("ar_out", [64, 1040], F32)

    em = EM(nc)
    es = em._es

    x = es.enter_context(nc.sbuf_tensor("x", [128, TBN, D], F32))
    w1ring = es.enter_context(nc.sbuf_tensor("w1ring", [128, 2, 4096], BF16))
    h1g = es.enter_context(nc.sbuf_tensor("h1g", [128, FCH, TWIN], BF16))
    xn_fm = es.enter_context(nc.sbuf_tensor("xn_fm", [128, DCH, TWIN], BF16))
    w2ring = es.enter_context(nc.sbuf_tensor("w2ring", [128, 3, D], BF16))
    xn_tm = es.enter_context(nc.sbuf_tensor("xn_tm", [128, 2, D], BF16))
    probs_g = es.enter_context(nc.sbuf_tensor("probs_g", [128, 2048], F32))
    s12 = es.enter_context(nc.sbuf_tensor("s12", [128, D], F32))
    s3 = es.enter_context(nc.sbuf_tensor("s3", [128, 512], F32))
    probs_l = es.enter_context(nc.sbuf_tensor("probs_l", [128, TBN, E], F32))
    gates = es.enter_context(nc.sbuf_tensor("gates", [128, TBN, E], F32))
    kv_bf = es.enter_context(nc.sbuf_tensor("kv_bf", [128, 8, DH], BF16))
    sum1 = es.enter_context(nc.sbuf_tensor("sum1", [128, TBN], F32))
    sum2 = es.enter_context(nc.sbuf_tensor("sum2", [128, TBN], F32))
    mmean = es.enter_context(nc.sbuf_tensor("mmean", [128, TBN], F32))
    rstd = es.enter_context(nc.sbuf_tensor("rstd", [128, TBN], F32))
    lo_a = es.enter_context(nc.sbuf_tensor("lo_a", [128, E], F32))
    lo_b = es.enter_context(nc.sbuf_tensor("lo_b", [128, E], F32))
    hi_a = es.enter_context(nc.sbuf_tensor("hi_a", [128, E], F32))
    hi_b = es.enter_context(nc.sbuf_tensor("hi_b", [128, E], F32))
    mid = es.enter_context(nc.sbuf_tensor("mid", [128, E], F32))
    cntf = es.enter_context(nc.sbuf_tensor("cntf", [128, E], F32))
    cntb = es.enter_context(nc.sbuf_tensor("cntb", [128, E], BF16))
    msku = es.enter_context(nc.sbuf_tensor("msku", [128, E], U8))
    rmsc = es.enter_context(nc.sbuf_tensor("rmsc", [128, TBN, E], F32))
    rm1 = es.enter_context(nc.sbuf_tensor("rm1", [128, TBN], F32))
    rm0 = es.enter_context(nc.sbuf_tensor("rm0", [128, TBN], F32))
    rw = es.enter_context(nc.sbuf_tensor("rw", [128, TBN], F32))
    ram = es.enter_context(nc.sbuf_tensor("ram", [128, TBN], F32))
    zdot = es.enter_context(nc.sbuf_tensor("zdot", [128, H], F32))
    zval = es.enter_context(nc.sbuf_tensor("zval", [128, H], F32))
    wg_sb = es.enter_context(nc.sbuf_tensor("wg_sb", [128, DCH, E], BF16))
    bias_sb = es.enter_context(nc.sbuf_tensor("bias_sb", [1, 2, D], BF16))
    id_bf = es.enter_context(nc.sbuf_tensor("id_bf", [128, 128], BF16))
    id_f = es.enter_context(nc.sbuf_tensor("id_f", [128, 128], F32))
    ones1b = es.enter_context(nc.sbuf_tensor("ones1b", [1, 128], BF16))
    ones1f = es.enter_context(nc.sbuf_tensor("ones1f", [1, 128], F32))
    ones_sq = es.enter_context(nc.sbuf_tensor("ones_sq", [128, 128], BF16))
    ones_c = es.enter_context(nc.sbuf_tensor("ones_c", [128, 1], BF16))

    ksum_bc = probs_g[:, 0:1024]
    kv_acc = probs_g[0:64, 1008:2048].rearrange("p (h c) -> p h c", h=16)
    k_tm = h1g[:, 0:8, :].rearrange("p a b -> p (a b)").rearrange("p (tb d) -> p tb d", tb=WTB)
    v_tm = h1g[:, 8:16, :].rearrange("p a b -> p (a b)").rearrange("p (tb d) -> p tb d", tb=WTB)
    attn_fm = h1g[:, 0:8, :]
    q_tm = h1g[:, 8:16, :].rearrange("p a b -> p (a b)").rearrange("p (tb d) -> p tb d", tb=WTB)
    qz_fm = [xn_fm[:, 4, :], xn_fm[:, 5, :]]
    stat_dmy = h1g[:, 0:4, :].rearrange("p a b -> p (a b)").bitcast(F32)
    stage = s12[:, :].bitcast(BF16)

    pb = [es.enter_context(nc.psum_tensor(f"pb{i}", [128, 512], F32)) for i in range(8)]
    bank_fill = [None] * 8
    bank_free = [None] * 8

    def pe_bank_acquire(b):
        if bank_free[b] is not None:
            em.waitd("t", bank_free[b])

    def evac_bank_wait(eng, b):
        assert bank_fill[b] is not None
        em.waitd(eng, bank_fill[b])

    em.op("y", lambda h: h.dma_start(out=x[:, :, :],
          in_=xin.ap().rearrange("(tb tp) d -> tp tb d", tp=128)), dmainc="XL")

    for dst, src in ((id_bf, identb), (id_f, identf), (ones1b, onesb),
                     (ones1f, onesf), (ones_sq, ones128), (ones_c, onescol)):
        em.op("g", lambda h, dst=dst, src=src: h.dma_start(out=dst[:, :], in_=src[:, :]), dmainc="MI")
    em.op("g", lambda h: h.dma_start(out=wg_sb[:, :, :],
          in_=wg.ap().rearrange("(dc dp) e -> dp dc e", dp=128)), dmainc="MI")
    mi_consts = em.cnt["MI"]

    pc_w1 = [None] * E
    pc_w2 = [None] * E
    attn_cast = {}

    def precast(dst, src, nrow):
        cols = src.shape[-1]
        rows_per = 2048 // cols * 128
        last = None
        r = 0
        while r < nrow:
            rr = min(rows_per, nrow - r)
            a = rr // 128
            em.wait("g", "PC", em.cnt.get("PC", 0))
            em.op("g", lambda h, src=src, r=r, rr=rr, a=a, cols=cols: h.dma_start(
                out=stage[:, 0:a * cols].rearrange("p (a c) -> p a c", a=a),
                in_=src[r:r + rr, :].rearrange("(a dp) c -> dp a c", dp=128)), dmainc="PC")
            em.wait("g", "PC", em.cnt["PC"])
            last = em.op("g", lambda h, dst=dst, r=r, rr=rr, a=a, cols=cols: h.dma_start(
                out=dst[r:r + rr, :].rearrange("(a dp) c -> dp a c", dp=128),
                in_=stage[:, 0:a * cols].rearrange("p (a c) -> p a c", a=a)), dmainc="PC")
            r += rr
        return last

    pc_w1[0] = precast(w1s[0], w1[0], D)
    pc_w2[0] = precast(w2s[0], w2[0], FF)

    def precast_rest():
        for e in range(1, E):
            pc_w1[e] = precast(w1s[e], w1[e], D)
            pc_w2[e] = precast(w2s[e], w2[e], FF)
        for name, src, dst in (("wk", wk, wks), ("wv", wv, wvs), ("wq", wq, wqs), ("wo", wo, wos)):
            attn_cast[name] = precast(dst, src, D)

    def load_bias_pair(na, nb, tdep):
        src = {"bq": bq, "bk": bk, "bv": bv, "bo": bo}
        em.waitd("g", tdep)
        em.wait("g", "MI", em.cnt["MI"])
        em.op("g", lambda h, s=src[na]: h.dma_start(out=bias_sb[:, 0, :], in_=s[:, :]), dmainc="MI")
        em.wait("g", "MI", em.cnt["MI"])
        return em.op("g", lambda h, s=src[nb]: h.dma_start(out=bias_sb[:, 1, :], in_=s[:, :]), dmainc="MI")

    def ln_stats(xdep):
        em.waitd("s", xdep)
        sc = None
        for tb in range(TBN):
            em.op("s", lambda h, tb=tb: h.activation(
                stat_dmy[:, :], x[:, tb, :], AF.Square, accum_out=sum2[:, tb:tb + 1]), inc="S")
            sc = em.op("s", lambda h, tb=tb: h.activation(
                stat_dmy[:, :], x[:, tb, :], AF.Copy, accum_out=sum1[:, tb:tb + 1]), inc="S")
        em.wait("v", "S", sc)
        em.op("v", lambda h: h.tensor_scalar(mmean[:, :], sum1[:, :], 1.0 / D, None, ALU.mult), inc="V")
        em.chain("v", lambda h: h.tensor_mul(ram[:, :], mmean[:, :], mmean[:, :]))
        em.chain("v", lambda h: h.scalar_tensor_tensor(
            rw[:, :], sum2[:, :], 1.0 / D, ram[:, :], ALU.mult, ALU.subtract))
        vc = em.chain("v", lambda h: h.tensor_scalar(rm0[:, :], rw[:, :], 1e-5, None, ALU.add))
        if DEBUG and DBG_CALL[0] == 0:
            em.wait("y", "V", vc)
            em.op("y", lambda h: h.dma_start(out=dbg["sum1"][:, :], in_=sum1[:, :]), dmainc="DBG")
            em.op("y", lambda h: h.dma_start(out=dbg["sum2"][:, :], in_=sum2[:, :]), dmainc="DBG")
            em.op("y", lambda h: h.dma_start(out=dbg["var"][:, :], in_=rm0[:, :]), dmainc="DBG")
        em.wait("s", "V", vc)
        sc = em.op("s", lambda h: h.activation(rm1[:, :], rm0[:, :], AF.Sqrt), inc="S")
        em.wait("v", "S", sc)
        c = em.op("v", lambda h: h.reciprocal(rstd[:, :], rm1[:, :]), inc="V")
        return ("V", c)

    tp_state = {}

    def xn_window(w, stats_dep):
        for tb in range(WTB):
            gtb = w * WTB + tb
            slot = gtb % 2
            em.waitd("v", stats_dep)
            prev = tp_state.get(slot)
            if prev is not None:
                em.waitd("v", prev)
            em.op("v", lambda h, gtb=gtb, slot=slot: h.tensor_scalar(
                xn_tm[:, slot, :], x[:, gtb, :], mmean[:, gtb:gtb + 1],
                rstd[:, gtb:gtb + 1], ALU.subtract, ALU.mult), inc="V")
            nv = em.cnt["V"]
            em.wait("t", "V", nv)
            for bi in range(2):
                b = bi
                pe_bank_acquire(b)
                tc = None
                for k in range(4):
                    dc = bi * 4 + k
                    tc = em.op("t", lambda h, slot=slot, dc=dc, b=b, k=k: h.transpose(
                        pb[b][:, :].bitcast(BF16)[:, 0:512].rearrange("p (a c) -> p a c", a=4)[:, k, :],
                        xn_tm[:, slot, dc * 128:(dc + 1) * 128], id_bf[:, :]),
                        inc="T" if k == 3 else None)
                bank_fill[b] = ("T", tc)
                evac_bank_wait("v", b)
                ev = em.op("v", lambda h, b=b, bi=bi, tb=tb: h.tensor_copy(
                    xn_fm[:, bi * 4:(bi + 1) * 4, tb * 128:(tb + 1) * 128],
                    pb[b][:, :].bitcast(BF16)[:, 0:512].rearrange("p (a c) -> p a c", a=4)), inc="V")
                bank_free[b] = ("V", ev)
            tp_state[slot] = ("T", tc)
        return ("V", em.cnt["V"])

    w1r_free = [None, None]
    w1r_sem = ["W1R0", "W1R1"]
    aq = [0]

    def w1q_load(slot, dst_view, src_ap, pc_dep):
        em.waitd("y", w1r_free[slot])
        em.waitd("y", pc_dep)
        c = em.op("y", lambda h, dst_view=dst_view, src_ap=src_ap: h.dma_start(
            out=dst_view, in_=src_ap), dmainc=w1r_sem[slot])
        return (w1r_sem[slot], c)

    w2r_free = [None, None, None]
    w2r_sem = ["W2R0", "W2R1", "W2R2"]

    def ffn_call(xdep, post_ag=None):
        stats_dep = ln_stats(xdep)

        last_probs = None
        for w in range(NW):
            xn_dep = xn_window(w, stats_dep)
            b = 2
            pe_bank_acquire(b)
            em.wait("t", "MI", mi_consts)
            em.waitd("t", xn_dep)
            lg = pb[b][:, 0:32].rearrange("p (s e) -> p s e", s=NSUB)
            tc = None
            for sub in range(NSUB):
                for dc in range(DCH):
                    tc = em.op("t", lambda h, sub=sub, dc=dc, lg=lg: h.matmul(
                        lg[:, sub, :], xn_fm[:, dc, sub * 128:(sub + 1) * 128],
                        wg_sb[:, dc, :], start=(sub == 0 and dc == 0),
                        stop=(sub == NSUB - 1 and dc == DCH - 1)),
                        inc="T" if (dc == DCH - 1) else None)
            bank_fill[b] = ("T", tc)
            evac_bank_wait("v", b)
            evac_bank_wait("s", b)
            for sub in range(NSUB):
                gtb = w * WTB + sub
                em.op("v", lambda h, sub=sub, lg=lg: h.tensor_reduce(
                    s3[:, 0:1], lg[:, sub, :], AX.X, ALU.max, negate=True), inc="V")
                vc = em.cnt["V"]
                em.wait("s", "V", vc)
                sc = em.op("s", lambda h, sub=sub, lg=lg: h.activation(
                    s3[:, 8:16], lg[:, sub, :], AF.Exp, bias=s3[:, 0:1]), inc="S")
                em.wait("v", "S", sc)
                em.op("v", lambda h: h.tensor_reduce(
                    s3[:, 1:2], s3[:, 8:16], AX.X, ALU.add), inc="V")
                em.chain("v", lambda h: h.reciprocal(s3[:, 2:3], s3[:, 1:2]))
                last_probs = em.chain("v", lambda h, gtb=gtb: h.tensor_scalar(
                    probs_l[:, gtb, :], s3[:, 8:16], s3[:, 2:3], None, ALU.mult))
            bank_free[b] = ("V", last_probs)

        em.wait("g", "V", last_probs)
        em.wait("g", "MI", em.cnt["MI"])
        dmp = em.op("g", lambda h: h.dma_start(
            out=ag_in[:, :], in_=probs_l[:, :, :].rearrange("p a e -> p (a e)")), dmainc="MI")
        em.wait("g", "MI", dmp)
        em.op("g", lambda h: h.collective_compute(
            "AllGather", ALU.bypass, replica_groups=[list(range(8))],
            ins=[ag_in.ap().opt()], outs=[ag_out.ap().opt()]), inc="CC")
        em.wait("g", "CC", em.cnt["CC"])
        pgl = em.op("g", lambda h: h.dma_start(
            out=probs_g[:, :].rearrange("p (r c) -> p r c", r=8),
            in_=ag_out.ap().rearrange("(r p) c -> p r c", p=128)), dmainc="MI")
        if post_ag is not None:
            post_ag()

        em.wait("v", "MI", pgl)
        em.op("v", lambda h: h.memset(lo_a[:, :], 0.0), inc="V")
        em.op("v", lambda h: h.memset(hi_a[:, :], 1.0), inc="V")
        pg4 = probs_g[:, :].rearrange("p (r tb e) -> p r tb e", r=8, e=8)
        s3v = s3[:, :].rearrange("p (r tb) -> p r tb", r=8)[:, :, 0:32]
        lo, hi, lon, hin = lo_a, hi_a, lo_b, hi_b
        for it in range(NBIS):
            em.chain("v", lambda h, lo=lo, hi=hi: h.tensor_add(mid[:, :], lo[:, :], hi[:, :]))
            em.chain("v", lambda h: h.tensor_scalar(mid[:, :], mid[:, :], 0.5, None, ALU.mult))
            first = True
            for e in range(E):
                if first:
                    em.chain("v", lambda h, e=e: h.tensor_scalar(
                        s3v, pg4[:, :, :, e], mid[:, e:e + 1], 0.0, ALU.is_gt, ALU.add,
                        accum_out=cntf[:, e:e + 1]))
                    first = False
                else:
                    em.op("v", lambda h, e=e: h.tensor_scalar(
                        s3v, pg4[:, :, :, e], mid[:, e:e + 1], 0.0, ALU.is_gt, ALU.add,
                        accum_out=cntf[:, e:e + 1]), inc="V")
            vc = em.chain("v", lambda h: h.tensor_copy(cntb[:, :], cntf[:, :]))
            b = 3
            pe_bank_acquire(b)
            em.wait("t", "V", vc)
            tc = em.op("t", lambda h, b=b: h.matmul(
                pb[b][:, 0:E], ones_sq[:, :], cntb[:, :], start=True, stop=True), inc="T")
            bank_fill[b] = ("T", tc)
            evac_bank_wait("v", b)
            em.op("v", lambda h, b=b: h.tensor_scalar(
                msku[:, :], pb[b][:, 0:E], float(CAP) - 0.5, None, ALU.is_ge), inc="V")
            em.chain("v", lambda h, lo=lo, lon=lon: h.select(
                lon[:, :], msku[:, :], mid[:, :], lo[:, :]))
            ev = em.chain("v", lambda h, hi=hi, hin=hin: h.select(
                hin[:, :], msku[:, :], hi[:, :], mid[:, :]))
            bank_free[b] = ("V", ev)
            lo, lon = lon, lo
            hi, hin = hin, hi
        theta = lo

        th_b = theta[:, :].rearrange("p (o e) -> p o e", o=1).broadcast_to([128, TBN, E])
        em.chain("v", lambda h, th_b=th_b: h.tensor_tensor(
            rmsc[:, :, :], probs_l[:, :, :], th_b, ALU.is_gt))
        em.chain("v", lambda h: h.tensor_mul(gates[:, :, :], probs_l[:, :, :], rmsc[:, :, :]))
        em.chain("v", lambda h: h.scalar_tensor_tensor(
            rmsc[:, :, :], rmsc[:, :, :], 1.0, gates[:, :, :], ALU.subtract, ALU.add))
        em.chain("v", lambda h: h.tensor_reduce(rm1[:, :], rmsc[:, :, :], AX.X, ALU.max))
        em.chain("v", lambda h: h.tensor_reduce(rm0[:, :], probs_l[:, :, :], AX.X, ALU.max))
        em.chain("v", lambda h: h.tensor_scalar(ram[:, :], rm1[:, :], -0.5, None, ALU.is_gt))
        em.chain("v", lambda h: h.tensor_mul(rw[:, :], ram[:, :], rm1[:, :]))
        em.chain("v", lambda h: h.tensor_scalar(
            ram[:, :], ram[:, :], -1.0, 1.0, ALU.mult, ALU.add))
        em.chain("v", lambda h: h.tensor_mul(ram[:, :], ram[:, :], rm0[:, :]))
        em.chain("v", lambda h: h.tensor_add(rw[:, :], rw[:, :], ram[:, :]))
        w_b = rw[:, :].rearrange("p (a o) -> p a o", o=1).broadcast_to([128, TBN, E])
        em.chain("v", lambda h, w_b=w_b: h.tensor_tensor(
            rmsc[:, :, :], probs_l[:, :, :], w_b, ALU.is_equal))
        em.chain("v", lambda h: h.tensor_scalar(rw[:, :], rw[:, :], 0.5, None, ALU.mult))
        gts = None
        first = True
        for tb in range(TBN):
            if first:
                gts = em.chain("v", lambda h, tb=tb: h.tensor_scalar(
                    gates[:, tb, :], rmsc[:, tb, :], rw[:, tb:tb + 1], None, ALU.mult))
                first = False
            else:
                gts = em.op("v", lambda h, tb=tb: h.tensor_scalar(
                    gates[:, tb, :], rmsc[:, tb, :], rw[:, tb:tb + 1], None, ALU.mult), inc="V")
        gates_dep = ("V", gts)
        if DEBUG and DBG_CALL[0] == 0:
            em.wait("y", "V", gts)
            em.op("y", lambda h: h.dma_start(out=dbg["mmean"][:, :], in_=mmean[:, :]), dmainc="DBG")
            em.op("y", lambda h: h.dma_start(out=dbg["rstd"][:, :], in_=rstd[:, :]), dmainc="DBG")
            em.op("y", lambda h: h.dma_start(out=dbg["probs"][:, :, :], in_=probs_l[:, :, :]), dmainc="DBG")
            em.op("y", lambda h: h.dma_start(out=dbg["theta"][:, :], in_=theta[:, :]), dmainc="DBG")
            em.op("y", lambda h: h.dma_start(out=dbg["gates"][:, :, :], in_=gates[:, :, :]), dmainc="DBG")
            em.op("y", lambda h: h.dma_start(out=dbg["cnt"][:, :], in_=cntf[:, :]), dmainc="DBG")

        h1_ready = [None] * FCH
        h1_freed = [None] * FCH
        for w in range(NW):
            xn_dep = xn_window(w, stats_dep)
            for e in range(E):
                for qt in range(4):
                    slot = aq[0] % 2
                    aq[0] += 1
                    src = w1s[e, :, qt * 512:(qt + 1) * 512].rearrange(
                        "(dc dp) f -> dp dc f", dp=128)
                    dstv = w1ring[:, slot, :].rearrange("p (dc f) -> p dc f", dc=8)
                    ldep = w1q_load(slot, dstv, src, ("PC", pc_w1[e]) if w == 0 else None)
                    em.waitd("t", ldep)
                    em.waitd("t", xn_dep)
                    w1v = dstv
                    tc = None
                    for fi in range(4):
                        f = qt * 4 + fi
                        b = f % 2
                        pe_bank_acquire(b)
                        if h1_freed[f] is not None:
                            em.waitd("s", h1_freed[f])
                        for dc in range(DCH):
                            tc = em.op("t", lambda h, w1v=w1v, fi=fi, dc=dc, b=b: h.matmul(
                                pb[b][:, :], w1v[:, dc, fi * 128:(fi + 1) * 128],
                                xn_fm[:, dc, :], start=(dc == 0), stop=(dc == DCH - 1)),
                                inc="T" if dc == DCH - 1 else None)
                        bank_fill[b] = ("T", tc)
                        evac_bank_wait("s", b)
                        sc = em.op("s", lambda h, b=b, f=f: h.activation(
                            h1g[:, f, :], pb[b][:, :], AF.Gelu), inc="S")
                        bank_free[b] = ("S", sc)
                        h1_ready[f] = ("S", sc)
                    w1r_free[slot] = ("T", tc)
                if DEBUG and DBG_CALL[0] == 0 and w == 0 and e == 0:
                    em.wait("g", "S", em.cnt["S"])
                    em.op("g", lambda h: h.dma_start(out=dbg["xnfm"][:, :, :], in_=xn_fm[:, :, :]), dmainc="DBG")
                    em.op("g", lambda h: h.dma_start(out=dbg["h1g"][:, :, :], in_=h1g[:, :, :]), dmainc="DBG")
                tc = None
                for f in range(FCH):
                    slot = f % 3
                    em.waitd("y", w2r_free[slot])
                    if w == 0:
                        em.wait("y", "PC", pc_w2[e])
                    lc = em.op("y", lambda h, slot=slot, e=e, f=f: h.dma_start(
                        out=w2ring[:, slot, :], in_=w2s[e, f * 128:(f + 1) * 128, :]),
                        dmainc=w2r_sem[slot])
                    em.wait("t", w2r_sem[slot], lc)
                    em.waitd("t", h1_ready[f])
                    for sub in range(NSUB):
                        for jh in range(2):
                            b = sub * 2 + jh
                            if f == 0:
                                pe_bank_acquire(b)
                            tc = em.op("t", lambda h, f=f, sub=sub, jh=jh, b=b, slot=slot: h.matmul(
                                pb[b][:, :], h1g[:, f, sub * 128:(sub + 1) * 128],
                                w2ring[:, slot, jh * 512:(jh + 1) * 512],
                                start=(f == 0), stop=(f == FCH - 1)),
                                inc="T" if (f == FCH - 1 or (sub == NSUB - 1 and jh == 1)) else None)
                    h1_freed[f] = ("T", tc)
                    w2r_free[slot] = ("T", tc)
                for sub in range(NSUB):
                    for jh in range(2):
                        bank_fill[sub * 2 + jh] = ("T", tc)
                em.waitd("v", gates_dep)
                for sub in range(NSUB):
                    gtb = w * WTB + sub
                    for jh in range(2):
                        b = sub * 2 + jh
                        evac_bank_wait("v", b)
                        ev = em.op("v", lambda h, b=b, gtb=gtb, jh=jh, e=e: h.scalar_tensor_tensor(
                            x[:, gtb, jh * 512:(jh + 1) * 512], pb[b][:, :],
                            gates[:, gtb, e:e + 1], x[:, gtb, jh * 512:(jh + 1) * 512],
                            ALU.mult, ALU.add), inc="V")
                        bank_free[b] = ("V", ev)
        DBG_CALL[0] += 1
        return ("V", em.cnt["V"])

    def attn_call(xdep):
        stats_dep = ln_stats(xdep)
        bias_c = load_bias_pair("bk", "bv", None)

        em.waitd("v", xdep)
        kvz = em.op("v", lambda h: h.memset(
            kv_acc[:, :, :].rearrange("p a c -> p (a c)"), 0.0), inc="V")

        for w in range(NW):
            xn_dep = xn_window(w, stats_dep)
            for proj in ("k", "v"):
                wsrc = wks if proj == "k" else wvs
                btile = 0 if proj == "k" else 1
                dst = k_tm if proj == "k" else v_tm
                cast_dep = attn_cast["wk" if proj == "k" else "wv"]
                for qt in range(4):
                    slot = aq[0] % 2
                    aq[0] += 1
                    src = wsrc[qt * 256:(qt + 1) * 256, :].rearrange(
                        "(dc dp) j -> dp dc j", dp=128)
                    dstv = w1ring[:, slot, 0:2048].rearrange("p (dc j) -> p dc j", dc=2)
                    ldep = w1q_load(slot, dstv, src, ("PC", cast_dep) if w == 0 else None)
                    em.waitd("t", ldep)
                    em.waitd("t", xn_dep)
                    tc = None
                    for dcr in range(2):
                        dc = qt * 2 + dcr
                        for sub in range(NSUB):
                            for jh in range(2):
                                b = sub * 2 + jh
                                if dc == 0:
                                    pe_bank_acquire(b)
                                tc = em.op("t", lambda h, dstv=dstv, dcr=dcr, dc=dc, sub=sub, jh=jh, b=b: h.matmul(
                                    pb[b][:, :], xn_fm[:, dc, sub * 128:(sub + 1) * 128],
                                    dstv[:, dcr, jh * 512:(jh + 1) * 512],
                                    start=(dc == 0), stop=False),
                                    inc="T" if (dcr == 1 and sub == NSUB - 1 and jh == 1) else None)
                    w1r_free[slot] = ("T", tc)
                em.wait("t", "MI", bias_c)
                for sub in range(NSUB):
                    for jh in range(2):
                        b = sub * 2 + jh
                        tc = em.op("t", lambda h, btile=btile, jh=jh, b=b: h.matmul(
                            pb[b][:, :], ones1b[:, :], bias_sb[:, btile, jh * 512:(jh + 1) * 512],
                            start=False, stop=True), inc="T")
                        bank_fill[b] = ("T", tc)
                em.wait("v", "PC", em.cnt["PC"] if w == -1 else pc_done[0])
                for sub in range(NSUB):
                    for jh in range(2):
                        b = sub * 2 + jh
                        evac_bank_wait("v", b)
                        if proj == "k":
                            em.chain("v", lambda h, b=b: h.tensor_scalar(
                                s12[:, 0:512], pb[b][:, :], 0.0, None, ALU.min))
                            vc = em.op("v", lambda h, b=b: h.tensor_scalar(
                                s12[:, 512:1024], pb[b][:, :], 0.0, None, ALU.max), inc="V")
                            em.wait("s", "V", vc)
                            sc = em.op("s", lambda h: h.activation(
                                s12[:, 0:512], s12[:, 0:512], AF.Exp), inc="S")
                            em.wait("v", "S", sc)
                            ev = em.op("v", lambda h, dst=dst, sub=sub, jh=jh: h.tensor_add(
                                dst[:, sub, jh * 512:(jh + 1) * 512],
                                s12[:, 0:512], s12[:, 512:1024]), inc="V")
                        else:
                            ev = em.op("v", lambda h, dst=dst, sub=sub, jh=jh, b=b: h.tensor_copy(
                                dst[:, sub, jh * 512:(jh + 1) * 512], pb[b][:, :]), inc="V")
                        bank_free[b] = ("V", ev)
            kv_dep = ("V", em.cnt["V"])
            for b in (0, 1, 2):
                pe_bank_acquire(b)
            em.waitd("t", kv_dep)
            em.wait("t", "V", kvz)
            em.wait("t", "MI", mi_consts)
            tc = None
            first = True
            for tb in range(WTB):
                for hh in range(H):
                    b = hh // 8
                    reg = pb[b][:, :].rearrange("p (a c) -> p a c", a=8)[0:64, hh % 8, :]
                    em.op("t", lambda h, tb=tb, hh=hh, reg=reg, first=first: h.matmul(
                        reg, k_tm[:, tb, hh * 64:(hh + 1) * 64],
                        v_tm[:, tb, hh * 64:(hh + 1) * 64],
                        start=first and (hh % 8 == 0) and tb == 0,
                        stop=(tb == WTB - 1)), inc=None)
                    ksreg = pb[2][0:64, hh:hh + 1]
                    tc = em.op("t", lambda h, tb=tb, hh=hh, ksreg=ksreg: h.matmul(
                        ksreg, k_tm[:, tb, hh * 64:(hh + 1) * 64], ones_c[:, :],
                        start=(tb == 0 and hh == 0), stop=(tb == WTB - 1)),
                        inc="T" if (tb == WTB - 1 and hh == H - 1) else None)
            for b in (0, 1, 2):
                bank_fill[b] = ("T", tc)
            for b in (0, 1):
                evac_bank_wait("v", b)
                ev = em.op("v", lambda h, b=b: h.tensor_add(
                    kv_acc[:, b * 8:(b + 1) * 8, 0:64],
                    pb[b][:, :].rearrange("p (a c) -> p a c", a=8)[0:64, :, :],
                    kv_acc[:, b * 8:(b + 1) * 8, 0:64]), inc="V")
                bank_free[b] = ("V", ev)
            evac_bank_wait("v", 2)
            ev = em.op("v", lambda h: h.tensor_add(
                kv_acc[:, :, 64:65].rearrange("p a c -> p (a c)"),
                pb[2][0:64, 0:16],
                kv_acc[:, :, 64:65].rearrange("p a c -> p (a c)")), inc="V")
            bank_free[2] = ("V", ev)
        kv_all = em.cnt["V"]

        em.wait("g", "V", kv_all)
        em.wait("g", "MI", em.cnt["MI"])
        d1 = em.op("g", lambda h: h.dma_start(
            out=ar_in[:, :], in_=kv_acc[:, :, :].rearrange("p a c -> p (a c)")), dmainc="MI")
        em.wait("g", "MI", d1)
        em.op("g", lambda h: h.collective_compute(
            "AllReduce", ALU.add, replica_groups=[[0, 1], [2, 3], [4, 5], [6, 7]],
            ins=[ar_in.ap().opt()], outs=[ar_out.ap().opt()]), inc="CC")
        em.wait("g", "CC", em.cnt["CC"])
        d2 = em.op("g", lambda h: h.dma_start(
            out=kv_acc[:, :, :].rearrange("p a c -> p (a c)"), in_=ar_out[:, :]), dmainc="MI")
        kvbf_d = None
        for two in range(2):
            em.wait("g", "MI", em.cnt["MI"])
            kvbf_d = em.op("g", lambda h, two=two: h.dma_start(
                out=kv_bf[two * 64:(two + 1) * 64, :, :],
                in_=ar_out.ap().rearrange("d (hh two c) -> two d hh c", two=2, c=65)[two, :, :, 0:64]), dmainc="MI")
        b = 4
        pe_bank_acquire(b)
        em.wait("t", "MI", d2)
        tc = em.op("t", lambda h, b=b: h.transpose(
            pb[b][0:16, 0:64], kv_acc[:, :, 64:65].rearrange("p a c -> p (a c)"),
            id_f[0:64, 0:64]), inc="T")
        bank_fill[b] = ("T", tc)
        evac_bank_wait("v", b)
        ks_ev = em.op("v", lambda h, b=b: h.tensor_copy(
            s12[0:16, 64:128], pb[b][0:16, 0:64]), inc="V")
        bank_free[b] = ("V", ks_ev)
        em.wait("g", "V", ks_ev)
        em.wait("g", "MI", em.cnt["MI"])
        em.op("g", lambda h: h.dma_start(
            out=kst[:, :], in_=s12[0:16, 64:128]), dmainc="MI")
        em.wait("g", "MI", em.cnt["MI"])
        d3 = em.op("g", lambda h: h.dma_start(
            out=s12[0:1, 0:1024],
            in_=kst.ap().rearrange("h d -> (h d)").rearrange("(o c) -> o c", o=1)), dmainc="MI")
        ks_fill = []
        for half in range(2):
            b = 4 + half
            pe_bank_acquire(b)
            em.wait("t", "MI", d3)
            tc = em.op("t", lambda h, b=b, half=half: h.matmul(
                pb[b][:, :], ones1f[:, :], s12[0:1, half * 512:(half + 1) * 512],
                start=True, stop=True), inc="T")
            bank_fill[b] = ("T", tc)
            evac_bank_wait("v", b)
            ev = em.op("v", lambda h, b=b, half=half: h.tensor_copy(
                ksum_bc[:, half * 512:(half + 1) * 512], pb[b][:, :]), inc="V")
            bank_free[b] = ("V", ev)
        ksbc_c = em.cnt["V"]

        bias2_c = load_bias_pair("bq", "bo", ("T", em.cnt["T"]))

        for w in range(NW):
            xn_dep = xn_window(w, stats_dep)
            for qt in range(4):
                slot = aq[0] % 2
                aq[0] += 1
                src = wqs[qt * 256:(qt + 1) * 256, :].rearrange("(dc dp) j -> dp dc j", dp=128)
                dstv = w1ring[:, slot, 0:2048].rearrange("p (dc j) -> p dc j", dc=2)
                ldep = w1q_load(slot, dstv, src, ("PC", attn_cast["wq"]) if w == 0 else None)
                em.waitd("t", ldep)
                em.waitd("t", xn_dep)
                tc = None
                for dcr in range(2):
                    dc = qt * 2 + dcr
                    for sub in range(NSUB):
                        for jh in range(2):
                            b = sub * 2 + jh
                            if dc == 0:
                                pe_bank_acquire(b)
                            tc = em.op("t", lambda h, dstv=dstv, dcr=dcr, dc=dc, sub=sub, jh=jh, b=b: h.matmul(
                                pb[b][:, :], xn_fm[:, dc, sub * 128:(sub + 1) * 128],
                                dstv[:, dcr, jh * 512:(jh + 1) * 512],
                                start=(dc == 0), stop=False),
                                inc="T" if (dcr == 1 and sub == NSUB - 1 and jh == 1) else None)
                w1r_free[slot] = ("T", tc)
            em.wait("t", "MI", bias2_c)
            for sub in range(NSUB):
                for jh in range(2):
                    b = sub * 2 + jh
                    tc = em.op("t", lambda h, jh=jh, b=b: h.matmul(
                        pb[b][:, :], ones1b[:, :], bias_sb[:, 0, jh * 512:(jh + 1) * 512],
                        start=False, stop=True), inc="T")
                    bank_fill[b] = ("T", tc)
            for sub in range(NSUB):
                for jh in range(2):
                    b = sub * 2 + jh
                    evac_bank_wait("v", b)
                    em.chain("v", lambda h, b=b: h.tensor_scalar(
                        s12[:, 0:512], pb[b][:, :], 0.0, None, ALU.min))
                    vc = em.op("v", lambda h, b=b: h.tensor_scalar(
                        s12[:, 512:1024], pb[b][:, :], 0.0, None, ALU.max), inc="V")
                    em.wait("s", "V", vc)
                    sc = em.op("s", lambda h: h.activation(
                        s12[:, 0:512], s12[:, 0:512], AF.Exp), inc="S")
                    em.wait("v", "S", sc)
                    ev = em.op("v", lambda h, sub=sub, jh=jh: h.tensor_add(
                        q_tm[:, sub, jh * 512:(jh + 1) * 512],
                        s12[:, 0:512], s12[:, 512:1024]), inc="V")
                    bank_free[b] = ("V", ev)
            em.wait("v", "V", ksbc_c)
            qz_c = None
            for tb in range(WTB):
                em.chain("v", lambda h, tb=tb: h.tensor_tensor(
                    s12[:, :].rearrange("p (a c) -> p a c", a=16),
                    q_tm[:, tb, :].rearrange("p (a c) -> p a c", a=16),
                    ksum_bc[:, :].rearrange("p (a c) -> p a c", a=16), ALU.mult))
                em.chain("v", lambda h: h.tensor_reduce(
                    zdot[:, :], s12[:, :].rearrange("p (a c) -> p a c", a=16),
                    AX.X, ALU.add))
                em.chain("v", lambda h: h.tensor_scalar(
                    zdot[:, :], zdot[:, :], 1e-6, None, ALU.add))
                em.chain("v", lambda h: h.reciprocal(zval[:, :], zdot[:, :]))
                zb = zval[:, :].rearrange("p (a o) -> p a o", o=1).broadcast_to([128, H, DH])
                qz_c = em.chain("v", lambda h, tb=tb, zb=zb: h.tensor_tensor(
                    q_tm[:, tb, :].rearrange("p (a c) -> p a c", a=16),
                    q_tm[:, tb, :].rearrange("p (a c) -> p a c", a=16), zb, ALU.mult))
            em.wait("t", "MI", kvbf_d)
            sc_banks = [2, 3, 4, 5, 6, 7]
            sc_i = [0]
            for jch in range(8):
                ring = jch % 2
                em.wait("t", "V", qz_c)
                b = ring
                pe_bank_acquire(b)
                tc = None
                for tb in range(WTB):
                    tc = em.op("t", lambda h, tb=tb, jch=jch, b=b: h.transpose(
                        pb[b][:, :].bitcast(BF16)[:, 0:512].rearrange("p (a c) -> p a c", a=4)[:, tb, :],
                        q_tm[:, tb, jch * 128:(jch + 1) * 128], id_bf[:, :]),
                        inc="T" if tb == WTB - 1 else None)
                bank_fill[b] = ("T", tc)
                evac_bank_wait("v", b)
                ev = em.op("v", lambda h, b=b, ring=ring: h.tensor_copy(
                    qz_fm[ring][:, :].rearrange("p (a c) -> p a c", a=4),
                    pb[b][:, :].bitcast(BF16)[:, 0:512].rearrange("p (a c) -> p a c", a=4)), inc="V")
                bank_free[b] = ("V", ev)
                for hr in range(2):
                    hh = jch * 2 + hr
                    sb = sc_banks[sc_i[0] % 6]
                    sc_i[0] += 1
                    pe_bank_acquire(sb)
                    em.wait("t", "V", ev)
                    tp = (64, 64) if hr == 1 else None
                    tc = em.op("t", lambda h, hh=hh, hr=hr, ring=ring, sb=sb, tp=tp: h.matmul(
                        pb[sb][hr * 64:(hr + 1) * 64, :],
                        kv_bf[hr * 64:(hr + 1) * 64, hh // 2, :],
                        qz_fm[ring][hr * 64:(hr + 1) * 64, :], start=True, stop=True,
                        tile_position=tp), inc="T")
                    bank_fill[sb] = ("T", tc)
                    evac_bank_wait("v", sb)
                    ev2 = em.op("v", lambda h, hh=hh, hr=hr, jch=jch, sb=sb: h.tensor_copy(
                        attn_fm[hr * 64:(hr + 1) * 64, jch, :],
                        pb[sb][hr * 64:(hr + 1) * 64, :]), inc="V")
                    bank_free[sb] = ("V", ev2)
            af_dep = ("V", em.cnt["V"])
            for qt in range(4):
                slot = aq[0] % 2
                aq[0] += 1
                src = wos[qt * 256:(qt + 1) * 256, :].rearrange("(dc dp) j -> dp dc j", dp=128)
                dstv = w1ring[:, slot, 0:2048].rearrange("p (dc j) -> p dc j", dc=2)
                ldep = w1q_load(slot, dstv, src, ("PC", attn_cast["wo"]) if w == 0 else None)
                em.waitd("t", ldep)
                em.waitd("t", af_dep)
                tc = None
                for dcr in range(2):
                    dc = qt * 2 + dcr
                    for sub in range(NSUB):
                        for jh in range(2):
                            b = sub * 2 + jh
                            if dc == 0:
                                pe_bank_acquire(b)
                            tc = em.op("t", lambda h, dstv=dstv, dcr=dcr, dc=dc, sub=sub, jh=jh, b=b: h.matmul(
                                pb[b][:, :], attn_fm[:, dc, sub * 128:(sub + 1) * 128],
                                dstv[:, dcr, jh * 512:(jh + 1) * 512],
                                start=(dc == 0), stop=False),
                                inc="T" if (dcr == 1 and sub == NSUB - 1 and jh == 1) else None)
                w1r_free[slot] = ("T", tc)
            for sub in range(NSUB):
                for jh in range(2):
                    b = sub * 2 + jh
                    tc = em.op("t", lambda h, jh=jh, b=b: h.matmul(
                        pb[b][:, :], ones1b[:, :], bias_sb[:, 1, jh * 512:(jh + 1) * 512],
                        start=False, stop=True), inc="T")
                    bank_fill[b] = ("T", tc)
            for sub in range(NSUB):
                gtb = w * WTB + sub
                for jh in range(2):
                    b = sub * 2 + jh
                    evac_bank_wait("v", b)
                    ev = em.op("v", lambda h, b=b, gtb=gtb, jh=jh: h.tensor_add(
                        x[:, gtb, jh * 512:(jh + 1) * 512], pb[b][:, :],
                        x[:, gtb, jh * 512:(jh + 1) * 512]), inc="V")
                    bank_free[b] = ("V", ev)
        return ("V", em.cnt["V"])

    dep = ("XL", 16)
    pc_done = [0]

    def _hook():
        precast_rest()
        pc_done[0] = em.cnt["PC"]
    if "ffn1" in STAGES:
        dep = ffn_call(dep, post_ag=_hook)
    else:
        _hook()
    if "attn" in STAGES:
        dep = attn_call(dep)
    if "ffn2" in STAGES:
        dep = ffn_call(dep)

    for w in range(NW):
        em.waitd("y", dep)
        em.op("y", lambda h, w=w: h.dma_start(
            out=out[w * TWIN:(w + 1) * TWIN, :].rearrange("(tb tp) d -> tp tb d", tp=128),
            in_=x[:, w * WTB:(w + 1) * WTB, :]), dmainc="ST")
    em.wait("y", "ST", em.cnt["ST"])
    if DEBUG:
        em.wait("y", "DBG", em.cnt.get("DBG", 0))

    with em._es:
        with nc.Block() as block:
            @block.tensor
            def _(h):
                for fn in em.q["t"]:
                    fn(h)

            @block.scalar
            def _(h):
                for fn in em.q["s"]:
                    fn(h)

            @block.vector
            def _(h):
                for fn in em.q["v"]:
                    fn(h)

            @block.gpsimd
            def _(h):
                for fn in em.q["g"]:
                    fn(h)

            @block.sync
            def _(h):
                for fn in em.q["y"]:
                    fn(h)
    return nc


_NC_CACHE = {}


def kernel(L0_attn_x, L0_attn_wq, L0_attn_bq, L0_attn_wk, L0_attn_bk,
           L0_attn_wv, L0_attn_bv, L0_attn_wo, L0_attn_bo,
           L0_ffn_Wg, L0_ffn_W1, L0_ffn_W2):
    xfull = np.ascontiguousarray(np.asarray(L0_attn_x, dtype=np.float32))
    shared = dict(
        w1=np.ascontiguousarray(np.asarray(L0_ffn_W1, dtype=np.float32)),
        w2=np.ascontiguousarray(np.asarray(L0_ffn_W2, dtype=np.float32)),
        wg=np.ascontiguousarray(np.asarray(L0_ffn_Wg, dtype=np.float32)),
        wq=np.ascontiguousarray(np.asarray(L0_attn_wq, dtype=np.float32)),
        wk=np.ascontiguousarray(np.asarray(L0_attn_wk, dtype=np.float32)),
        wv=np.ascontiguousarray(np.asarray(L0_attn_wv, dtype=np.float32)),
        wo=np.ascontiguousarray(np.asarray(L0_attn_wo, dtype=np.float32)),
        bq=np.ascontiguousarray(np.asarray(L0_attn_bq, dtype=np.float32)).reshape(1, D),
        bk=np.ascontiguousarray(np.asarray(L0_attn_bk, dtype=np.float32)).reshape(1, D),
        bv=np.ascontiguousarray(np.asarray(L0_attn_bv, dtype=np.float32)).reshape(1, D),
        bo=np.ascontiguousarray(np.asarray(L0_attn_bo, dtype=np.float32)).reshape(1, D),
        identb=np.eye(128, dtype=ml_dtypes.bfloat16),
        identf=np.eye(128, dtype=np.float32),
        onesb=np.ones((1, 128), dtype=ml_dtypes.bfloat16),
        onesf=np.ones((1, 128), dtype=np.float32),
        ones128=np.ones((128, 128), dtype=ml_dtypes.bfloat16),
        onescol=np.ones((128, 1), dtype=ml_dtypes.bfloat16),
    )
    in_maps = []
    for c in range(NCORES):
        b, half = c // 2, c % 2
        shard = np.ascontiguousarray(xfull[b, half * NT:(half + 1) * NT, :])
        in_maps.append(dict(xin=shard, **shared))

    key = STAGES
    if key not in _NC_CACHE:
        _NC_CACHE[key] = build_nc()
    nc = _NC_CACHE[key]
    res = run_bass_kernel_spmd(nc, in_maps, core_ids=list(range(NCORES)))
    outp = np.empty((4, 8192, D), dtype=np.float32)
    for c in range(NCORES):
        b, half = c // 2, c % 2
        outp[b, half * NT:(half + 1) * NT, :] = np.asarray(res.results[c]["out"])
    if DEBUG:
        global DEBUG_RES
        DEBUG_RES = [{k: np.asarray(v) for k, v in r.items() if k.startswith("d_")}
                     for r in res.results]
    return outp


# version 06
# speedup vs baseline: 1.1208x; 1.0514x over previous
import numpy as np
import ml_dtypes
import contextlib
import concourse.bass as bass
import concourse.mybir as mybir
from concourse.bass_utils import run_bass_kernel_spmd

F32 = mybir.dt.float32
BF16 = mybir.dt.bfloat16
U8 = mybir.dt.uint8
AF = mybir.ActivationFunctionType
ALU = mybir.AluOpType
AX = mybir.AxisListType

NCORES = 8
NT = 4096
TBN = 32
D = 1024
DCH = 8
FF = 2048
FCH = 16
E = 8
H = 16
DH = 64
NW = 8
WTB = 4
TWIN = 512
NSUB = 4
CAP = 4096
NBIS = 26

STAGES = ("ffn1", "attn", "ffn2")
DEBUG = False
DBG_CALL = [0]


class EM:
    ENGS = ("t", "s", "v", "g", "y")

    def __init__(self, nc):
        self.nc = nc
        self.q = {e: [] for e in self.ENGS}
        self.sems = {}
        self.cnt = {}
        self._es = contextlib.ExitStack()
        self.last_wait = {}

    def sem(self, name):
        if name not in self.sems:
            self.sems[name] = self._es.enter_context(self.nc.semaphore(name))
            self.cnt[name] = 0
        return self.sems[name]

    def wait(self, eng, name, val):
        if val is None or val <= 0:
            return
        self.sem(name)
        assert val <= self.cnt[name], f"wait on future count {name} {val} > {self.cnt[name]}"
        key = (eng, name)
        if self.last_wait.get(key, 0) >= val:
            return
        self.last_wait[key] = val
        s = self.sems[name]
        self.q[eng].append(lambda h, s=s, v=val: h.wait_ge(s, v))

    def waitd(self, eng, dep):
        if dep is not None:
            self.wait(eng, dep[0], dep[1])

    ESEM = {"t": "T", "s": "S", "v": "V", "g": "G", "y": "Y"}

    def chain(self, eng, fn):
        nm = self.ESEM[eng]
        self.sem(nm)
        self.wait(eng, nm, self.cnt[nm])
        return self.op(eng, fn, inc=nm)

    def op(self, eng, fn, inc=None, dmainc=None):
        if inc is not None:
            s = self.sem(inc)
            self.cnt[inc] += 1
            c = self.cnt[inc]
            self.q[eng].append(lambda h, fn=fn, s=s: fn(h).then_inc(s, 1))
            return c
        if dmainc is not None:
            s = self.sem(dmainc)
            self.cnt[dmainc] += 16
            c = self.cnt[dmainc]
            self.q[eng].append(lambda h, fn=fn, s=s: fn(h).then_inc(s, 16))
            return c
        self.q[eng].append(fn)
        return None


def build_nc():
    nc = bass.Bass("TRN2", target_bir_lowering=False)

    xin = nc.dram_tensor("xin", [NT, D], F32, kind="ExternalInput")
    w1 = nc.dram_tensor("w1", [E, D, FF], F32, kind="ExternalInput")
    w2 = nc.dram_tensor("w2", [E, FF, D], F32, kind="ExternalInput")
    wg = nc.dram_tensor("wg", [D, E], F32, kind="ExternalInput")
    wq = nc.dram_tensor("wq", [D, D], F32, kind="ExternalInput")
    wk = nc.dram_tensor("wk", [D, D], F32, kind="ExternalInput")
    wv = nc.dram_tensor("wv", [D, D], F32, kind="ExternalInput")
    wo = nc.dram_tensor("wo", [D, D], F32, kind="ExternalInput")
    bq = nc.dram_tensor("bq", [1, D], F32, kind="ExternalInput")
    bk = nc.dram_tensor("bk", [1, D], F32, kind="ExternalInput")
    bv = nc.dram_tensor("bv", [1, D], F32, kind="ExternalInput")
    bo = nc.dram_tensor("bo", [1, D], F32, kind="ExternalInput")
    identb = nc.dram_tensor("identb", [128, 128], BF16, kind="ExternalInput")
    identf = nc.dram_tensor("identf", [128, 128], F32, kind="ExternalInput")
    onesb = nc.dram_tensor("onesb", [1, 128], BF16, kind="ExternalInput")
    onesf = nc.dram_tensor("onesf", [1, 128], F32, kind="ExternalInput")
    ones128 = nc.dram_tensor("ones128", [128, 128], BF16, kind="ExternalInput")
    onescol = nc.dram_tensor("onescol", [128, 1], BF16, kind="ExternalInput")
    out = nc.dram_tensor("out", [NT, D], F32, kind="ExternalOutput")
    dbg = {}
    if DEBUG:
        dbg["mmean"] = nc.dram_tensor("d_mmean", [128, TBN], F32, kind="ExternalOutput")
        dbg["rstd"] = nc.dram_tensor("d_rstd", [128, TBN], F32, kind="ExternalOutput")
        dbg["probs"] = nc.dram_tensor("d_probs", [128, TBN, E], F32, kind="ExternalOutput")
        dbg["theta"] = nc.dram_tensor("d_theta", [128, E], F32, kind="ExternalOutput")
        dbg["gates"] = nc.dram_tensor("d_gates", [128, TBN, E], F32, kind="ExternalOutput")
        dbg["xnfm"] = nc.dram_tensor("d_xnfm", [128, DCH, TWIN], F32, kind="ExternalOutput")
        dbg["h1g"] = nc.dram_tensor("d_h1g", [128, FCH, TWIN], F32, kind="ExternalOutput")
        dbg["cnt"] = nc.dram_tensor("d_cnt", [128, E], F32, kind="ExternalOutput")
        dbg["sum1"] = nc.dram_tensor("d_sum1", [128, TBN], F32, kind="ExternalOutput")
        dbg["sum2"] = nc.dram_tensor("d_sum2", [128, TBN], F32, kind="ExternalOutput")
        dbg["var"] = nc.dram_tensor("d_var", [128, TBN], F32, kind="ExternalOutput")

    w1s = nc.dram_tensor("w1s", [E, D, FF], BF16)
    w2s = nc.dram_tensor("w2s", [E, FF, D], BF16)
    wqs = nc.dram_tensor("wqs", [D, D], BF16)
    wks = nc.dram_tensor("wks", [D, D], BF16)
    wvs = nc.dram_tensor("wvs", [D, D], BF16)
    wos = nc.dram_tensor("wos", [D, D], BF16)
    ag_in = nc.dram_tensor("ag_in", [128, 256], F32)
    ag_out = nc.dram_tensor("ag_out", [8 * 128, 256], F32)
    ar_in = nc.dram_tensor("ar_in", [64, 1040], F32)
    kst = nc.dram_tensor("kst", [16, 64], F32)
    ar_out = nc.dram_tensor("ar_out", [64, 1040], F32)

    em = EM(nc)
    es = em._es

    x = es.enter_context(nc.sbuf_tensor("x", [128, TBN, D], F32))
    w1ring = es.enter_context(nc.sbuf_tensor("w1ring", [128, 2, 4096], BF16))
    h1g = es.enter_context(nc.sbuf_tensor("h1g", [128, FCH, TWIN], BF16))
    xn_fm = es.enter_context(nc.sbuf_tensor("xn_fm", [128, DCH, TWIN], BF16))
    w2ring = es.enter_context(nc.sbuf_tensor("w2ring", [128, 3, D], BF16))
    xn_tm = es.enter_context(nc.sbuf_tensor("xn_tm", [128, 2, D], BF16))
    probs_g = es.enter_context(nc.sbuf_tensor("probs_g", [128, 2048], F32))
    s12 = es.enter_context(nc.sbuf_tensor("s12", [128, D], F32))
    s3 = es.enter_context(nc.sbuf_tensor("s3", [128, 512], F32))
    probs_l = es.enter_context(nc.sbuf_tensor("probs_l", [128, TBN, E], F32))
    gates = es.enter_context(nc.sbuf_tensor("gates", [128, TBN, E], F32))
    kv_bf = es.enter_context(nc.sbuf_tensor("kv_bf", [128, 8, DH], BF16))
    sum1 = es.enter_context(nc.sbuf_tensor("sum1", [128, TBN], F32))
    sum2 = es.enter_context(nc.sbuf_tensor("sum2", [128, TBN], F32))
    mmean = es.enter_context(nc.sbuf_tensor("mmean", [128, TBN], F32))
    rstd = es.enter_context(nc.sbuf_tensor("rstd", [128, TBN], F32))
    lo_a = es.enter_context(nc.sbuf_tensor("lo_a", [128, E], F32))
    lo_b = es.enter_context(nc.sbuf_tensor("lo_b", [128, E], F32))
    hi_a = es.enter_context(nc.sbuf_tensor("hi_a", [128, E], F32))
    hi_b = es.enter_context(nc.sbuf_tensor("hi_b", [128, E], F32))
    mid = es.enter_context(nc.sbuf_tensor("mid", [128, E], F32))
    cntf = es.enter_context(nc.sbuf_tensor("cntf", [128, E], F32))
    cntb = es.enter_context(nc.sbuf_tensor("cntb", [128, E], BF16))
    msku = es.enter_context(nc.sbuf_tensor("msku", [128, E], U8))
    rmsc = es.enter_context(nc.sbuf_tensor("rmsc", [128, TBN, E], F32))
    rm1 = es.enter_context(nc.sbuf_tensor("rm1", [128, TBN], F32))
    rm0 = es.enter_context(nc.sbuf_tensor("rm0", [128, TBN], F32))
    rw = es.enter_context(nc.sbuf_tensor("rw", [128, TBN], F32))
    ram = es.enter_context(nc.sbuf_tensor("ram", [128, TBN], F32))
    zdot = es.enter_context(nc.sbuf_tensor("zdot", [128, H], F32))
    zval = es.enter_context(nc.sbuf_tensor("zval", [128, H], F32))
    wg_sb = es.enter_context(nc.sbuf_tensor("wg_sb", [128, DCH, E], BF16))
    bias_sb = es.enter_context(nc.sbuf_tensor("bias_sb", [1, 2, D], BF16))
    id_bf = es.enter_context(nc.sbuf_tensor("id_bf", [128, 128], BF16))
    id_f = es.enter_context(nc.sbuf_tensor("id_f", [128, 128], F32))
    ones1b = es.enter_context(nc.sbuf_tensor("ones1b", [1, 128], BF16))
    ones1f = es.enter_context(nc.sbuf_tensor("ones1f", [1, 128], F32))
    ones_sq = es.enter_context(nc.sbuf_tensor("ones_sq", [128, 128], BF16))
    ones_c = es.enter_context(nc.sbuf_tensor("ones_c", [128, 1], BF16))

    ksum_bc = probs_g[:, 0:1024]
    kv_acc = probs_g[0:64, 1008:2048].rearrange("p (h c) -> p h c", h=16)
    k_tm = h1g[:, 0:8, :].rearrange("p a b -> p (a b)").rearrange("p (tb d) -> p tb d", tb=WTB)
    v_tm = h1g[:, 8:16, :].rearrange("p a b -> p (a b)").rearrange("p (tb d) -> p tb d", tb=WTB)
    attn_fm = h1g[:, 0:8, :]
    q_tm = h1g[:, 8:16, :].rearrange("p a b -> p (a b)").rearrange("p (tb d) -> p tb d", tb=WTB)
    qz_fm = [xn_fm[:, 4, :], xn_fm[:, 5, :]]
    stat_dmy = h1g[:, 0:4, :].rearrange("p a b -> p (a b)").bitcast(F32)
    stage = s12[:, :].bitcast(BF16)

    pb = [es.enter_context(nc.psum_tensor(f"pb{i}", [128, 512], F32)) for i in range(8)]
    bank_fill = [None] * 8
    bank_free = [None] * 8

    def pe_bank_acquire(b):
        if bank_free[b] is not None:
            em.waitd("t", bank_free[b])

    def evac_bank_wait(eng, b):
        assert bank_fill[b] is not None
        em.waitd(eng, bank_fill[b])

    em.op("y", lambda h: h.dma_start(out=x[:, :, :],
          in_=xin.ap().rearrange("(tb tp) d -> tp tb d", tp=128)), dmainc="XL")

    for dst, src in ((id_bf, identb), (id_f, identf), (ones1b, onesb),
                     (ones1f, onesf), (ones_sq, ones128), (ones_c, onescol)):
        em.op("g", lambda h, dst=dst, src=src: h.dma_start(out=dst[:, :], in_=src[:, :]), dmainc="MI")
    em.op("g", lambda h: h.dma_start(out=wg_sb[:, :, :],
          in_=wg.ap().rearrange("(dc dp) e -> dp dc e", dp=128)), dmainc="MI")
    mi_consts = em.cnt["MI"]

    pc_w1 = [None] * E
    pc_w2 = [None] * E
    attn_cast = {}

    def precast(dst, src, nrow):
        cols = src.shape[-1]
        rows_per = 2048 // cols * 128
        last = None
        r = 0
        while r < nrow:
            rr = min(rows_per, nrow - r)
            a = rr // 128
            em.wait("g", "PC", em.cnt.get("PC", 0))
            em.op("g", lambda h, src=src, r=r, rr=rr, a=a, cols=cols: h.dma_start(
                out=stage[:, 0:a * cols].rearrange("p (a c) -> p a c", a=a),
                in_=src[r:r + rr, :].rearrange("(a dp) c -> dp a c", dp=128)), dmainc="PC")
            em.wait("g", "PC", em.cnt["PC"])
            last = em.op("g", lambda h, dst=dst, r=r, rr=rr, a=a, cols=cols: h.dma_start(
                out=dst[r:r + rr, :].rearrange("(a dp) c -> dp a c", dp=128),
                in_=stage[:, 0:a * cols].rearrange("p (a c) -> p a c", a=a)), dmainc="PC")
            r += rr
        return last

    pc_w1[0] = precast(w1s[0], w1[0], D)
    pc_w2[0] = precast(w2s[0], w2[0], FF)

    def precast_rest():
        for e in range(1, E):
            pc_w1[e] = precast(w1s[e], w1[e], D)
            pc_w2[e] = precast(w2s[e], w2[e], FF)
        for name, src, dst in (("wk", wk, wks), ("wv", wv, wvs), ("wq", wq, wqs), ("wo", wo, wos)):
            attn_cast[name] = precast(dst, src, D)

    def load_bias_pair(na, nb, tdep):
        src = {"bq": bq, "bk": bk, "bv": bv, "bo": bo}
        em.waitd("g", tdep)
        em.wait("g", "MI", em.cnt["MI"])
        em.op("g", lambda h, s=src[na]: h.dma_start(out=bias_sb[:, 0, :], in_=s[:, :]), dmainc="MI")
        em.wait("g", "MI", em.cnt["MI"])
        return em.op("g", lambda h, s=src[nb]: h.dma_start(out=bias_sb[:, 1, :], in_=s[:, :]), dmainc="MI")

    def ln_stats(xdep):
        em.waitd("s", xdep)
        sc = None
        for tb in range(TBN):
            em.op("s", lambda h, tb=tb: h.activation(
                stat_dmy[:, :], x[:, tb, :], AF.Square, accum_out=sum2[:, tb:tb + 1]), inc="S")
            sc = em.op("s", lambda h, tb=tb: h.activation(
                stat_dmy[:, :], x[:, tb, :], AF.Copy, accum_out=sum1[:, tb:tb + 1]), inc="S")
        em.wait("v", "S", sc)
        em.op("v", lambda h: h.tensor_scalar(mmean[:, :], sum1[:, :], 1.0 / D, None, ALU.mult), inc="V")
        em.chain("v", lambda h: h.tensor_mul(ram[:, :], mmean[:, :], mmean[:, :]))
        em.chain("v", lambda h: h.scalar_tensor_tensor(
            rw[:, :], sum2[:, :], 1.0 / D, ram[:, :], ALU.mult, ALU.subtract))
        vc = em.chain("v", lambda h: h.tensor_scalar(rm0[:, :], rw[:, :], 1e-5, None, ALU.add))
        if DEBUG and DBG_CALL[0] == 0:
            em.wait("y", "V", vc)
            em.op("y", lambda h: h.dma_start(out=dbg["sum1"][:, :], in_=sum1[:, :]), dmainc="DBG")
            em.op("y", lambda h: h.dma_start(out=dbg["sum2"][:, :], in_=sum2[:, :]), dmainc="DBG")
            em.op("y", lambda h: h.dma_start(out=dbg["var"][:, :], in_=rm0[:, :]), dmainc="DBG")
        em.wait("s", "V", vc)
        sc = em.op("s", lambda h: h.activation(rm1[:, :], rm0[:, :], AF.Sqrt), inc="S")
        em.wait("v", "S", sc)
        c = em.op("v", lambda h: h.reciprocal(rstd[:, :], rm1[:, :]), inc="V")
        return ("V", c)

    tp_state = {}

    def xn_window(w, stats_dep):
        for tb in range(WTB):
            gtb = w * WTB + tb
            slot = gtb % 2
            em.waitd("v", stats_dep)
            prev = tp_state.get(slot)
            if prev is not None:
                em.waitd("v", prev)
            em.op("v", lambda h, gtb=gtb, slot=slot: h.tensor_scalar(
                xn_tm[:, slot, :], x[:, gtb, :], mmean[:, gtb:gtb + 1],
                rstd[:, gtb:gtb + 1], ALU.subtract, ALU.mult), inc="V")
            nv = em.cnt["V"]
            em.wait("t", "V", nv)
            for bi in range(2):
                b = bi
                pe_bank_acquire(b)
                tc = None
                for k in range(4):
                    dc = bi * 4 + k
                    tc = em.op("t", lambda h, slot=slot, dc=dc, b=b, k=k: h.transpose(
                        pb[b][:, :].bitcast(BF16)[:, 0:512].rearrange("p (a c) -> p a c", a=4)[:, k, :],
                        xn_tm[:, slot, dc * 128:(dc + 1) * 128], id_bf[:, :]),
                        inc="T" if k == 3 else None)
                bank_fill[b] = ("T", tc)
                evac_bank_wait("v", b)
                ev = em.op("v", lambda h, b=b, bi=bi, tb=tb: h.tensor_copy(
                    xn_fm[:, bi * 4:(bi + 1) * 4, tb * 128:(tb + 1) * 128],
                    pb[b][:, :].bitcast(BF16)[:, 0:512].rearrange("p (a c) -> p a c", a=4)), inc="V")
                bank_free[b] = ("V", ev)
            tp_state[slot] = ("T", tc)
        return ("V", em.cnt["V"])

    w1r_free = [None, None]
    w1r_sem = ["W1R0", "W1R1"]
    aq = [0]

    def w1q_load(slot, dst_view, src_ap, pc_dep):
        em.waitd("y", w1r_free[slot])
        em.waitd("y", pc_dep)
        c = em.op("y", lambda h, dst_view=dst_view, src_ap=src_ap: h.dma_start(
            out=dst_view, in_=src_ap), dmainc=w1r_sem[slot])
        return (w1r_sem[slot], c)

    w2r_free = [None, None, None]
    w2r_sem = ["W2R0", "W2R1", "W2R2"]

    def ffn_call(xdep, post_ag=None):
        stats_dep = ln_stats(xdep)

        last_probs = None
        for w in range(NW):
            xn_dep = xn_window(w, stats_dep)
            b = 2
            pe_bank_acquire(b)
            em.wait("t", "MI", mi_consts)
            em.waitd("t", xn_dep)
            lg = pb[b][:, 0:32].rearrange("p (s e) -> p s e", s=NSUB)
            tc = None
            for sub in range(NSUB):
                for dc in range(DCH):
                    tc = em.op("t", lambda h, sub=sub, dc=dc, lg=lg: h.matmul(
                        lg[:, sub, :], xn_fm[:, dc, sub * 128:(sub + 1) * 128],
                        wg_sb[:, dc, :], start=(sub == 0 and dc == 0),
                        stop=(sub == NSUB - 1 and dc == DCH - 1)),
                        inc="T" if (dc == DCH - 1) else None)
            bank_fill[b] = ("T", tc)
            evac_bank_wait("v", b)
            evac_bank_wait("s", b)
            for sub in range(NSUB):
                gtb = w * WTB + sub
                em.op("v", lambda h, sub=sub, lg=lg: h.tensor_reduce(
                    s3[:, 0:1], lg[:, sub, :], AX.X, ALU.max, negate=True), inc="V")
                vc = em.cnt["V"]
                em.wait("s", "V", vc)
                sc = em.op("s", lambda h, sub=sub, lg=lg: h.activation(
                    s3[:, 8:16], lg[:, sub, :], AF.Exp, bias=s3[:, 0:1]), inc="S")
                em.wait("v", "S", sc)
                em.op("v", lambda h: h.tensor_reduce(
                    s3[:, 1:2], s3[:, 8:16], AX.X, ALU.add), inc="V")
                em.chain("v", lambda h: h.reciprocal(s3[:, 2:3], s3[:, 1:2]))
                last_probs = em.chain("v", lambda h, gtb=gtb: h.tensor_scalar(
                    probs_l[:, gtb, :], s3[:, 8:16], s3[:, 2:3], None, ALU.mult))
            bank_free[b] = ("V", last_probs)

        em.wait("g", "V", last_probs)
        em.wait("g", "MI", em.cnt["MI"])
        dmp = em.op("g", lambda h: h.dma_start(
            out=ag_in[:, :], in_=probs_l[:, :, :].rearrange("p a e -> p (a e)")), dmainc="MI")
        em.wait("g", "MI", dmp)
        em.op("g", lambda h: h.collective_compute(
            "AllGather", ALU.bypass, replica_groups=[list(range(8))],
            ins=[ag_in.ap().opt()], outs=[ag_out.ap().opt()]), inc="CC")
        em.wait("g", "CC", em.cnt["CC"])
        pgl = em.op("g", lambda h: h.dma_start(
            out=probs_g[:, :].rearrange("p (r c) -> p r c", r=8),
            in_=ag_out.ap().rearrange("(r p) c -> p r c", p=128)), dmainc="MI")
        if post_ag is not None:
            post_ag()

        em.wait("v", "MI", pgl)
        em.op("v", lambda h: h.memset(lo_a[:, :], 0.0), inc="V")
        em.op("v", lambda h: h.memset(hi_a[:, :], 1.0), inc="V")
        pg4 = probs_g[:, :].rearrange("p (r tb e) -> p r tb e", r=8, e=8)
        s3v = s3[:, :].rearrange("p (r tb) -> p r tb", r=8)[:, :, 0:32]
        lo, hi, lon, hin = lo_a, hi_a, lo_b, hi_b
        for it in range(NBIS):
            em.chain("v", lambda h, lo=lo, hi=hi: h.tensor_add(mid[:, :], lo[:, :], hi[:, :]))
            em.chain("v", lambda h: h.tensor_scalar(mid[:, :], mid[:, :], 0.5, None, ALU.mult))
            first = True
            for e in range(E):
                if first:
                    em.chain("v", lambda h, e=e: h.tensor_scalar(
                        s3v, pg4[:, :, :, e], mid[:, e:e + 1], 0.0, ALU.is_gt, ALU.add,
                        accum_out=cntf[:, e:e + 1]))
                    first = False
                else:
                    em.op("v", lambda h, e=e: h.tensor_scalar(
                        s3v, pg4[:, :, :, e], mid[:, e:e + 1], 0.0, ALU.is_gt, ALU.add,
                        accum_out=cntf[:, e:e + 1]), inc="V")
            vc = em.chain("v", lambda h: h.tensor_copy(cntb[:, :], cntf[:, :]))
            b = 3
            pe_bank_acquire(b)
            em.wait("t", "V", vc)
            tc = em.op("t", lambda h, b=b: h.matmul(
                pb[b][:, 0:E], ones_sq[:, :], cntb[:, :], start=True, stop=True), inc="T")
            bank_fill[b] = ("T", tc)
            evac_bank_wait("v", b)
            em.op("v", lambda h, b=b: h.tensor_scalar(
                msku[:, :], pb[b][:, 0:E], float(CAP) - 0.5, None, ALU.is_ge), inc="V")
            em.chain("v", lambda h, lo=lo, lon=lon: h.select(
                lon[:, :], msku[:, :], mid[:, :], lo[:, :]))
            ev = em.chain("v", lambda h, hi=hi, hin=hin: h.select(
                hin[:, :], msku[:, :], hi[:, :], mid[:, :]))
            bank_free[b] = ("V", ev)
            lo, lon = lon, lo
            hi, hin = hin, hi
        theta = lo

        th_b = theta[:, :].rearrange("p (o e) -> p o e", o=1).broadcast_to([128, TBN, E])
        em.chain("v", lambda h, th_b=th_b: h.tensor_tensor(
            rmsc[:, :, :], probs_l[:, :, :], th_b, ALU.is_gt))
        em.chain("v", lambda h: h.tensor_mul(gates[:, :, :], probs_l[:, :, :], rmsc[:, :, :]))
        em.chain("v", lambda h: h.scalar_tensor_tensor(
            rmsc[:, :, :], rmsc[:, :, :], 1.0, gates[:, :, :], ALU.subtract, ALU.add))
        em.chain("v", lambda h: h.tensor_reduce(rm1[:, :], rmsc[:, :, :], AX.X, ALU.max))
        em.chain("v", lambda h: h.tensor_reduce(rm0[:, :], probs_l[:, :, :], AX.X, ALU.max))
        em.chain("v", lambda h: h.tensor_scalar(ram[:, :], rm1[:, :], -0.5, None, ALU.is_gt))
        em.chain("v", lambda h: h.tensor_mul(rw[:, :], ram[:, :], rm1[:, :]))
        em.chain("v", lambda h: h.tensor_scalar(
            ram[:, :], ram[:, :], -1.0, 1.0, ALU.mult, ALU.add))
        em.chain("v", lambda h: h.tensor_mul(ram[:, :], ram[:, :], rm0[:, :]))
        em.chain("v", lambda h: h.tensor_add(rw[:, :], rw[:, :], ram[:, :]))
        w_b = rw[:, :].rearrange("p (a o) -> p a o", o=1).broadcast_to([128, TBN, E])
        em.chain("v", lambda h, w_b=w_b: h.tensor_tensor(
            rmsc[:, :, :], probs_l[:, :, :], w_b, ALU.is_equal))
        em.chain("v", lambda h: h.tensor_scalar(rw[:, :], rw[:, :], 0.5, None, ALU.mult))
        gts = None
        first = True
        for tb in range(TBN):
            if first:
                gts = em.chain("v", lambda h, tb=tb: h.tensor_scalar(
                    gates[:, tb, :], rmsc[:, tb, :], rw[:, tb:tb + 1], None, ALU.mult))
                first = False
            else:
                gts = em.op("v", lambda h, tb=tb: h.tensor_scalar(
                    gates[:, tb, :], rmsc[:, tb, :], rw[:, tb:tb + 1], None, ALU.mult), inc="V")
        gates_dep = ("V", gts)
        if DEBUG and DBG_CALL[0] == 0:
            em.wait("y", "V", gts)
            em.op("y", lambda h: h.dma_start(out=dbg["mmean"][:, :], in_=mmean[:, :]), dmainc="DBG")
            em.op("y", lambda h: h.dma_start(out=dbg["rstd"][:, :], in_=rstd[:, :]), dmainc="DBG")
            em.op("y", lambda h: h.dma_start(out=dbg["probs"][:, :, :], in_=probs_l[:, :, :]), dmainc="DBG")
            em.op("y", lambda h: h.dma_start(out=dbg["theta"][:, :], in_=theta[:, :]), dmainc="DBG")
            em.op("y", lambda h: h.dma_start(out=dbg["gates"][:, :, :], in_=gates[:, :, :]), dmainc="DBG")
            em.op("y", lambda h: h.dma_start(out=dbg["cnt"][:, :], in_=cntf[:, :]), dmainc="DBG")

        h1_ready = [None] * FCH
        h1_freed = [None] * FCH
        for w in range(NW):
            xn_dep = xn_window(w, stats_dep)
            for e in range(E):
                for qt in range(4):
                    slot = aq[0] % 2
                    aq[0] += 1
                    src = w1s[e, :, qt * 512:(qt + 1) * 512].rearrange(
                        "(dc dp) f -> dp dc f", dp=128)
                    dstv = w1ring[:, slot, :].rearrange("p (dc f) -> p dc f", dc=8)
                    ldep = w1q_load(slot, dstv, src, ("PC", pc_w1[e]) if w == 0 else None)
                    em.waitd("t", ldep)
                    em.waitd("t", xn_dep)
                    w1v = dstv
                    tc = None
                    for fi in range(4):
                        f = qt * 4 + fi
                        b = f % 2
                        pe_bank_acquire(b)
                        if h1_freed[f] is not None:
                            em.waitd("s", h1_freed[f])
                        for dc in range(DCH):
                            tc = em.op("t", lambda h, w1v=w1v, fi=fi, dc=dc, b=b: h.matmul(
                                pb[b][:, :], w1v[:, dc, fi * 128:(fi + 1) * 128],
                                xn_fm[:, dc, :], start=(dc == 0), stop=(dc == DCH - 1)),
                                inc="T" if dc == DCH - 1 else None)
                        bank_fill[b] = ("T", tc)
                        evac_bank_wait("s", b)
                        sc = em.op("s", lambda h, b=b, f=f: h.activation(
                            h1g[:, f, :], pb[b][:, :], AF.Gelu), inc="S")
                        bank_free[b] = ("S", sc)
                        h1_ready[f] = ("S", sc)
                    w1r_free[slot] = ("T", tc)
                if DEBUG and DBG_CALL[0] == 0 and w == 0 and e == 0:
                    em.wait("g", "S", em.cnt["S"])
                    em.op("g", lambda h: h.dma_start(out=dbg["xnfm"][:, :, :], in_=xn_fm[:, :, :]), dmainc="DBG")
                    em.op("g", lambda h: h.dma_start(out=dbg["h1g"][:, :, :], in_=h1g[:, :, :]), dmainc="DBG")
                tc = None
                for f in range(FCH):
                    slot = f % 3
                    em.waitd("y", w2r_free[slot])
                    if w == 0:
                        em.wait("y", "PC", pc_w2[e])
                    lc = em.op("y", lambda h, slot=slot, e=e, f=f: h.dma_start(
                        out=w2ring[:, slot, :], in_=w2s[e, f * 128:(f + 1) * 128, :]),
                        dmainc=w2r_sem[slot])
                    em.wait("t", w2r_sem[slot], lc)
                    em.waitd("t", h1_ready[f])
                    for sub in range(NSUB):
                        for jh in range(2):
                            b = sub * 2 + jh
                            if f == 0:
                                pe_bank_acquire(b)
                            tc = em.op("t", lambda h, f=f, sub=sub, jh=jh, b=b, slot=slot: h.matmul(
                                pb[b][:, :], h1g[:, f, sub * 128:(sub + 1) * 128],
                                w2ring[:, slot, jh * 512:(jh + 1) * 512],
                                start=(f == 0), stop=(f == FCH - 1)),
                                inc="T" if (f == FCH - 1 or (sub == NSUB - 1 and jh == 1)) else None)
                    h1_freed[f] = ("T", tc)
                    w2r_free[slot] = ("T", tc)
                for sub in range(NSUB):
                    for jh in range(2):
                        bank_fill[sub * 2 + jh] = ("T", tc)
                em.waitd("v", gates_dep)
                for sub in range(NSUB):
                    gtb = w * WTB + sub
                    for jh in range(2):
                        b = sub * 2 + jh
                        evac_bank_wait("v", b)
                        ev = em.op("v", lambda h, b=b, gtb=gtb, jh=jh, e=e: h.scalar_tensor_tensor(
                            x[:, gtb, jh * 512:(jh + 1) * 512], pb[b][:, :],
                            gates[:, gtb, e:e + 1], x[:, gtb, jh * 512:(jh + 1) * 512],
                            ALU.mult, ALU.add), inc="V")
                        bank_free[b] = ("V", ev)
        DBG_CALL[0] += 1
        return ("V", em.cnt["V"])

    def attn_call(xdep):
        stats_dep = ln_stats(xdep)
        bias_c = load_bias_pair("bk", "bv", None)

        em.waitd("v", xdep)
        kvz = em.op("v", lambda h: h.memset(
            kv_acc[:, :, :].rearrange("p a c -> p (a c)"), 0.0), inc="V")

        for w in range(NW):
            xn_dep = xn_window(w, stats_dep)
            for proj in ("k", "v"):
                wsrc = wks if proj == "k" else wvs
                btile = 0 if proj == "k" else 1
                dst = k_tm if proj == "k" else v_tm
                cast_dep = attn_cast["wk" if proj == "k" else "wv"]
                for qt in range(4):
                    slot = aq[0] % 2
                    aq[0] += 1
                    src = wsrc[qt * 256:(qt + 1) * 256, :].rearrange(
                        "(dc dp) j -> dp dc j", dp=128)
                    dstv = w1ring[:, slot, 0:2048].rearrange("p (dc j) -> p dc j", dc=2)
                    ldep = w1q_load(slot, dstv, src, ("PC", cast_dep) if w == 0 else None)
                    em.waitd("t", ldep)
                    em.waitd("t", xn_dep)
                    tc = None
                    for dcr in range(2):
                        dc = qt * 2 + dcr
                        for sub in range(NSUB):
                            for jh in range(2):
                                b = sub * 2 + jh
                                if dc == 0:
                                    pe_bank_acquire(b)
                                tc = em.op("t", lambda h, dstv=dstv, dcr=dcr, dc=dc, sub=sub, jh=jh, b=b: h.matmul(
                                    pb[b][:, :], xn_fm[:, dc, sub * 128:(sub + 1) * 128],
                                    dstv[:, dcr, jh * 512:(jh + 1) * 512],
                                    start=(dc == 0), stop=False),
                                    inc="T" if (dcr == 1 and sub == NSUB - 1 and jh == 1) else None)
                    w1r_free[slot] = ("T", tc)
                em.wait("t", "MI", bias_c)
                for sub in range(NSUB):
                    for jh in range(2):
                        b = sub * 2 + jh
                        tc = em.op("t", lambda h, btile=btile, jh=jh, b=b: h.matmul(
                            pb[b][:, :], ones1b[:, :], bias_sb[:, btile, jh * 512:(jh + 1) * 512],
                            start=False, stop=True), inc="T")
                        bank_fill[b] = ("T", tc)
                em.wait("v", "PC", em.cnt["PC"] if w == -1 else pc_done[0])
                for sub in range(NSUB):
                    for jh in range(2):
                        b = sub * 2 + jh
                        evac_bank_wait("v", b)
                        if proj == "k":
                            em.chain("v", lambda h, b=b: h.tensor_scalar(
                                s12[:, 0:512], pb[b][:, :], 0.0, None, ALU.min))
                            vc = em.op("v", lambda h, b=b: h.tensor_scalar(
                                s12[:, 512:1024], pb[b][:, :], 0.0, None, ALU.max), inc="V")
                            em.wait("s", "V", vc)
                            sc = em.op("s", lambda h: h.activation(
                                s12[:, 0:512], s12[:, 0:512], AF.Exp), inc="S")
                            em.wait("v", "S", sc)
                            ev = em.op("v", lambda h, dst=dst, sub=sub, jh=jh: h.tensor_add(
                                dst[:, sub, jh * 512:(jh + 1) * 512],
                                s12[:, 0:512], s12[:, 512:1024]), inc="V")
                        else:
                            ev = em.op("v", lambda h, dst=dst, sub=sub, jh=jh, b=b: h.tensor_copy(
                                dst[:, sub, jh * 512:(jh + 1) * 512], pb[b][:, :]), inc="V")
                        bank_free[b] = ("V", ev)
            kv_dep = ("V", em.cnt["V"])
            for b in (0, 1, 2):
                pe_bank_acquire(b)
            em.waitd("t", kv_dep)
            em.wait("t", "V", kvz)
            em.wait("t", "MI", mi_consts)
            tc = None
            first = True
            for tb in range(WTB):
                for hh in range(H):
                    b = hh // 8
                    reg = pb[b][:, :].rearrange("p (a c) -> p a c", a=8)[0:64, hh % 8, :]
                    em.op("t", lambda h, tb=tb, hh=hh, reg=reg, first=first: h.matmul(
                        reg, k_tm[:, tb, hh * 64:(hh + 1) * 64],
                        v_tm[:, tb, hh * 64:(hh + 1) * 64],
                        start=first and (hh % 8 == 0) and tb == 0,
                        stop=(tb == WTB - 1)), inc=None)
                    ksreg = pb[2][0:64, hh:hh + 1]
                    tc = em.op("t", lambda h, tb=tb, hh=hh, ksreg=ksreg: h.matmul(
                        ksreg, k_tm[:, tb, hh * 64:(hh + 1) * 64], ones_c[:, :],
                        start=(tb == 0 and hh == 0), stop=(tb == WTB - 1)),
                        inc="T" if (tb == WTB - 1 and hh == H - 1) else None)
            for b in (0, 1, 2):
                bank_fill[b] = ("T", tc)
            for b in (0, 1):
                evac_bank_wait("v", b)
                ev = em.op("v", lambda h, b=b: h.tensor_add(
                    kv_acc[:, b * 8:(b + 1) * 8, 0:64],
                    pb[b][:, :].rearrange("p (a c) -> p a c", a=8)[0:64, :, :],
                    kv_acc[:, b * 8:(b + 1) * 8, 0:64]), inc="V")
                bank_free[b] = ("V", ev)
            evac_bank_wait("v", 2)
            ev = em.op("v", lambda h: h.tensor_add(
                kv_acc[:, :, 64:65].rearrange("p a c -> p (a c)"),
                pb[2][0:64, 0:16],
                kv_acc[:, :, 64:65].rearrange("p a c -> p (a c)")), inc="V")
            bank_free[2] = ("V", ev)
        kv_all = em.cnt["V"]

        em.wait("g", "V", kv_all)
        em.wait("g", "MI", em.cnt["MI"])
        d1 = em.op("g", lambda h: h.dma_start(
            out=ar_in[:, :], in_=kv_acc[:, :, :].rearrange("p a c -> p (a c)")), dmainc="MI")
        em.wait("g", "MI", d1)
        em.op("g", lambda h: h.collective_compute(
            "AllReduce", ALU.add, replica_groups=[[0, 1], [2, 3], [4, 5], [6, 7]],
            ins=[ar_in.ap().opt()], outs=[ar_out.ap().opt()]), inc="CC")
        em.wait("g", "CC", em.cnt["CC"])
        d2 = em.op("g", lambda h: h.dma_start(
            out=kv_acc[:, :, :].rearrange("p a c -> p (a c)"), in_=ar_out[:, :]), dmainc="MI")
        kvbf_d = None
        for two in range(2):
            em.wait("g", "MI", em.cnt["MI"])
            kvbf_d = em.op("g", lambda h, two=two: h.dma_start(
                out=kv_bf[two * 64:(two + 1) * 64, :, :],
                in_=ar_out.ap().rearrange("d (hh two c) -> two d hh c", two=2, c=65)[two, :, :, 0:64]), dmainc="MI")
        b = 4
        pe_bank_acquire(b)
        em.wait("t", "MI", d2)
        tc = em.op("t", lambda h, b=b: h.transpose(
            pb[b][0:16, 0:64], kv_acc[:, :, 64:65].rearrange("p a c -> p (a c)"),
            id_f[0:64, 0:64]), inc="T")
        bank_fill[b] = ("T", tc)
        evac_bank_wait("v", b)
        ks_ev = em.op("v", lambda h, b=b: h.tensor_copy(
            s12[0:16, 64:128], pb[b][0:16, 0:64]), inc="V")
        bank_free[b] = ("V", ks_ev)
        em.wait("g", "V", ks_ev)
        em.wait("g", "MI", em.cnt["MI"])
        em.op("g", lambda h: h.dma_start(
            out=kst[:, :], in_=s12[0:16, 64:128]), dmainc="MI")
        em.wait("g", "MI", em.cnt["MI"])
        d3 = em.op("g", lambda h: h.dma_start(
            out=s12[0:1, 0:1024],
            in_=kst.ap().rearrange("h d -> (h d)").rearrange("(o c) -> o c", o=1)), dmainc="MI")
        ks_fill = []
        for half in range(2):
            b = 4 + half
            pe_bank_acquire(b)
            em.wait("t", "MI", d3)
            tc = em.op("t", lambda h, b=b, half=half: h.matmul(
                pb[b][:, :], ones1f[:, :], s12[0:1, half * 512:(half + 1) * 512],
                start=True, stop=True), inc="T")
            bank_fill[b] = ("T", tc)
            evac_bank_wait("v", b)
            ev = em.op("v", lambda h, b=b, half=half: h.tensor_copy(
                ksum_bc[:, half * 512:(half + 1) * 512], pb[b][:, :]), inc="V")
            bank_free[b] = ("V", ev)
        ksbc_c = em.cnt["V"]

        bias2_c = load_bias_pair("bq", "bo", ("T", em.cnt["T"]))

        def emit_wo(wv, af):
            for qt in range(4):
                slot = aq[0] % 2
                aq[0] += 1
                src_ = wos[qt * 256:(qt + 1) * 256, :].rearrange("(dc dp) j -> dp dc j", dp=128)
                dstv = w1ring[:, slot, 0:2048].rearrange("p (dc j) -> p dc j", dc=2)
                ldep = w1q_load(slot, dstv, src_, ("PC", attn_cast["wo"]) if wv == 0 else None)
                em.waitd("t", ldep)
                em.waitd("t", af)
                tc = None
                for dcr in range(2):
                    dc = qt * 2 + dcr
                    for sub in range(NSUB):
                        for jh in range(2):
                            b = sub * 2 + jh
                            if dc == 0:
                                pe_bank_acquire(b)
                            tc = em.op("t", lambda h, dstv=dstv, dcr=dcr, dc=dc, sub=sub, jh=jh, b=b: h.matmul(
                                pb[b][:, :], attn_fm[:, dc, sub * 128:(sub + 1) * 128],
                                dstv[:, dcr, jh * 512:(jh + 1) * 512],
                                start=(dc == 0), stop=False),
                                inc="T" if (dcr == 1 and sub == NSUB - 1 and jh == 1) else None)
                w1r_free[slot] = ("T", tc)
            for sub in range(NSUB):
                for jh in range(2):
                    b = sub * 2 + jh
                    tc = em.op("t", lambda h, jh=jh, b=b: h.matmul(
                        pb[b][:, :], ones1b[:, :], bias_sb[:, 1, jh * 512:(jh + 1) * 512],
                        start=False, stop=True), inc="T")
                    bank_fill[b] = ("T", tc)
            for sub in range(NSUB):
                gtb = wv * WTB + sub
                for jh in range(2):
                    b = sub * 2 + jh
                    evac_bank_wait("v", b)
                    ev = em.op("v", lambda h, b=b, gtb=gtb, jh=jh: h.tensor_add(
                        x[:, gtb, jh * 512:(jh + 1) * 512], pb[b][:, :],
                        x[:, gtb, jh * 512:(jh + 1) * 512]), inc="V")
                    bank_free[b] = ("V", ev)

        pend_wo = [None]

        for w in range(NW):
            xn_dep = xn_window(w, stats_dep)
            for qt in range(4):
                slot = aq[0] % 2
                aq[0] += 1
                src = wqs[qt * 256:(qt + 1) * 256, :].rearrange("(dc dp) j -> dp dc j", dp=128)
                dstv = w1ring[:, slot, 0:2048].rearrange("p (dc j) -> p dc j", dc=2)
                ldep = w1q_load(slot, dstv, src, ("PC", attn_cast["wq"]) if w == 0 else None)
                em.waitd("t", ldep)
                em.waitd("t", xn_dep)
                tc = None
                for dcr in range(2):
                    dc = qt * 2 + dcr
                    for sub in range(NSUB):
                        for jh in range(2):
                            b = sub * 2 + jh
                            if dc == 0:
                                pe_bank_acquire(b)
                            tc = em.op("t", lambda h, dstv=dstv, dcr=dcr, dc=dc, sub=sub, jh=jh, b=b: h.matmul(
                                pb[b][:, :], xn_fm[:, dc, sub * 128:(sub + 1) * 128],
                                dstv[:, dcr, jh * 512:(jh + 1) * 512],
                                start=(dc == 0), stop=False),
                                inc="T" if (dcr == 1 and sub == NSUB - 1 and jh == 1) else None)
                w1r_free[slot] = ("T", tc)
            em.wait("t", "MI", bias2_c)
            for sub in range(NSUB):
                for jh in range(2):
                    b = sub * 2 + jh
                    tc = em.op("t", lambda h, jh=jh, b=b: h.matmul(
                        pb[b][:, :], ones1b[:, :], bias_sb[:, 0, jh * 512:(jh + 1) * 512],
                        start=False, stop=True), inc="T")
                    bank_fill[b] = ("T", tc)
            for sub in range(NSUB):
                for jh in range(2):
                    b = sub * 2 + jh
                    evac_bank_wait("v", b)
                    em.chain("v", lambda h, b=b: h.tensor_scalar(
                        s12[:, 0:512], pb[b][:, :], 0.0, None, ALU.min))
                    vc = em.op("v", lambda h, b=b: h.tensor_scalar(
                        s12[:, 512:1024], pb[b][:, :], 0.0, None, ALU.max), inc="V")
                    em.wait("s", "V", vc)
                    sc = em.op("s", lambda h: h.activation(
                        s12[:, 0:512], s12[:, 0:512], AF.Exp), inc="S")
                    em.wait("v", "S", sc)
                    ev = em.op("v", lambda h, sub=sub, jh=jh: h.tensor_add(
                        q_tm[:, sub, jh * 512:(jh + 1) * 512],
                        s12[:, 0:512], s12[:, 512:1024]), inc="V")
                    bank_free[b] = ("V", ev)
            if pend_wo[0] is not None and pend_wo[0][0] == w - 1:
                emit_wo(*pend_wo[0])
                pend_wo[0] = None
            em.wait("v", "V", ksbc_c)
            qz_c = None
            for tb in range(WTB):
                em.chain("v", lambda h, tb=tb: h.tensor_tensor(
                    s12[:, :].rearrange("p (a c) -> p a c", a=16),
                    q_tm[:, tb, :].rearrange("p (a c) -> p a c", a=16),
                    ksum_bc[:, :].rearrange("p (a c) -> p a c", a=16), ALU.mult))
                em.chain("v", lambda h: h.tensor_reduce(
                    zdot[:, :], s12[:, :].rearrange("p (a c) -> p a c", a=16),
                    AX.X, ALU.add))
                em.chain("v", lambda h: h.tensor_scalar(
                    zdot[:, :], zdot[:, :], 1e-6, None, ALU.add))
                em.chain("v", lambda h: h.reciprocal(zval[:, :], zdot[:, :]))
                zb = zval[:, :].rearrange("p (a o) -> p a o", o=1).broadcast_to([128, H, DH])
                qz_c = em.chain("v", lambda h, tb=tb, zb=zb: h.tensor_tensor(
                    q_tm[:, tb, :].rearrange("p (a c) -> p a c", a=16),
                    q_tm[:, tb, :].rearrange("p (a c) -> p a c", a=16), zb, ALU.mult))
            em.wait("t", "MI", kvbf_d)
            sc_banks = [2, 3, 4, 5, 6, 7]
            sc_i = [0]
            for jch in range(8):
                ring = jch % 2
                em.wait("t", "V", qz_c)
                b = ring
                pe_bank_acquire(b)
                tc = None
                for tb in range(WTB):
                    tc = em.op("t", lambda h, tb=tb, jch=jch, b=b: h.transpose(
                        pb[b][:, :].bitcast(BF16)[:, 0:512].rearrange("p (a c) -> p a c", a=4)[:, tb, :],
                        q_tm[:, tb, jch * 128:(jch + 1) * 128], id_bf[:, :]),
                        inc="T" if tb == WTB - 1 else None)
                bank_fill[b] = ("T", tc)
                evac_bank_wait("v", b)
                ev = em.op("v", lambda h, b=b, ring=ring: h.tensor_copy(
                    qz_fm[ring][:, :].rearrange("p (a c) -> p a c", a=4),
                    pb[b][:, :].bitcast(BF16)[:, 0:512].rearrange("p (a c) -> p a c", a=4)), inc="V")
                bank_free[b] = ("V", ev)
                for hr in range(2):
                    hh = jch * 2 + hr
                    sb = sc_banks[sc_i[0] % 6]
                    sc_i[0] += 1
                    pe_bank_acquire(sb)
                    em.wait("t", "V", ev)
                    tp = (64, 64) if hr == 1 else None
                    tc = em.op("t", lambda h, hh=hh, hr=hr, ring=ring, sb=sb, tp=tp: h.matmul(
                        pb[sb][hr * 64:(hr + 1) * 64, :],
                        kv_bf[hr * 64:(hr + 1) * 64, hh // 2, :],
                        qz_fm[ring][hr * 64:(hr + 1) * 64, :], start=True, stop=True,
                        tile_position=tp), inc="T")
                    bank_fill[sb] = ("T", tc)
                    evac_bank_wait("v", sb)
                    ev2 = em.op("v", lambda h, hh=hh, hr=hr, jch=jch, sb=sb: h.tensor_copy(
                        attn_fm[hr * 64:(hr + 1) * 64, jch, :],
                        pb[sb][hr * 64:(hr + 1) * 64, :]), inc="V")
                    bank_free[sb] = ("V", ev2)
            af_dep = ("V", em.cnt["V"])
            pend_wo[0] = (w, af_dep)
        emit_wo(*pend_wo[0])
        pend_wo[0] = None
        return ("V", em.cnt["V"])

    dep = ("XL", 16)
    pc_done = [0]

    def _hook():
        precast_rest()
        pc_done[0] = em.cnt["PC"]
    if "ffn1" in STAGES:
        dep = ffn_call(dep, post_ag=_hook)
    else:
        _hook()
    if "attn" in STAGES:
        dep = attn_call(dep)
    if "ffn2" in STAGES:
        dep = ffn_call(dep)

    for w in range(NW):
        em.waitd("y", dep)
        em.op("y", lambda h, w=w: h.dma_start(
            out=out[w * TWIN:(w + 1) * TWIN, :].rearrange("(tb tp) d -> tp tb d", tp=128),
            in_=x[:, w * WTB:(w + 1) * WTB, :]), dmainc="ST")
    em.wait("y", "ST", em.cnt["ST"])
    if DEBUG:
        em.wait("y", "DBG", em.cnt.get("DBG", 0))

    with em._es:
        with nc.Block() as block:
            @block.tensor
            def _(h):
                for fn in em.q["t"]:
                    fn(h)

            @block.scalar
            def _(h):
                for fn in em.q["s"]:
                    fn(h)

            @block.vector
            def _(h):
                for fn in em.q["v"]:
                    fn(h)

            @block.gpsimd
            def _(h):
                for fn in em.q["g"]:
                    fn(h)

            @block.sync
            def _(h):
                for fn in em.q["y"]:
                    fn(h)
    return nc


_NC_CACHE = {}


def kernel(L0_attn_x, L0_attn_wq, L0_attn_bq, L0_attn_wk, L0_attn_bk,
           L0_attn_wv, L0_attn_bv, L0_attn_wo, L0_attn_bo,
           L0_ffn_Wg, L0_ffn_W1, L0_ffn_W2):
    xfull = np.ascontiguousarray(np.asarray(L0_attn_x, dtype=np.float32))
    shared = dict(
        w1=np.ascontiguousarray(np.asarray(L0_ffn_W1, dtype=np.float32)),
        w2=np.ascontiguousarray(np.asarray(L0_ffn_W2, dtype=np.float32)),
        wg=np.ascontiguousarray(np.asarray(L0_ffn_Wg, dtype=np.float32)),
        wq=np.ascontiguousarray(np.asarray(L0_attn_wq, dtype=np.float32)),
        wk=np.ascontiguousarray(np.asarray(L0_attn_wk, dtype=np.float32)),
        wv=np.ascontiguousarray(np.asarray(L0_attn_wv, dtype=np.float32)),
        wo=np.ascontiguousarray(np.asarray(L0_attn_wo, dtype=np.float32)),
        bq=np.ascontiguousarray(np.asarray(L0_attn_bq, dtype=np.float32)).reshape(1, D),
        bk=np.ascontiguousarray(np.asarray(L0_attn_bk, dtype=np.float32)).reshape(1, D),
        bv=np.ascontiguousarray(np.asarray(L0_attn_bv, dtype=np.float32)).reshape(1, D),
        bo=np.ascontiguousarray(np.asarray(L0_attn_bo, dtype=np.float32)).reshape(1, D),
        identb=np.eye(128, dtype=ml_dtypes.bfloat16),
        identf=np.eye(128, dtype=np.float32),
        onesb=np.ones((1, 128), dtype=ml_dtypes.bfloat16),
        onesf=np.ones((1, 128), dtype=np.float32),
        ones128=np.ones((128, 128), dtype=ml_dtypes.bfloat16),
        onescol=np.ones((128, 1), dtype=ml_dtypes.bfloat16),
    )
    in_maps = []
    for c in range(NCORES):
        b, half = c // 2, c % 2
        shard = np.ascontiguousarray(xfull[b, half * NT:(half + 1) * NT, :])
        in_maps.append(dict(xin=shard, **shared))

    key = STAGES
    if key not in _NC_CACHE:
        _NC_CACHE[key] = build_nc()
    nc = _NC_CACHE[key]
    res = run_bass_kernel_spmd(nc, in_maps, core_ids=list(range(NCORES)))
    outp = np.empty((4, 8192, D), dtype=np.float32)
    for c in range(NCORES):
        b, half = c // 2, c % 2
        outp[b, half * NT:(half + 1) * NT, :] = np.asarray(res.results[c]["out"])
    if DEBUG:
        global DEBUG_RES
        DEBUG_RES = [{k: np.asarray(v) for k, v in r.items() if k.startswith("d_")}
                     for r in res.results]
    return outp


# version 08 (final)
# speedup vs baseline: 1.1228x; 1.0018x over previous
import numpy as np
import ml_dtypes
import contextlib
import concourse.bass as bass
import concourse.mybir as mybir
from concourse.bass_utils import run_bass_kernel_spmd

F32 = mybir.dt.float32
BF16 = mybir.dt.bfloat16
U8 = mybir.dt.uint8
AF = mybir.ActivationFunctionType
ALU = mybir.AluOpType
AX = mybir.AxisListType

NCORES = 8
NT = 4096
TBN = 32
D = 1024
DCH = 8
FF = 2048
FCH = 16
E = 8
H = 16
DH = 64
NW = 8
WTB = 4
TWIN = 512
NSUB = 4
CAP = 4096
NBIS = 26

STAGES = ("ffn1", "attn", "ffn2")
DEBUG = False
DBG_CALL = [0]


class EM:
    ENGS = ("t", "s", "v", "g", "y")

    def __init__(self, nc):
        self.nc = nc
        self.q = {e: [] for e in self.ENGS}
        self.sems = {}
        self.cnt = {}
        self._es = contextlib.ExitStack()
        self.last_wait = {}

    def sem(self, name):
        if name not in self.sems:
            self.sems[name] = self._es.enter_context(self.nc.semaphore(name))
            self.cnt[name] = 0
        return self.sems[name]

    def wait(self, eng, name, val):
        if val is None or val <= 0:
            return
        self.sem(name)
        assert val <= self.cnt[name], f"wait on future count {name} {val} > {self.cnt[name]}"
        key = (eng, name)
        if self.last_wait.get(key, 0) >= val:
            return
        self.last_wait[key] = val
        s = self.sems[name]
        self.q[eng].append(lambda h, s=s, v=val: h.wait_ge(s, v))

    def waitd(self, eng, dep):
        if dep is not None:
            self.wait(eng, dep[0], dep[1])

    ESEM = {"t": "T", "s": "S", "v": "V", "g": "G", "y": "Y"}

    def chain(self, eng, fn):
        nm = self.ESEM[eng]
        self.sem(nm)
        self.wait(eng, nm, self.cnt[nm])
        return self.op(eng, fn, inc=nm)

    def op(self, eng, fn, inc=None, dmainc=None):
        if inc is not None:
            s = self.sem(inc)
            self.cnt[inc] += 1
            c = self.cnt[inc]
            self.q[eng].append(lambda h, fn=fn, s=s: fn(h).then_inc(s, 1))
            return c
        if dmainc is not None:
            s = self.sem(dmainc)
            self.cnt[dmainc] += 16
            c = self.cnt[dmainc]
            self.q[eng].append(lambda h, fn=fn, s=s: fn(h).then_inc(s, 16))
            return c
        self.q[eng].append(fn)
        return None


def build_nc():
    nc = bass.Bass("TRN2", target_bir_lowering=False)

    xin = nc.dram_tensor("xin", [NT, D], F32, kind="ExternalInput")
    w1 = nc.dram_tensor("w1", [E, D, FF], F32, kind="ExternalInput")
    w2 = nc.dram_tensor("w2", [E, FF, D], F32, kind="ExternalInput")
    wg = nc.dram_tensor("wg", [D, E], F32, kind="ExternalInput")
    wq = nc.dram_tensor("wq", [D, D], F32, kind="ExternalInput")
    wk = nc.dram_tensor("wk", [D, D], F32, kind="ExternalInput")
    wv = nc.dram_tensor("wv", [D, D], F32, kind="ExternalInput")
    wo = nc.dram_tensor("wo", [D, D], F32, kind="ExternalInput")
    bq = nc.dram_tensor("bq", [1, D], F32, kind="ExternalInput")
    bk = nc.dram_tensor("bk", [1, D], F32, kind="ExternalInput")
    bv = nc.dram_tensor("bv", [1, D], F32, kind="ExternalInput")
    bo = nc.dram_tensor("bo", [1, D], F32, kind="ExternalInput")
    identb = nc.dram_tensor("identb", [128, 128], BF16, kind="ExternalInput")
    identf = nc.dram_tensor("identf", [128, 128], F32, kind="ExternalInput")
    onesb = nc.dram_tensor("onesb", [1, 128], BF16, kind="ExternalInput")
    onesf = nc.dram_tensor("onesf", [1, 128], F32, kind="ExternalInput")
    ones128 = nc.dram_tensor("ones128", [128, 128], BF16, kind="ExternalInput")
    onescol = nc.dram_tensor("onescol", [128, 1], BF16, kind="ExternalInput")
    out = nc.dram_tensor("out", [NT, D], F32, kind="ExternalOutput")
    dbg = {}
    if DEBUG:
        dbg["mmean"] = nc.dram_tensor("d_mmean", [128, TBN], F32, kind="ExternalOutput")
        dbg["rstd"] = nc.dram_tensor("d_rstd", [128, TBN], F32, kind="ExternalOutput")
        dbg["probs"] = nc.dram_tensor("d_probs", [128, TBN, E], F32, kind="ExternalOutput")
        dbg["theta"] = nc.dram_tensor("d_theta", [128, E], F32, kind="ExternalOutput")
        dbg["gates"] = nc.dram_tensor("d_gates", [128, TBN, E], F32, kind="ExternalOutput")
        dbg["xnfm"] = nc.dram_tensor("d_xnfm", [128, DCH, TWIN], F32, kind="ExternalOutput")
        dbg["h1g"] = nc.dram_tensor("d_h1g", [128, FCH, TWIN], F32, kind="ExternalOutput")
        dbg["cnt"] = nc.dram_tensor("d_cnt", [128, E], F32, kind="ExternalOutput")
        dbg["sum1"] = nc.dram_tensor("d_sum1", [128, TBN], F32, kind="ExternalOutput")
        dbg["sum2"] = nc.dram_tensor("d_sum2", [128, TBN], F32, kind="ExternalOutput")
        dbg["var"] = nc.dram_tensor("d_var", [128, TBN], F32, kind="ExternalOutput")

    w1s = nc.dram_tensor("w1s", [E, D, FF], BF16)
    w2s = nc.dram_tensor("w2s", [E, FF, D], BF16)
    wqs = nc.dram_tensor("wqs", [D, D], BF16)
    wks = nc.dram_tensor("wks", [D, D], BF16)
    wvs = nc.dram_tensor("wvs", [D, D], BF16)
    wos = nc.dram_tensor("wos", [D, D], BF16)
    ag_in = nc.dram_tensor("ag_in", [128, 256], F32)
    ag_out = nc.dram_tensor("ag_out", [8 * 128, 256], F32)
    ar_in = nc.dram_tensor("ar_in", [64, 1040], F32)
    kst = nc.dram_tensor("kst", [16, 64], F32)
    ar_out = nc.dram_tensor("ar_out", [64, 1040], F32)

    em = EM(nc)
    es = em._es

    x = es.enter_context(nc.sbuf_tensor("x", [128, TBN, D], F32))
    w1ring = es.enter_context(nc.sbuf_tensor("w1ring", [128, 2, 4096], BF16))
    h1g = es.enter_context(nc.sbuf_tensor("h1g", [128, FCH, TWIN], BF16))
    xn_fm = es.enter_context(nc.sbuf_tensor("xn_fm", [128, DCH, TWIN], BF16))
    w2ring = es.enter_context(nc.sbuf_tensor("w2ring", [128, 3, D], BF16))
    xn_tm = es.enter_context(nc.sbuf_tensor("xn_tm", [128, 2, D], BF16))
    probs_g = es.enter_context(nc.sbuf_tensor("probs_g", [128, 2048], F32))
    s12 = es.enter_context(nc.sbuf_tensor("s12", [128, D], F32))
    s3 = es.enter_context(nc.sbuf_tensor("s3", [128, 512], F32))
    probs_l = es.enter_context(nc.sbuf_tensor("probs_l", [128, TBN, E], F32))
    gates = es.enter_context(nc.sbuf_tensor("gates", [128, TBN, E], F32))
    kv_bf = es.enter_context(nc.sbuf_tensor("kv_bf", [128, 8, DH], BF16))
    sum1 = es.enter_context(nc.sbuf_tensor("sum1", [128, TBN], F32))
    sum2 = es.enter_context(nc.sbuf_tensor("sum2", [128, TBN], F32))
    mmean = es.enter_context(nc.sbuf_tensor("mmean", [128, TBN], F32))
    rstd = es.enter_context(nc.sbuf_tensor("rstd", [128, TBN], F32))
    lo_a = es.enter_context(nc.sbuf_tensor("lo_a", [128, E], F32))
    lo_b = es.enter_context(nc.sbuf_tensor("lo_b", [128, E], F32))
    hi_a = es.enter_context(nc.sbuf_tensor("hi_a", [128, E], F32))
    hi_b = es.enter_context(nc.sbuf_tensor("hi_b", [128, E], F32))
    mid = es.enter_context(nc.sbuf_tensor("mid", [128, E], F32))
    cntf = es.enter_context(nc.sbuf_tensor("cntf", [128, E], F32))
    cntb = es.enter_context(nc.sbuf_tensor("cntb", [128, E], BF16))
    msku = es.enter_context(nc.sbuf_tensor("msku", [128, E], U8))
    rmsc = es.enter_context(nc.sbuf_tensor("rmsc", [128, TBN, E], F32))
    rm1 = es.enter_context(nc.sbuf_tensor("rm1", [128, TBN], F32))
    rm0 = es.enter_context(nc.sbuf_tensor("rm0", [128, TBN], F32))
    rw = es.enter_context(nc.sbuf_tensor("rw", [128, TBN], F32))
    ram = es.enter_context(nc.sbuf_tensor("ram", [128, TBN], F32))
    zdot = es.enter_context(nc.sbuf_tensor("zdot", [128, H], F32))
    zval = es.enter_context(nc.sbuf_tensor("zval", [128, H], F32))
    wg_sb = es.enter_context(nc.sbuf_tensor("wg_sb", [128, DCH, E], BF16))
    bias_sb = es.enter_context(nc.sbuf_tensor("bias_sb", [1, 2, D], BF16))
    id_bf = es.enter_context(nc.sbuf_tensor("id_bf", [128, 128], BF16))
    id_f = es.enter_context(nc.sbuf_tensor("id_f", [128, 128], F32))
    ones1b = es.enter_context(nc.sbuf_tensor("ones1b", [1, 128], BF16))
    ones1f = es.enter_context(nc.sbuf_tensor("ones1f", [1, 128], F32))
    ones_sq = es.enter_context(nc.sbuf_tensor("ones_sq", [128, 128], BF16))
    ones_c = es.enter_context(nc.sbuf_tensor("ones_c", [128, 1], BF16))

    ksum_bc = probs_g[:, 0:1024]
    kv_acc = probs_g[0:64, 1008:2048].rearrange("p (h c) -> p h c", h=16)
    k_tm = h1g[:, 0:8, :].rearrange("p a b -> p (a b)").rearrange("p (tb d) -> p tb d", tb=WTB)
    v_tm = h1g[:, 8:16, :].rearrange("p a b -> p (a b)").rearrange("p (tb d) -> p tb d", tb=WTB)
    attn_fm = h1g[:, 0:8, :]
    q_tm = h1g[:, 8:16, :].rearrange("p a b -> p (a b)").rearrange("p (tb d) -> p tb d", tb=WTB)
    qz_fm = [xn_fm[:, 4, :], xn_fm[:, 5, :]]
    stat_dmy = h1g[:, 0:4, :].rearrange("p a b -> p (a b)").bitcast(F32)
    stage = s12[:, :].bitcast(BF16)

    pb = [es.enter_context(nc.psum_tensor(f"pb{i}", [128, 512], F32)) for i in range(8)]
    bank_fill = [None] * 8
    bank_free = [None] * 8

    def pe_bank_acquire(b):
        if bank_free[b] is not None:
            em.waitd("t", bank_free[b])

    def evac_bank_wait(eng, b):
        assert bank_fill[b] is not None
        em.waitd(eng, bank_fill[b])

    em.op("y", lambda h: h.dma_start(out=x[:, :, :],
          in_=xin.ap().rearrange("(tb tp) d -> tp tb d", tp=128)), dmainc="XL")

    for dst, src in ((id_bf, identb), (id_f, identf), (ones1b, onesb),
                     (ones1f, onesf), (ones_sq, ones128), (ones_c, onescol)):
        em.op("g", lambda h, dst=dst, src=src: h.dma_start(out=dst[:, :], in_=src[:, :]), dmainc="MI")
    em.op("g", lambda h: h.dma_start(out=wg_sb[:, :, :],
          in_=wg.ap().rearrange("(dc dp) e -> dp dc e", dp=128)), dmainc="MI")
    mi_consts = em.cnt["MI"]

    pc_w1 = [None] * E
    pc_w2 = [None] * E
    attn_cast = {}

    def precast(dst, src, nrow):
        cols = src.shape[-1]
        rows_per = 2048 // cols * 128
        last = None
        r = 0
        while r < nrow:
            rr = min(rows_per, nrow - r)
            a = rr // 128
            em.wait("g", "PC", em.cnt.get("PC", 0))
            em.op("g", lambda h, src=src, r=r, rr=rr, a=a, cols=cols: h.dma_start(
                out=stage[:, 0:a * cols].rearrange("p (a c) -> p a c", a=a),
                in_=src[r:r + rr, :].rearrange("(a dp) c -> dp a c", dp=128)), dmainc="PC")
            em.wait("g", "PC", em.cnt["PC"])
            last = em.op("g", lambda h, dst=dst, r=r, rr=rr, a=a, cols=cols: h.dma_start(
                out=dst[r:r + rr, :].rearrange("(a dp) c -> dp a c", dp=128),
                in_=stage[:, 0:a * cols].rearrange("p (a c) -> p a c", a=a)), dmainc="PC")
            r += rr
        return last

    pc_w1[0] = precast(w1s[0], w1[0], D)
    pc_w2[0] = precast(w2s[0], w2[0], FF)

    def precast_rest():
        for e in range(1, E):
            pc_w1[e] = precast(w1s[e], w1[e], D)
            pc_w2[e] = precast(w2s[e], w2[e], FF)
        for name, src, dst in (("wk", wk, wks), ("wv", wv, wvs), ("wq", wq, wqs), ("wo", wo, wos)):
            attn_cast[name] = precast(dst, src, D)

    def load_bias_pair(na, nb, tdep):
        src = {"bq": bq, "bk": bk, "bv": bv, "bo": bo}
        em.waitd("g", tdep)
        em.wait("g", "MI", em.cnt["MI"])
        em.op("g", lambda h, s=src[na]: h.dma_start(out=bias_sb[:, 0, :], in_=s[:, :]), dmainc="MI")
        em.wait("g", "MI", em.cnt["MI"])
        return em.op("g", lambda h, s=src[nb]: h.dma_start(out=bias_sb[:, 1, :], in_=s[:, :]), dmainc="MI")

    def ln_stats(xdep):
        em.waitd("s", xdep)
        sc = None
        for tb in range(TBN):
            em.op("s", lambda h, tb=tb: h.activation(
                stat_dmy[:, :], x[:, tb, :], AF.Square, accum_out=sum2[:, tb:tb + 1]), inc="S")
            sc = em.op("s", lambda h, tb=tb: h.activation(
                stat_dmy[:, :], x[:, tb, :], AF.Copy, accum_out=sum1[:, tb:tb + 1]), inc="S")
        em.wait("v", "S", sc)
        em.op("v", lambda h: h.tensor_scalar(mmean[:, :], sum1[:, :], 1.0 / D, None, ALU.mult), inc="V")
        em.chain("v", lambda h: h.tensor_mul(ram[:, :], mmean[:, :], mmean[:, :]))
        em.chain("v", lambda h: h.scalar_tensor_tensor(
            rw[:, :], sum2[:, :], 1.0 / D, ram[:, :], ALU.mult, ALU.subtract))
        vc = em.chain("v", lambda h: h.tensor_scalar(rm0[:, :], rw[:, :], 1e-5, None, ALU.add))
        if DEBUG and DBG_CALL[0] == 0:
            em.wait("y", "V", vc)
            em.op("y", lambda h: h.dma_start(out=dbg["sum1"][:, :], in_=sum1[:, :]), dmainc="DBG")
            em.op("y", lambda h: h.dma_start(out=dbg["sum2"][:, :], in_=sum2[:, :]), dmainc="DBG")
            em.op("y", lambda h: h.dma_start(out=dbg["var"][:, :], in_=rm0[:, :]), dmainc="DBG")
        em.wait("s", "V", vc)
        sc = em.op("s", lambda h: h.activation(rm1[:, :], rm0[:, :], AF.Sqrt), inc="S")
        em.wait("v", "S", sc)
        c = em.op("v", lambda h: h.reciprocal(rstd[:, :], rm1[:, :]), inc="V")
        return ("V", c)

    tp_state = {}

    def xn_window(w, stats_dep):
        for tb in range(WTB):
            gtb = w * WTB + tb
            slot = gtb % 2
            em.waitd("v", stats_dep)
            prev = tp_state.get(slot)
            if prev is not None:
                em.waitd("v", prev)
            em.op("v", lambda h, gtb=gtb, slot=slot: h.tensor_scalar(
                xn_tm[:, slot, :], x[:, gtb, :], mmean[:, gtb:gtb + 1],
                rstd[:, gtb:gtb + 1], ALU.subtract, ALU.mult), inc="V")
            nv = em.cnt["V"]
            em.wait("t", "V", nv)
            for bi in range(2):
                b = bi
                pe_bank_acquire(b)
                tc = None
                for k in range(4):
                    dc = bi * 4 + k
                    tc = em.op("t", lambda h, slot=slot, dc=dc, b=b, k=k: h.transpose(
                        pb[b][:, :].bitcast(BF16)[:, 0:512].rearrange("p (a c) -> p a c", a=4)[:, k, :],
                        xn_tm[:, slot, dc * 128:(dc + 1) * 128], id_bf[:, :]),
                        inc="T" if k == 3 else None)
                bank_fill[b] = ("T", tc)
                evac_bank_wait("v", b)
                ev = em.op("v", lambda h, b=b, bi=bi, tb=tb: h.tensor_copy(
                    xn_fm[:, bi * 4:(bi + 1) * 4, tb * 128:(tb + 1) * 128],
                    pb[b][:, :].bitcast(BF16)[:, 0:512].rearrange("p (a c) -> p a c", a=4)), inc="V")
                bank_free[b] = ("V", ev)
            tp_state[slot] = ("T", tc)
        return ("V", em.cnt["V"])

    w1r_free = [None, None]
    w1r_sem = ["W1R0", "W1R1"]
    aq = [0]

    def w1q_load(slot, dst_view, src_ap, pc_dep):
        em.waitd("y", w1r_free[slot])
        em.waitd("y", pc_dep)
        c = em.op("y", lambda h, dst_view=dst_view, src_ap=src_ap: h.dma_start(
            out=dst_view, in_=src_ap), dmainc=w1r_sem[slot])
        return (w1r_sem[slot], c)

    w2r_free = [None, None, None]
    w2r_sem = ["W2R0", "W2R1", "W2R2"]

    def ffn_call(xdep, post_ag=None):
        stats_dep = ln_stats(xdep)

        last_probs = None
        for w in range(NW):
            xn_dep = xn_window(w, stats_dep)
            b = 2
            pe_bank_acquire(b)
            em.wait("t", "MI", mi_consts)
            em.waitd("t", xn_dep)
            lg = pb[b][:, 0:32].rearrange("p (s e) -> p s e", s=NSUB)
            tc = None
            for sub in range(NSUB):
                for dc in range(DCH):
                    tc = em.op("t", lambda h, sub=sub, dc=dc, lg=lg: h.matmul(
                        lg[:, sub, :], xn_fm[:, dc, sub * 128:(sub + 1) * 128],
                        wg_sb[:, dc, :], start=(sub == 0 and dc == 0),
                        stop=(sub == NSUB - 1 and dc == DCH - 1)),
                        inc="T" if (dc == DCH - 1) else None)
            bank_fill[b] = ("T", tc)
            evac_bank_wait("v", b)
            evac_bank_wait("s", b)
            for sub in range(NSUB):
                gtb = w * WTB + sub
                em.op("v", lambda h, sub=sub, lg=lg: h.tensor_reduce(
                    s3[:, 0:1], lg[:, sub, :], AX.X, ALU.max, negate=True), inc="V")
                vc = em.cnt["V"]
                em.wait("s", "V", vc)
                sc = em.op("s", lambda h, sub=sub, lg=lg: h.activation(
                    s3[:, 8:16], lg[:, sub, :], AF.Exp, bias=s3[:, 0:1]), inc="S")
                em.wait("v", "S", sc)
                em.op("v", lambda h: h.tensor_reduce(
                    s3[:, 1:2], s3[:, 8:16], AX.X, ALU.add), inc="V")
                em.chain("v", lambda h: h.reciprocal(s3[:, 2:3], s3[:, 1:2]))
                last_probs = em.chain("v", lambda h, gtb=gtb: h.tensor_scalar(
                    probs_l[:, gtb, :], s3[:, 8:16], s3[:, 2:3], None, ALU.mult))
            bank_free[b] = ("V", last_probs)

        em.wait("g", "V", last_probs)
        em.wait("g", "MI", em.cnt["MI"])
        dmp = em.op("g", lambda h: h.dma_start(
            out=ag_in[:, :], in_=probs_l[:, :, :].rearrange("p a e -> p (a e)")), dmainc="MI")
        em.wait("g", "MI", dmp)
        em.op("g", lambda h: h.collective_compute(
            "AllGather", ALU.bypass, replica_groups=[list(range(8))],
            ins=[ag_in.ap().opt()], outs=[ag_out.ap().opt()]), inc="CC")
        em.wait("g", "CC", em.cnt["CC"])
        pgl = em.op("g", lambda h: h.dma_start(
            out=probs_g[:, :].rearrange("p (r c) -> p r c", r=8),
            in_=ag_out.ap().rearrange("(r p) c -> p r c", p=128)), dmainc="MI")
        if post_ag is not None:
            post_ag()

        em.wait("v", "MI", pgl)
        em.op("v", lambda h: h.memset(lo_a[:, :], 0.0), inc="V")
        em.op("v", lambda h: h.memset(hi_a[:, :], 1.0), inc="V")
        pg4 = probs_g[:, :].rearrange("p (r tb e) -> p r tb e", r=8, e=8)
        s3v = s3[:, :].rearrange("p (r tb) -> p r tb", r=8)[:, :, 0:32]
        lo, hi, lon, hin = lo_a, hi_a, lo_b, hi_b
        for it in range(NBIS):
            em.chain("v", lambda h, lo=lo, hi=hi: h.tensor_add(mid[:, :], lo[:, :], hi[:, :]))
            em.chain("v", lambda h: h.tensor_scalar(mid[:, :], mid[:, :], 0.5, None, ALU.mult))
            first = True
            for e in range(E):
                if first:
                    em.chain("v", lambda h, e=e: h.tensor_scalar(
                        s3v, pg4[:, :, :, e], mid[:, e:e + 1], 0.0, ALU.is_gt, ALU.add,
                        accum_out=cntf[:, e:e + 1]))
                    first = False
                else:
                    em.op("v", lambda h, e=e: h.tensor_scalar(
                        s3v, pg4[:, :, :, e], mid[:, e:e + 1], 0.0, ALU.is_gt, ALU.add,
                        accum_out=cntf[:, e:e + 1]), inc="V")
            vc = em.chain("v", lambda h: h.tensor_copy(cntb[:, :], cntf[:, :]))
            b = 3
            pe_bank_acquire(b)
            em.wait("t", "V", vc)
            tc = em.op("t", lambda h, b=b: h.matmul(
                pb[b][:, 0:E], ones_sq[:, :], cntb[:, :], start=True, stop=True), inc="T")
            bank_fill[b] = ("T", tc)
            evac_bank_wait("v", b)
            em.op("v", lambda h, b=b: h.tensor_scalar(
                msku[:, :], pb[b][:, 0:E], float(CAP) - 0.5, None, ALU.is_ge), inc="V")
            em.chain("v", lambda h, lo=lo, lon=lon: h.select(
                lon[:, :], msku[:, :], mid[:, :], lo[:, :]))
            ev = em.chain("v", lambda h, hi=hi, hin=hin: h.select(
                hin[:, :], msku[:, :], hi[:, :], mid[:, :]))
            bank_free[b] = ("V", ev)
            lo, lon = lon, lo
            hi, hin = hin, hi
        theta = lo

        th_b = theta[:, :].rearrange("p (o e) -> p o e", o=1).broadcast_to([128, TBN, E])
        em.chain("v", lambda h, th_b=th_b: h.tensor_tensor(
            rmsc[:, :, :], probs_l[:, :, :], th_b, ALU.is_gt))
        em.chain("v", lambda h: h.tensor_mul(gates[:, :, :], probs_l[:, :, :], rmsc[:, :, :]))
        em.chain("v", lambda h: h.scalar_tensor_tensor(
            rmsc[:, :, :], rmsc[:, :, :], 1.0, gates[:, :, :], ALU.subtract, ALU.add))
        em.chain("v", lambda h: h.tensor_reduce(rm1[:, :], rmsc[:, :, :], AX.X, ALU.max))
        em.chain("v", lambda h: h.tensor_reduce(rm0[:, :], probs_l[:, :, :], AX.X, ALU.max))
        em.chain("v", lambda h: h.tensor_scalar(ram[:, :], rm1[:, :], -0.5, None, ALU.is_gt))
        em.chain("v", lambda h: h.tensor_mul(rw[:, :], ram[:, :], rm1[:, :]))
        em.chain("v", lambda h: h.tensor_scalar(
            ram[:, :], ram[:, :], -1.0, 1.0, ALU.mult, ALU.add))
        em.chain("v", lambda h: h.tensor_mul(ram[:, :], ram[:, :], rm0[:, :]))
        em.chain("v", lambda h: h.tensor_add(rw[:, :], rw[:, :], ram[:, :]))
        w_b = rw[:, :].rearrange("p (a o) -> p a o", o=1).broadcast_to([128, TBN, E])
        em.chain("v", lambda h, w_b=w_b: h.tensor_tensor(
            rmsc[:, :, :], probs_l[:, :, :], w_b, ALU.is_equal))
        em.chain("v", lambda h: h.tensor_scalar(rw[:, :], rw[:, :], 0.5, None, ALU.mult))
        gts = None
        first = True
        for tb in range(TBN):
            if first:
                gts = em.chain("v", lambda h, tb=tb: h.tensor_scalar(
                    gates[:, tb, :], rmsc[:, tb, :], rw[:, tb:tb + 1], None, ALU.mult))
                first = False
            else:
                gts = em.op("v", lambda h, tb=tb: h.tensor_scalar(
                    gates[:, tb, :], rmsc[:, tb, :], rw[:, tb:tb + 1], None, ALU.mult), inc="V")
        gates_dep = ("V", gts)
        if DEBUG and DBG_CALL[0] == 0:
            em.wait("y", "V", gts)
            em.op("y", lambda h: h.dma_start(out=dbg["mmean"][:, :], in_=mmean[:, :]), dmainc="DBG")
            em.op("y", lambda h: h.dma_start(out=dbg["rstd"][:, :], in_=rstd[:, :]), dmainc="DBG")
            em.op("y", lambda h: h.dma_start(out=dbg["probs"][:, :, :], in_=probs_l[:, :, :]), dmainc="DBG")
            em.op("y", lambda h: h.dma_start(out=dbg["theta"][:, :], in_=theta[:, :]), dmainc="DBG")
            em.op("y", lambda h: h.dma_start(out=dbg["gates"][:, :, :], in_=gates[:, :, :]), dmainc="DBG")
            em.op("y", lambda h: h.dma_start(out=dbg["cnt"][:, :], in_=cntf[:, :]), dmainc="DBG")

        h1_ready = [None] * FCH
        h1_freed = [None] * FCH
        for w in range(NW):
            xn_dep = xn_window(w, stats_dep)
            for e in range(E):
                for qt in range(4):
                    slot = aq[0] % 2
                    aq[0] += 1
                    src = w1s[e, :, qt * 512:(qt + 1) * 512].rearrange(
                        "(dc dp) f -> dp dc f", dp=128)
                    dstv = w1ring[:, slot, :].rearrange("p (dc f) -> p dc f", dc=8)
                    ldep = w1q_load(slot, dstv, src, ("PC", pc_w1[e]) if w == 0 else None)
                    em.waitd("t", ldep)
                    em.waitd("t", xn_dep)
                    w1v = dstv
                    tc = None
                    for fi in range(4):
                        f = qt * 4 + fi
                        b = f % 4
                        pe_bank_acquire(b)
                        if h1_freed[f] is not None:
                            em.waitd("s", h1_freed[f])
                        for dc in range(DCH):
                            tc = em.op("t", lambda h, w1v=w1v, fi=fi, dc=dc, b=b: h.matmul(
                                pb[b][:, :], w1v[:, dc, fi * 128:(fi + 1) * 128],
                                xn_fm[:, dc, :], start=(dc == 0), stop=(dc == DCH - 1)),
                                inc="T" if dc == DCH - 1 else None)
                        bank_fill[b] = ("T", tc)
                        evac_bank_wait("s", b)
                        sc = em.op("s", lambda h, b=b, f=f: h.activation(
                            h1g[:, f, :], pb[b][:, :], AF.Gelu), inc="S")
                        bank_free[b] = ("S", sc)
                        h1_ready[f] = ("S", sc)
                    w1r_free[slot] = ("T", tc)
                if DEBUG and DBG_CALL[0] == 0 and w == 0 and e == 0:
                    em.wait("g", "S", em.cnt["S"])
                    em.op("g", lambda h: h.dma_start(out=dbg["xnfm"][:, :, :], in_=xn_fm[:, :, :]), dmainc="DBG")
                    em.op("g", lambda h: h.dma_start(out=dbg["h1g"][:, :, :], in_=h1g[:, :, :]), dmainc="DBG")
                tc = None
                for f in range(FCH):
                    slot = f % 3
                    em.waitd("y", w2r_free[slot])
                    if w == 0:
                        em.wait("y", "PC", pc_w2[e])
                    lc = em.op("y", lambda h, slot=slot, e=e, f=f: h.dma_start(
                        out=w2ring[:, slot, :], in_=w2s[e, f * 128:(f + 1) * 128, :]),
                        dmainc=w2r_sem[slot])
                    em.wait("t", w2r_sem[slot], lc)
                    em.waitd("t", h1_ready[f])
                    for sub in range(NSUB):
                        for jh in range(2):
                            b = sub * 2 + jh
                            if f == 0:
                                pe_bank_acquire(b)
                            tc = em.op("t", lambda h, f=f, sub=sub, jh=jh, b=b, slot=slot: h.matmul(
                                pb[b][:, :], h1g[:, f, sub * 128:(sub + 1) * 128],
                                w2ring[:, slot, jh * 512:(jh + 1) * 512],
                                start=(f == 0), stop=(f == FCH - 1)),
                                inc="T" if (f == FCH - 1 or (sub == NSUB - 1 and jh == 1)) else None)
                    h1_freed[f] = ("T", tc)
                    w2r_free[slot] = ("T", tc)
                for sub in range(NSUB):
                    for jh in range(2):
                        bank_fill[sub * 2 + jh] = ("T", tc)
                em.waitd("v", gates_dep)
                for sub in range(NSUB):
                    gtb = w * WTB + sub
                    for jh in range(2):
                        b = sub * 2 + jh
                        evac_bank_wait("v", b)
                        ev = em.op("v", lambda h, b=b, gtb=gtb, jh=jh, e=e: h.scalar_tensor_tensor(
                            x[:, gtb, jh * 512:(jh + 1) * 512], pb[b][:, :],
                            gates[:, gtb, e:e + 1], x[:, gtb, jh * 512:(jh + 1) * 512],
                            ALU.mult, ALU.add), inc="V")
                        bank_free[b] = ("V", ev)
        DBG_CALL[0] += 1
        return ("V", em.cnt["V"])

    def attn_call(xdep):
        stats_dep = ln_stats(xdep)
        bias_c = load_bias_pair("bk", "bv", None)

        em.waitd("v", xdep)
        kvz = em.op("v", lambda h: h.memset(
            kv_acc[:, :, :].rearrange("p a c -> p (a c)"), 0.0), inc="V")

        for w in range(NW):
            xn_dep = xn_window(w, stats_dep)
            for proj in ("k", "v"):
                wsrc = wks if proj == "k" else wvs
                btile = 0 if proj == "k" else 1
                dst = k_tm if proj == "k" else v_tm
                cast_dep = attn_cast["wk" if proj == "k" else "wv"]
                for qt in range(4):
                    slot = aq[0] % 2
                    aq[0] += 1
                    src = wsrc[qt * 256:(qt + 1) * 256, :].rearrange(
                        "(dc dp) j -> dp dc j", dp=128)
                    dstv = w1ring[:, slot, 0:2048].rearrange("p (dc j) -> p dc j", dc=2)
                    ldep = w1q_load(slot, dstv, src, ("PC", cast_dep) if w == 0 else None)
                    em.waitd("t", ldep)
                    em.waitd("t", xn_dep)
                    tc = None
                    for dcr in range(2):
                        dc = qt * 2 + dcr
                        for sub in range(NSUB):
                            for jh in range(2):
                                b = sub * 2 + jh
                                if dc == 0:
                                    pe_bank_acquire(b)
                                tc = em.op("t", lambda h, dstv=dstv, dcr=dcr, dc=dc, sub=sub, jh=jh, b=b: h.matmul(
                                    pb[b][:, :], xn_fm[:, dc, sub * 128:(sub + 1) * 128],
                                    dstv[:, dcr, jh * 512:(jh + 1) * 512],
                                    start=(dc == 0), stop=False),
                                    inc="T" if (dcr == 1 and sub == NSUB - 1 and jh == 1) else None)
                    w1r_free[slot] = ("T", tc)
                em.wait("t", "MI", bias_c)
                for sub in range(NSUB):
                    for jh in range(2):
                        b = sub * 2 + jh
                        tc = em.op("t", lambda h, btile=btile, jh=jh, b=b: h.matmul(
                            pb[b][:, :], ones1b[:, :], bias_sb[:, btile, jh * 512:(jh + 1) * 512],
                            start=False, stop=True), inc="T")
                        bank_fill[b] = ("T", tc)
                em.wait("v", "PC", em.cnt["PC"] if w == -1 else pc_done[0])
                for sub in range(NSUB):
                    for jh in range(2):
                        b = sub * 2 + jh
                        evac_bank_wait("v", b)
                        if proj == "k":
                            em.chain("v", lambda h, b=b: h.tensor_scalar(
                                s12[:, 0:512], pb[b][:, :], 0.0, None, ALU.min))
                            vc = em.op("v", lambda h, b=b: h.tensor_scalar(
                                s12[:, 512:1024], pb[b][:, :], 0.0, None, ALU.max), inc="V")
                            em.wait("s", "V", vc)
                            sc = em.op("s", lambda h: h.activation(
                                s12[:, 0:512], s12[:, 0:512], AF.Exp), inc="S")
                            em.wait("v", "S", sc)
                            ev = em.op("v", lambda h, dst=dst, sub=sub, jh=jh: h.tensor_add(
                                dst[:, sub, jh * 512:(jh + 1) * 512],
                                s12[:, 0:512], s12[:, 512:1024]), inc="V")
                        else:
                            ev = em.op("v", lambda h, dst=dst, sub=sub, jh=jh, b=b: h.tensor_copy(
                                dst[:, sub, jh * 512:(jh + 1) * 512], pb[b][:, :]), inc="V")
                        bank_free[b] = ("V", ev)
            kv_dep = ("V", em.cnt["V"])
            for b in (0, 1, 2):
                pe_bank_acquire(b)
            em.waitd("t", kv_dep)
            em.wait("t", "V", kvz)
            em.wait("t", "MI", mi_consts)
            tc = None
            first = True
            for tb in range(WTB):
                for hh in range(H):
                    b = hh // 8
                    reg = pb[b][:, :].rearrange("p (a c) -> p a c", a=8)[0:64, hh % 8, :]
                    em.op("t", lambda h, tb=tb, hh=hh, reg=reg, first=first: h.matmul(
                        reg, k_tm[:, tb, hh * 64:(hh + 1) * 64],
                        v_tm[:, tb, hh * 64:(hh + 1) * 64],
                        start=first and (hh % 8 == 0) and tb == 0,
                        stop=(tb == WTB - 1)), inc=None)
                    ksreg = pb[2][0:64, hh:hh + 1]
                    tc = em.op("t", lambda h, tb=tb, hh=hh, ksreg=ksreg: h.matmul(
                        ksreg, k_tm[:, tb, hh * 64:(hh + 1) * 64], ones_c[:, :],
                        start=(tb == 0 and hh == 0), stop=(tb == WTB - 1)),
                        inc="T" if (tb == WTB - 1 and hh == H - 1) else None)
            for b in (0, 1, 2):
                bank_fill[b] = ("T", tc)
            for b in (0, 1):
                evac_bank_wait("v", b)
                ev = em.op("v", lambda h, b=b: h.tensor_add(
                    kv_acc[:, b * 8:(b + 1) * 8, 0:64],
                    pb[b][:, :].rearrange("p (a c) -> p a c", a=8)[0:64, :, :],
                    kv_acc[:, b * 8:(b + 1) * 8, 0:64]), inc="V")
                bank_free[b] = ("V", ev)
            evac_bank_wait("v", 2)
            ev = em.op("v", lambda h: h.tensor_add(
                kv_acc[:, :, 64:65].rearrange("p a c -> p (a c)"),
                pb[2][0:64, 0:16],
                kv_acc[:, :, 64:65].rearrange("p a c -> p (a c)")), inc="V")
            bank_free[2] = ("V", ev)
        kv_all = em.cnt["V"]

        em.wait("g", "V", kv_all)
        em.wait("g", "MI", em.cnt["MI"])
        d1 = em.op("g", lambda h: h.dma_start(
            out=ar_in[:, :], in_=kv_acc[:, :, :].rearrange("p a c -> p (a c)")), dmainc="MI")
        em.wait("g", "MI", d1)
        em.op("g", lambda h: h.collective_compute(
            "AllReduce", ALU.add, replica_groups=[[0, 1], [2, 3], [4, 5], [6, 7]],
            ins=[ar_in.ap().opt()], outs=[ar_out.ap().opt()]), inc="CC")
        em.wait("g", "CC", em.cnt["CC"])
        d2 = em.op("g", lambda h: h.dma_start(
            out=kv_acc[:, :, :].rearrange("p a c -> p (a c)"), in_=ar_out[:, :]), dmainc="MI")
        kvbf_d = None
        for two in range(2):
            em.wait("g", "MI", em.cnt["MI"])
            kvbf_d = em.op("g", lambda h, two=two: h.dma_start(
                out=kv_bf[two * 64:(two + 1) * 64, :, :],
                in_=ar_out.ap().rearrange("d (hh two c) -> two d hh c", two=2, c=65)[two, :, :, 0:64]), dmainc="MI")
        b = 4
        pe_bank_acquire(b)
        em.wait("t", "MI", d2)
        tc = em.op("t", lambda h, b=b: h.transpose(
            pb[b][0:16, 0:64], kv_acc[:, :, 64:65].rearrange("p a c -> p (a c)"),
            id_f[0:64, 0:64]), inc="T")
        bank_fill[b] = ("T", tc)
        evac_bank_wait("v", b)
        ks_ev = em.op("v", lambda h, b=b: h.tensor_copy(
            s12[0:16, 64:128], pb[b][0:16, 0:64]), inc="V")
        bank_free[b] = ("V", ks_ev)
        em.wait("g", "V", ks_ev)
        em.wait("g", "MI", em.cnt["MI"])
        em.op("g", lambda h: h.dma_start(
            out=kst[:, :], in_=s12[0:16, 64:128]), dmainc="MI")
        em.wait("g", "MI", em.cnt["MI"])
        d3 = em.op("g", lambda h: h.dma_start(
            out=s12[0:1, 0:1024],
            in_=kst.ap().rearrange("h d -> (h d)").rearrange("(o c) -> o c", o=1)), dmainc="MI")
        ks_fill = []
        for half in range(2):
            b = 4 + half
            pe_bank_acquire(b)
            em.wait("t", "MI", d3)
            tc = em.op("t", lambda h, b=b, half=half: h.matmul(
                pb[b][:, :], ones1f[:, :], s12[0:1, half * 512:(half + 1) * 512],
                start=True, stop=True), inc="T")
            bank_fill[b] = ("T", tc)
            evac_bank_wait("v", b)
            ev = em.op("v", lambda h, b=b, half=half: h.tensor_copy(
                ksum_bc[:, half * 512:(half + 1) * 512], pb[b][:, :]), inc="V")
            bank_free[b] = ("V", ev)
        ksbc_c = em.cnt["V"]

        bias2_c = load_bias_pair("bq", "bo", ("T", em.cnt["T"]))

        def emit_wo(wv, af):
            for qt in range(4):
                slot = aq[0] % 2
                aq[0] += 1
                src_ = wos[qt * 256:(qt + 1) * 256, :].rearrange("(dc dp) j -> dp dc j", dp=128)
                dstv = w1ring[:, slot, 0:2048].rearrange("p (dc j) -> p dc j", dc=2)
                ldep = w1q_load(slot, dstv, src_, ("PC", attn_cast["wo"]) if wv == 0 else None)
                em.waitd("t", ldep)
                em.waitd("t", af)
                tc = None
                for dcr in range(2):
                    dc = qt * 2 + dcr
                    for sub in range(NSUB):
                        for jh in range(2):
                            b = sub * 2 + jh
                            if dc == 0:
                                pe_bank_acquire(b)
                            tc = em.op("t", lambda h, dstv=dstv, dcr=dcr, dc=dc, sub=sub, jh=jh, b=b: h.matmul(
                                pb[b][:, :], attn_fm[:, dc, sub * 128:(sub + 1) * 128],
                                dstv[:, dcr, jh * 512:(jh + 1) * 512],
                                start=(dc == 0), stop=False),
                                inc="T" if (dcr == 1 and sub == NSUB - 1 and jh == 1) else None)
                w1r_free[slot] = ("T", tc)
            for sub in range(NSUB):
                for jh in range(2):
                    b = sub * 2 + jh
                    tc = em.op("t", lambda h, jh=jh, b=b: h.matmul(
                        pb[b][:, :], ones1b[:, :], bias_sb[:, 1, jh * 512:(jh + 1) * 512],
                        start=False, stop=True), inc="T")
                    bank_fill[b] = ("T", tc)
            for sub in range(NSUB):
                gtb = wv * WTB + sub
                for jh in range(2):
                    b = sub * 2 + jh
                    evac_bank_wait("v", b)
                    ev = em.op("v", lambda h, b=b, gtb=gtb, jh=jh: h.tensor_add(
                        x[:, gtb, jh * 512:(jh + 1) * 512], pb[b][:, :],
                        x[:, gtb, jh * 512:(jh + 1) * 512]), inc="V")
                    bank_free[b] = ("V", ev)

        pend_wo = [None]

        for w in range(NW):
            xn_dep = xn_window(w, stats_dep)
            for qt in range(4):
                slot = aq[0] % 2
                aq[0] += 1
                src = wqs[qt * 256:(qt + 1) * 256, :].rearrange("(dc dp) j -> dp dc j", dp=128)
                dstv = w1ring[:, slot, 0:2048].rearrange("p (dc j) -> p dc j", dc=2)
                ldep = w1q_load(slot, dstv, src, ("PC", attn_cast["wq"]) if w == 0 else None)
                em.waitd("t", ldep)
                em.waitd("t", xn_dep)
                tc = None
                for dcr in range(2):
                    dc = qt * 2 + dcr
                    for sub in range(NSUB):
                        for jh in range(2):
                            b = sub * 2 + jh
                            if dc == 0:
                                pe_bank_acquire(b)
                            tc = em.op("t", lambda h, dstv=dstv, dcr=dcr, dc=dc, sub=sub, jh=jh, b=b: h.matmul(
                                pb[b][:, :], xn_fm[:, dc, sub * 128:(sub + 1) * 128],
                                dstv[:, dcr, jh * 512:(jh + 1) * 512],
                                start=(dc == 0), stop=False),
                                inc="T" if (dcr == 1 and sub == NSUB - 1 and jh == 1) else None)
                w1r_free[slot] = ("T", tc)
            em.wait("t", "MI", bias2_c)
            for sub in range(NSUB):
                for jh in range(2):
                    b = sub * 2 + jh
                    tc = em.op("t", lambda h, jh=jh, b=b: h.matmul(
                        pb[b][:, :], ones1b[:, :], bias_sb[:, 0, jh * 512:(jh + 1) * 512],
                        start=False, stop=True), inc="T")
                    bank_fill[b] = ("T", tc)
            for sub in range(NSUB):
                for jh in range(2):
                    b = sub * 2 + jh
                    evac_bank_wait("v", b)
                    em.chain("v", lambda h, b=b: h.tensor_scalar(
                        s12[:, 0:512], pb[b][:, :], 0.0, None, ALU.min))
                    vc = em.op("v", lambda h, b=b: h.tensor_scalar(
                        s12[:, 512:1024], pb[b][:, :], 0.0, None, ALU.max), inc="V")
                    em.wait("s", "V", vc)
                    sc = em.op("s", lambda h: h.activation(
                        s12[:, 0:512], s12[:, 0:512], AF.Exp), inc="S")
                    em.wait("v", "S", sc)
                    ev = em.op("v", lambda h, sub=sub, jh=jh: h.tensor_add(
                        q_tm[:, sub, jh * 512:(jh + 1) * 512],
                        s12[:, 0:512], s12[:, 512:1024]), inc="V")
                    bank_free[b] = ("V", ev)
            if pend_wo[0] is not None and pend_wo[0][0] == w - 1:
                emit_wo(*pend_wo[0])
                pend_wo[0] = None
            em.wait("v", "V", ksbc_c)
            qz_c = None
            for tb in range(WTB):
                em.chain("v", lambda h, tb=tb: h.tensor_tensor(
                    s12[:, :].rearrange("p (a c) -> p a c", a=16),
                    q_tm[:, tb, :].rearrange("p (a c) -> p a c", a=16),
                    ksum_bc[:, :].rearrange("p (a c) -> p a c", a=16), ALU.mult))
                em.chain("v", lambda h: h.tensor_reduce(
                    zdot[:, :], s12[:, :].rearrange("p (a c) -> p a c", a=16),
                    AX.X, ALU.add))
                em.chain("v", lambda h: h.tensor_scalar(
                    zdot[:, :], zdot[:, :], 1e-6, None, ALU.add))
                em.chain("v", lambda h: h.reciprocal(zval[:, :], zdot[:, :]))
                zb = zval[:, :].rearrange("p (a o) -> p a o", o=1).broadcast_to([128, H, DH])
                qz_c = em.chain("v", lambda h, tb=tb, zb=zb: h.tensor_tensor(
                    q_tm[:, tb, :].rearrange("p (a c) -> p a c", a=16),
                    q_tm[:, tb, :].rearrange("p (a c) -> p a c", a=16), zb, ALU.mult))
            em.wait("t", "MI", kvbf_d)
            sc_banks = [2, 3, 4, 5, 6, 7]
            sc_i = [0]
            for jch in range(8):
                ring = jch % 2
                em.wait("t", "V", qz_c)
                b = ring
                pe_bank_acquire(b)
                tc = None
                for tb in range(WTB):
                    tc = em.op("t", lambda h, tb=tb, jch=jch, b=b: h.transpose(
                        pb[b][:, :].bitcast(BF16)[:, 0:512].rearrange("p (a c) -> p a c", a=4)[:, tb, :],
                        q_tm[:, tb, jch * 128:(jch + 1) * 128], id_bf[:, :]),
                        inc="T" if tb == WTB - 1 else None)
                bank_fill[b] = ("T", tc)
                evac_bank_wait("v", b)
                ev = em.op("v", lambda h, b=b, ring=ring: h.tensor_copy(
                    qz_fm[ring][:, :].rearrange("p (a c) -> p a c", a=4),
                    pb[b][:, :].bitcast(BF16)[:, 0:512].rearrange("p (a c) -> p a c", a=4)), inc="V")
                bank_free[b] = ("V", ev)
                for hr in range(2):
                    hh = jch * 2 + hr
                    sb = sc_banks[sc_i[0] % 6]
                    sc_i[0] += 1
                    pe_bank_acquire(sb)
                    em.wait("t", "V", ev)
                    tp = (64, 64) if hr == 1 else None
                    tc = em.op("t", lambda h, hh=hh, hr=hr, ring=ring, sb=sb, tp=tp: h.matmul(
                        pb[sb][hr * 64:(hr + 1) * 64, :],
                        kv_bf[hr * 64:(hr + 1) * 64, hh // 2, :],
                        qz_fm[ring][hr * 64:(hr + 1) * 64, :], start=True, stop=True,
                        tile_position=tp), inc="T")
                    bank_fill[sb] = ("T", tc)
                    evac_bank_wait("v", sb)
                    ev2 = em.op("v", lambda h, hh=hh, hr=hr, jch=jch, sb=sb: h.tensor_copy(
                        attn_fm[hr * 64:(hr + 1) * 64, jch, :],
                        pb[sb][hr * 64:(hr + 1) * 64, :]), inc="V")
                    bank_free[sb] = ("V", ev2)
            af_dep = ("V", em.cnt["V"])
            pend_wo[0] = (w, af_dep)
        emit_wo(*pend_wo[0])
        pend_wo[0] = None
        return ("V", em.cnt["V"])

    dep = ("XL", 16)
    pc_done = [0]

    def _hook():
        precast_rest()
        pc_done[0] = em.cnt["PC"]
    if "ffn1" in STAGES:
        dep = ffn_call(dep, post_ag=_hook)
    else:
        _hook()
    if "attn" in STAGES:
        dep = attn_call(dep)
    if "ffn2" in STAGES:
        dep = ffn_call(dep)

    for w in range(NW):
        em.waitd("y", dep)
        em.op("y", lambda h, w=w: h.dma_start(
            out=out[w * TWIN:(w + 1) * TWIN, :].rearrange("(tb tp) d -> tp tb d", tp=128),
            in_=x[:, w * WTB:(w + 1) * WTB, :]), dmainc="ST")
    em.wait("y", "ST", em.cnt["ST"])
    if DEBUG:
        em.wait("y", "DBG", em.cnt.get("DBG", 0))

    with em._es:
        with nc.Block() as block:
            @block.tensor
            def _(h):
                for fn in em.q["t"]:
                    fn(h)

            @block.scalar
            def _(h):
                for fn in em.q["s"]:
                    fn(h)

            @block.vector
            def _(h):
                for fn in em.q["v"]:
                    fn(h)

            @block.gpsimd
            def _(h):
                for fn in em.q["g"]:
                    fn(h)

            @block.sync
            def _(h):
                for fn in em.q["y"]:
                    fn(h)
    return nc


_NC_CACHE = {}


def kernel(L0_attn_x, L0_attn_wq, L0_attn_bq, L0_attn_wk, L0_attn_bk,
           L0_attn_wv, L0_attn_bv, L0_attn_wo, L0_attn_bo,
           L0_ffn_Wg, L0_ffn_W1, L0_ffn_W2):
    xfull = np.ascontiguousarray(np.asarray(L0_attn_x, dtype=np.float32))
    shared = dict(
        w1=np.ascontiguousarray(np.asarray(L0_ffn_W1, dtype=np.float32)),
        w2=np.ascontiguousarray(np.asarray(L0_ffn_W2, dtype=np.float32)),
        wg=np.ascontiguousarray(np.asarray(L0_ffn_Wg, dtype=np.float32)),
        wq=np.ascontiguousarray(np.asarray(L0_attn_wq, dtype=np.float32)),
        wk=np.ascontiguousarray(np.asarray(L0_attn_wk, dtype=np.float32)),
        wv=np.ascontiguousarray(np.asarray(L0_attn_wv, dtype=np.float32)),
        wo=np.ascontiguousarray(np.asarray(L0_attn_wo, dtype=np.float32)),
        bq=np.ascontiguousarray(np.asarray(L0_attn_bq, dtype=np.float32)).reshape(1, D),
        bk=np.ascontiguousarray(np.asarray(L0_attn_bk, dtype=np.float32)).reshape(1, D),
        bv=np.ascontiguousarray(np.asarray(L0_attn_bv, dtype=np.float32)).reshape(1, D),
        bo=np.ascontiguousarray(np.asarray(L0_attn_bo, dtype=np.float32)).reshape(1, D),
        identb=np.eye(128, dtype=ml_dtypes.bfloat16),
        identf=np.eye(128, dtype=np.float32),
        onesb=np.ones((1, 128), dtype=ml_dtypes.bfloat16),
        onesf=np.ones((1, 128), dtype=np.float32),
        ones128=np.ones((128, 128), dtype=ml_dtypes.bfloat16),
        onescol=np.ones((128, 1), dtype=ml_dtypes.bfloat16),
    )
    in_maps = []
    for c in range(NCORES):
        b, half = c // 2, c % 2
        shard = np.ascontiguousarray(xfull[b, half * NT:(half + 1) * NT, :])
        in_maps.append(dict(xin=shard, **shared))

    key = STAGES
    if key not in _NC_CACHE:
        _NC_CACHE[key] = build_nc()
    nc = _NC_CACHE[key]
    res = run_bass_kernel_spmd(nc, in_maps, core_ids=list(range(NCORES)))
    outp = np.empty((4, 8192, D), dtype=np.float32)
    for c in range(NCORES):
        b, half = c // 2, c % 2
        outp[b, half * NT:(half + 1) * NT, :] = np.asarray(res.results[c]["out"])
    if DEBUG:
        global DEBUG_RES
        DEBUG_RES = [{k: np.asarray(v) for k, v in r.items() if k.startswith("d_")}
                     for r in res.results]
    return outp


# version 09
# speedup vs baseline: 1.1235x; 1.0006x over previous
import numpy as np
import ml_dtypes
import contextlib
import concourse.bass as bass
import concourse.mybir as mybir
from concourse.bass_utils import run_bass_kernel_spmd

F32 = mybir.dt.float32
BF16 = mybir.dt.bfloat16
U8 = mybir.dt.uint8
AF = mybir.ActivationFunctionType
ALU = mybir.AluOpType
AX = mybir.AxisListType

NCORES = 8
NT = 4096
TBN = 32
D = 1024
DCH = 8
FF = 2048
FCH = 16
E = 8
H = 16
DH = 64
NW = 8
WTB = 4
TWIN = 512
NSUB = 4
CAP = 4096
NBIS = 26

STAGES = ("ffn1", "attn", "ffn2")
DEBUG = False
DBG_CALL = [0]


class EM:
    ENGS = ("t", "s", "v", "g", "y")

    def __init__(self, nc):
        self.nc = nc
        self.q = {e: [] for e in self.ENGS}
        self.sems = {}
        self.cnt = {}
        self._es = contextlib.ExitStack()
        self.last_wait = {}

    def sem(self, name):
        if name not in self.sems:
            self.sems[name] = self._es.enter_context(self.nc.semaphore(name))
            self.cnt[name] = 0
        return self.sems[name]

    def wait(self, eng, name, val):
        if val is None or val <= 0:
            return
        self.sem(name)
        assert val <= self.cnt[name], f"wait on future count {name} {val} > {self.cnt[name]}"
        key = (eng, name)
        if self.last_wait.get(key, 0) >= val:
            return
        self.last_wait[key] = val
        s = self.sems[name]
        self.q[eng].append(lambda h, s=s, v=val: h.wait_ge(s, v))

    def waitd(self, eng, dep):
        if dep is not None:
            self.wait(eng, dep[0], dep[1])

    ESEM = {"t": "T", "s": "S", "v": "V", "g": "G", "y": "Y"}

    def chain(self, eng, fn):
        nm = self.ESEM[eng]
        self.sem(nm)
        self.wait(eng, nm, self.cnt[nm])
        return self.op(eng, fn, inc=nm)

    def op(self, eng, fn, inc=None, dmainc=None):
        if inc is not None:
            s = self.sem(inc)
            self.cnt[inc] += 1
            c = self.cnt[inc]
            self.q[eng].append(lambda h, fn=fn, s=s: fn(h).then_inc(s, 1))
            return c
        if dmainc is not None:
            s = self.sem(dmainc)
            self.cnt[dmainc] += 16
            c = self.cnt[dmainc]
            self.q[eng].append(lambda h, fn=fn, s=s: fn(h).then_inc(s, 16))
            return c
        self.q[eng].append(fn)
        return None


def build_nc():
    nc = bass.Bass("TRN2", target_bir_lowering=False)

    xin = nc.dram_tensor("xin", [NT, D], F32, kind="ExternalInput")
    w1 = nc.dram_tensor("w1", [E, D, FF], F32, kind="ExternalInput")
    w2 = nc.dram_tensor("w2", [E, FF, D], F32, kind="ExternalInput")
    wg = nc.dram_tensor("wg", [D, E], F32, kind="ExternalInput")
    wq = nc.dram_tensor("wq", [D, D], F32, kind="ExternalInput")
    wk = nc.dram_tensor("wk", [D, D], F32, kind="ExternalInput")
    wv = nc.dram_tensor("wv", [D, D], F32, kind="ExternalInput")
    wo = nc.dram_tensor("wo", [D, D], F32, kind="ExternalInput")
    bq = nc.dram_tensor("bq", [1, D], F32, kind="ExternalInput")
    bk = nc.dram_tensor("bk", [1, D], F32, kind="ExternalInput")
    bv = nc.dram_tensor("bv", [1, D], F32, kind="ExternalInput")
    bo = nc.dram_tensor("bo", [1, D], F32, kind="ExternalInput")
    identb = nc.dram_tensor("identb", [128, 128], BF16, kind="ExternalInput")
    identf = nc.dram_tensor("identf", [128, 128], F32, kind="ExternalInput")
    onesb = nc.dram_tensor("onesb", [1, 128], BF16, kind="ExternalInput")
    onesf = nc.dram_tensor("onesf", [1, 128], F32, kind="ExternalInput")
    ones128 = nc.dram_tensor("ones128", [128, 128], BF16, kind="ExternalInput")
    onescol = nc.dram_tensor("onescol", [128, 1], BF16, kind="ExternalInput")
    out = nc.dram_tensor("out", [NT, D], F32, kind="ExternalOutput")
    dbg = {}
    if DEBUG:
        dbg["mmean"] = nc.dram_tensor("d_mmean", [128, TBN], F32, kind="ExternalOutput")
        dbg["rstd"] = nc.dram_tensor("d_rstd", [128, TBN], F32, kind="ExternalOutput")
        dbg["probs"] = nc.dram_tensor("d_probs", [128, TBN, E], F32, kind="ExternalOutput")
        dbg["theta"] = nc.dram_tensor("d_theta", [128, E], F32, kind="ExternalOutput")
        dbg["gates"] = nc.dram_tensor("d_gates", [128, TBN, E], F32, kind="ExternalOutput")
        dbg["xnfm"] = nc.dram_tensor("d_xnfm", [128, DCH, TWIN], F32, kind="ExternalOutput")
        dbg["h1g"] = nc.dram_tensor("d_h1g", [128, FCH, TWIN], F32, kind="ExternalOutput")
        dbg["cnt"] = nc.dram_tensor("d_cnt", [128, E], F32, kind="ExternalOutput")
        dbg["sum1"] = nc.dram_tensor("d_sum1", [128, TBN], F32, kind="ExternalOutput")
        dbg["sum2"] = nc.dram_tensor("d_sum2", [128, TBN], F32, kind="ExternalOutput")
        dbg["var"] = nc.dram_tensor("d_var", [128, TBN], F32, kind="ExternalOutput")

    w1s = nc.dram_tensor("w1s", [E, D, FF], BF16)
    w2s = nc.dram_tensor("w2s", [E, FF, D], BF16)
    wqs = nc.dram_tensor("wqs", [D, D], BF16)
    wks = nc.dram_tensor("wks", [D, D], BF16)
    wvs = nc.dram_tensor("wvs", [D, D], BF16)
    wos = nc.dram_tensor("wos", [D, D], BF16)
    ag_in = nc.dram_tensor("ag_in", [128, 256], F32)
    ag_out = nc.dram_tensor("ag_out", [8 * 128, 256], F32)
    ar_in = nc.dram_tensor("ar_in", [64, 1040], F32)
    kst = nc.dram_tensor("kst", [16, 64], F32)
    ar_out = nc.dram_tensor("ar_out", [64, 1040], F32)

    em = EM(nc)
    es = em._es

    x = es.enter_context(nc.sbuf_tensor("x", [128, TBN, D], F32))
    w1ring = es.enter_context(nc.sbuf_tensor("w1ring", [128, 2, 4096], BF16))
    h1g = es.enter_context(nc.sbuf_tensor("h1g", [128, FCH, TWIN], BF16))
    xn_fm = es.enter_context(nc.sbuf_tensor("xn_fm", [128, DCH, TWIN], BF16))
    w2ring = es.enter_context(nc.sbuf_tensor("w2ring", [128, 3, D], BF16))
    xn_tm = es.enter_context(nc.sbuf_tensor("xn_tm", [128, 2, D], BF16))
    probs_g = es.enter_context(nc.sbuf_tensor("probs_g", [128, 2048], F32))
    s12 = es.enter_context(nc.sbuf_tensor("s12", [128, D], F32))
    s3 = es.enter_context(nc.sbuf_tensor("s3", [128, 512], F32))
    probs_l = es.enter_context(nc.sbuf_tensor("probs_l", [128, TBN, E], F32))
    gates = es.enter_context(nc.sbuf_tensor("gates", [128, TBN, E], F32))
    kv_bf = es.enter_context(nc.sbuf_tensor("kv_bf", [128, 8, DH], BF16))
    sum1 = es.enter_context(nc.sbuf_tensor("sum1", [128, TBN], F32))
    sum2 = es.enter_context(nc.sbuf_tensor("sum2", [128, TBN], F32))
    mmean = es.enter_context(nc.sbuf_tensor("mmean", [128, TBN], F32))
    rstd = es.enter_context(nc.sbuf_tensor("rstd", [128, TBN], F32))
    lo_a = es.enter_context(nc.sbuf_tensor("lo_a", [128, E], F32))
    lo_b = es.enter_context(nc.sbuf_tensor("lo_b", [128, E], F32))
    hi_a = es.enter_context(nc.sbuf_tensor("hi_a", [128, E], F32))
    hi_b = es.enter_context(nc.sbuf_tensor("hi_b", [128, E], F32))
    mid = es.enter_context(nc.sbuf_tensor("mid", [128, E], F32))
    cntf = es.enter_context(nc.sbuf_tensor("cntf", [128, E], F32))
    cntb = es.enter_context(nc.sbuf_tensor("cntb", [128, E], BF16))
    msku = es.enter_context(nc.sbuf_tensor("msku", [128, E], U8))
    rmsc = es.enter_context(nc.sbuf_tensor("rmsc", [128, TBN, E], F32))
    rm1 = es.enter_context(nc.sbuf_tensor("rm1", [128, TBN], F32))
    rm0 = es.enter_context(nc.sbuf_tensor("rm0", [128, TBN], F32))
    rw = es.enter_context(nc.sbuf_tensor("rw", [128, TBN], F32))
    ram = es.enter_context(nc.sbuf_tensor("ram", [128, TBN], F32))
    zdot = es.enter_context(nc.sbuf_tensor("zdot", [128, WTB * H], F32))
    zval = es.enter_context(nc.sbuf_tensor("zval", [128, WTB * H], F32))
    wg_sb = es.enter_context(nc.sbuf_tensor("wg_sb", [128, DCH, E], BF16))
    bias_sb = es.enter_context(nc.sbuf_tensor("bias_sb", [1, 2, D], BF16))
    id_bf = es.enter_context(nc.sbuf_tensor("id_bf", [128, 128], BF16))
    id_f = es.enter_context(nc.sbuf_tensor("id_f", [128, 128], F32))
    ones1b = es.enter_context(nc.sbuf_tensor("ones1b", [1, 128], BF16))
    ones1f = es.enter_context(nc.sbuf_tensor("ones1f", [1, 128], F32))
    ones_sq = es.enter_context(nc.sbuf_tensor("ones_sq", [128, 128], BF16))
    ones_c = es.enter_context(nc.sbuf_tensor("ones_c", [128, 1], BF16))

    ksum_bc = probs_g[:, 0:1024]
    kv_acc = probs_g[0:64, 1008:2048].rearrange("p (h c) -> p h c", h=16)
    k_tm = h1g[:, 0:8, :].rearrange("p a b -> p (a b)").rearrange("p (tb d) -> p tb d", tb=WTB)
    v_tm = h1g[:, 8:16, :].rearrange("p a b -> p (a b)").rearrange("p (tb d) -> p tb d", tb=WTB)
    attn_fm = h1g[:, 0:8, :]
    q_tm = h1g[:, 8:16, :].rearrange("p a b -> p (a b)").rearrange("p (tb d) -> p tb d", tb=WTB)
    qz_fm = [xn_fm[:, 4, :], xn_fm[:, 5, :]]
    stat_dmy = h1g[:, 0:4, :].rearrange("p a b -> p (a b)").bitcast(F32)
    stage = s12[:, :].bitcast(BF16)

    pb = [es.enter_context(nc.psum_tensor(f"pb{i}", [128, 512], F32)) for i in range(8)]
    bank_fill = [None] * 8
    bank_free = [None] * 8

    def pe_bank_acquire(b):
        if bank_free[b] is not None:
            em.waitd("t", bank_free[b])

    def evac_bank_wait(eng, b):
        assert bank_fill[b] is not None
        em.waitd(eng, bank_fill[b])

    em.op("y", lambda h: h.dma_start(out=x[:, :, :],
          in_=xin.ap().rearrange("(tb tp) d -> tp tb d", tp=128)), dmainc="XL")

    for dst, src in ((id_bf, identb), (id_f, identf), (ones1b, onesb),
                     (ones1f, onesf), (ones_sq, ones128), (ones_c, onescol)):
        em.op("g", lambda h, dst=dst, src=src: h.dma_start(out=dst[:, :], in_=src[:, :]), dmainc="MI")
    em.op("g", lambda h: h.dma_start(out=wg_sb[:, :, :],
          in_=wg.ap().rearrange("(dc dp) e -> dp dc e", dp=128)), dmainc="MI")
    mi_consts = em.cnt["MI"]

    pc_w1 = [None] * E
    pc_w2 = [None] * E
    attn_cast = {}

    def precast(dst, src, nrow):
        cols = src.shape[-1]
        rows_per = 2048 // cols * 128
        last = None
        r = 0
        while r < nrow:
            rr = min(rows_per, nrow - r)
            a = rr // 128
            em.wait("g", "PC", em.cnt.get("PC", 0))
            em.op("g", lambda h, src=src, r=r, rr=rr, a=a, cols=cols: h.dma_start(
                out=stage[:, 0:a * cols].rearrange("p (a c) -> p a c", a=a),
                in_=src[r:r + rr, :].rearrange("(a dp) c -> dp a c", dp=128)), dmainc="PC")
            em.wait("g", "PC", em.cnt["PC"])
            last = em.op("g", lambda h, dst=dst, r=r, rr=rr, a=a, cols=cols: h.dma_start(
                out=dst[r:r + rr, :].rearrange("(a dp) c -> dp a c", dp=128),
                in_=stage[:, 0:a * cols].rearrange("p (a c) -> p a c", a=a)), dmainc="PC")
            r += rr
        return last

    pc_w1[0] = precast(w1s[0], w1[0], D)
    pc_w2[0] = precast(w2s[0], w2[0], FF)

    def precast_rest():
        for e in range(1, E):
            pc_w1[e] = precast(w1s[e], w1[e], D)
            pc_w2[e] = precast(w2s[e], w2[e], FF)
        for name, src, dst in (("wk", wk, wks), ("wv", wv, wvs), ("wq", wq, wqs), ("wo", wo, wos)):
            attn_cast[name] = precast(dst, src, D)

    def load_bias_pair(na, nb, tdep):
        src = {"bq": bq, "bk": bk, "bv": bv, "bo": bo}
        em.waitd("g", tdep)
        em.wait("g", "MI", em.cnt["MI"])
        em.op("g", lambda h, s=src[na]: h.dma_start(out=bias_sb[:, 0, :], in_=s[:, :]), dmainc="MI")
        em.wait("g", "MI", em.cnt["MI"])
        return em.op("g", lambda h, s=src[nb]: h.dma_start(out=bias_sb[:, 1, :], in_=s[:, :]), dmainc="MI")

    def ln_stats(xdep):
        em.waitd("s", xdep)
        sc = None
        for tb in range(TBN):
            em.op("s", lambda h, tb=tb: h.activation(
                stat_dmy[:, :], x[:, tb, :], AF.Square, accum_out=sum2[:, tb:tb + 1]), inc="S")
            sc = em.op("s", lambda h, tb=tb: h.activation(
                stat_dmy[:, :], x[:, tb, :], AF.Copy, accum_out=sum1[:, tb:tb + 1]), inc="S")
        em.wait("v", "S", sc)
        em.op("v", lambda h: h.tensor_scalar(mmean[:, :], sum1[:, :], 1.0 / D, None, ALU.mult), inc="V")
        em.chain("v", lambda h: h.tensor_mul(ram[:, :], mmean[:, :], mmean[:, :]))
        em.chain("v", lambda h: h.scalar_tensor_tensor(
            rw[:, :], sum2[:, :], 1.0 / D, ram[:, :], ALU.mult, ALU.subtract))
        vc = em.chain("v", lambda h: h.tensor_scalar(rm0[:, :], rw[:, :], 1e-5, None, ALU.add))
        if DEBUG and DBG_CALL[0] == 0:
            em.wait("y", "V", vc)
            em.op("y", lambda h: h.dma_start(out=dbg["sum1"][:, :], in_=sum1[:, :]), dmainc="DBG")
            em.op("y", lambda h: h.dma_start(out=dbg["sum2"][:, :], in_=sum2[:, :]), dmainc="DBG")
            em.op("y", lambda h: h.dma_start(out=dbg["var"][:, :], in_=rm0[:, :]), dmainc="DBG")
        em.wait("s", "V", vc)
        sc = em.op("s", lambda h: h.activation(rm1[:, :], rm0[:, :], AF.Sqrt), inc="S")
        em.wait("v", "S", sc)
        c = em.op("v", lambda h: h.reciprocal(rstd[:, :], rm1[:, :]), inc="V")
        return ("V", c)

    tp_state = {}

    def xn_window(w, stats_dep):
        for tb in range(WTB):
            gtb = w * WTB + tb
            slot = gtb % 2
            em.waitd("v", stats_dep)
            prev = tp_state.get(slot)
            if prev is not None:
                em.waitd("v", prev)
            em.op("v", lambda h, gtb=gtb, slot=slot: h.tensor_scalar(
                xn_tm[:, slot, :], x[:, gtb, :], mmean[:, gtb:gtb + 1],
                rstd[:, gtb:gtb + 1], ALU.subtract, ALU.mult), inc="V")
            nv = em.cnt["V"]
            em.wait("t", "V", nv)
            for bi in range(2):
                b = bi
                pe_bank_acquire(b)
                tc = None
                for k in range(4):
                    dc = bi * 4 + k
                    tc = em.op("t", lambda h, slot=slot, dc=dc, b=b, k=k: h.transpose(
                        pb[b][:, :].bitcast(BF16)[:, 0:512].rearrange("p (a c) -> p a c", a=4)[:, k, :],
                        xn_tm[:, slot, dc * 128:(dc + 1) * 128], id_bf[:, :]),
                        inc="T" if k == 3 else None)
                bank_fill[b] = ("T", tc)
                evac_bank_wait("v", b)
                ev = em.op("v", lambda h, b=b, bi=bi, tb=tb: h.tensor_copy(
                    xn_fm[:, bi * 4:(bi + 1) * 4, tb * 128:(tb + 1) * 128],
                    pb[b][:, :].bitcast(BF16)[:, 0:512].rearrange("p (a c) -> p a c", a=4)), inc="V")
                bank_free[b] = ("V", ev)
            tp_state[slot] = ("T", tc)
        return ("V", em.cnt["V"])

    w1r_free = [None, None]
    w1r_sem = ["W1R0", "W1R1"]
    aq = [0]

    def w1q_load(slot, dst_view, src_ap, pc_dep):
        em.waitd("y", w1r_free[slot])
        em.waitd("y", pc_dep)
        c = em.op("y", lambda h, dst_view=dst_view, src_ap=src_ap: h.dma_start(
            out=dst_view, in_=src_ap), dmainc=w1r_sem[slot])
        return (w1r_sem[slot], c)

    w2r_free = [None, None, None]
    w2r_sem = ["W2R0", "W2R1", "W2R2"]

    def ffn_call(xdep, post_ag=None):
        stats_dep = ln_stats(xdep)

        last_probs = None
        for w in range(NW):
            xn_dep = xn_window(w, stats_dep)
            b = 2
            pe_bank_acquire(b)
            em.wait("t", "MI", mi_consts)
            em.waitd("t", xn_dep)
            lg = pb[b][:, 0:32].rearrange("p (s e) -> p s e", s=NSUB)
            tc = None
            for sub in range(NSUB):
                for dc in range(DCH):
                    tc = em.op("t", lambda h, sub=sub, dc=dc, lg=lg: h.matmul(
                        lg[:, sub, :], xn_fm[:, dc, sub * 128:(sub + 1) * 128],
                        wg_sb[:, dc, :], start=(sub == 0 and dc == 0),
                        stop=(sub == NSUB - 1 and dc == DCH - 1)),
                        inc="T" if (dc == DCH - 1) else None)
            bank_fill[b] = ("T", tc)
            evac_bank_wait("v", b)
            evac_bank_wait("s", b)
            for sub in range(NSUB):
                gtb = w * WTB + sub
                em.op("v", lambda h, sub=sub, lg=lg: h.tensor_reduce(
                    s3[:, 0:1], lg[:, sub, :], AX.X, ALU.max, negate=True), inc="V")
                vc = em.cnt["V"]
                em.wait("s", "V", vc)
                sc = em.op("s", lambda h, sub=sub, lg=lg: h.activation(
                    s3[:, 8:16], lg[:, sub, :], AF.Exp, bias=s3[:, 0:1]), inc="S")
                em.wait("v", "S", sc)
                em.op("v", lambda h: h.tensor_reduce(
                    s3[:, 1:2], s3[:, 8:16], AX.X, ALU.add), inc="V")
                em.chain("v", lambda h: h.reciprocal(s3[:, 2:3], s3[:, 1:2]))
                last_probs = em.chain("v", lambda h, gtb=gtb: h.tensor_scalar(
                    probs_l[:, gtb, :], s3[:, 8:16], s3[:, 2:3], None, ALU.mult))
            bank_free[b] = ("V", last_probs)

        em.wait("g", "V", last_probs)
        em.wait("g", "MI", em.cnt["MI"])
        dmp = em.op("g", lambda h: h.dma_start(
            out=ag_in[:, :], in_=probs_l[:, :, :].rearrange("p a e -> p (a e)")), dmainc="MI")
        em.wait("g", "MI", dmp)
        em.op("g", lambda h: h.collective_compute(
            "AllGather", ALU.bypass, replica_groups=[list(range(8))],
            ins=[ag_in.ap().opt()], outs=[ag_out.ap().opt()]), inc="CC")
        em.wait("g", "CC", em.cnt["CC"])
        pgl = em.op("g", lambda h: h.dma_start(
            out=probs_g[:, :].rearrange("p (r c) -> p r c", r=8),
            in_=ag_out.ap().rearrange("(r p) c -> p r c", p=128)), dmainc="MI")
        if post_ag is not None:
            post_ag()

        em.wait("v", "MI", pgl)
        em.op("v", lambda h: h.memset(lo_a[:, :], 0.0), inc="V")
        em.op("v", lambda h: h.memset(hi_a[:, :], 1.0), inc="V")
        pg4 = probs_g[:, :].rearrange("p (r tb e) -> p r tb e", r=8, e=8)
        s3v = s3[:, :].rearrange("p (r tb) -> p r tb", r=8)[:, :, 0:32]
        lo, hi, lon, hin = lo_a, hi_a, lo_b, hi_b
        for it in range(NBIS):
            em.chain("v", lambda h, lo=lo, hi=hi: h.tensor_add(mid[:, :], lo[:, :], hi[:, :]))
            em.chain("v", lambda h: h.tensor_scalar(mid[:, :], mid[:, :], 0.5, None, ALU.mult))
            first = True
            for e in range(E):
                if first:
                    em.chain("v", lambda h, e=e: h.tensor_scalar(
                        s3v, pg4[:, :, :, e], mid[:, e:e + 1], 0.0, ALU.is_gt, ALU.add,
                        accum_out=cntf[:, e:e + 1]))
                    first = False
                else:
                    em.op("v", lambda h, e=e: h.tensor_scalar(
                        s3v, pg4[:, :, :, e], mid[:, e:e + 1], 0.0, ALU.is_gt, ALU.add,
                        accum_out=cntf[:, e:e + 1]), inc="V")
            vc = em.chain("v", lambda h: h.tensor_copy(cntb[:, :], cntf[:, :]))
            b = 3
            pe_bank_acquire(b)
            em.wait("t", "V", vc)
            tc = em.op("t", lambda h, b=b: h.matmul(
                pb[b][:, 0:E], ones_sq[:, :], cntb[:, :], start=True, stop=True), inc="T")
            bank_fill[b] = ("T", tc)
            evac_bank_wait("v", b)
            em.op("v", lambda h, b=b: h.tensor_scalar(
                msku[:, :], pb[b][:, 0:E], float(CAP) - 0.5, None, ALU.is_ge), inc="V")
            em.chain("v", lambda h, lo=lo, lon=lon: h.select(
                lon[:, :], msku[:, :], mid[:, :], lo[:, :]))
            ev = em.chain("v", lambda h, hi=hi, hin=hin: h.select(
                hin[:, :], msku[:, :], hi[:, :], mid[:, :]))
            bank_free[b] = ("V", ev)
            lo, lon = lon, lo
            hi, hin = hin, hi
        theta = lo

        th_b = theta[:, :].rearrange("p (o e) -> p o e", o=1).broadcast_to([128, TBN, E])
        em.chain("v", lambda h, th_b=th_b: h.tensor_tensor(
            rmsc[:, :, :], probs_l[:, :, :], th_b, ALU.is_gt))
        em.chain("v", lambda h: h.tensor_mul(gates[:, :, :], probs_l[:, :, :], rmsc[:, :, :]))
        em.chain("v", lambda h: h.scalar_tensor_tensor(
            rmsc[:, :, :], rmsc[:, :, :], 1.0, gates[:, :, :], ALU.subtract, ALU.add))
        em.chain("v", lambda h: h.tensor_reduce(rm1[:, :], rmsc[:, :, :], AX.X, ALU.max))
        em.chain("v", lambda h: h.tensor_reduce(rm0[:, :], probs_l[:, :, :], AX.X, ALU.max))
        em.chain("v", lambda h: h.tensor_scalar(ram[:, :], rm1[:, :], -0.5, None, ALU.is_gt))
        em.chain("v", lambda h: h.tensor_mul(rw[:, :], ram[:, :], rm1[:, :]))
        em.chain("v", lambda h: h.tensor_scalar(
            ram[:, :], ram[:, :], -1.0, 1.0, ALU.mult, ALU.add))
        em.chain("v", lambda h: h.tensor_mul(ram[:, :], ram[:, :], rm0[:, :]))
        em.chain("v", lambda h: h.tensor_add(rw[:, :], rw[:, :], ram[:, :]))
        w_b = rw[:, :].rearrange("p (a o) -> p a o", o=1).broadcast_to([128, TBN, E])
        em.chain("v", lambda h, w_b=w_b: h.tensor_tensor(
            rmsc[:, :, :], probs_l[:, :, :], w_b, ALU.is_equal))
        em.chain("v", lambda h: h.tensor_scalar(rw[:, :], rw[:, :], 0.5, None, ALU.mult))
        gts = None
        first = True
        for tb in range(TBN):
            if first:
                gts = em.chain("v", lambda h, tb=tb: h.tensor_scalar(
                    gates[:, tb, :], rmsc[:, tb, :], rw[:, tb:tb + 1], None, ALU.mult))
                first = False
            else:
                gts = em.op("v", lambda h, tb=tb: h.tensor_scalar(
                    gates[:, tb, :], rmsc[:, tb, :], rw[:, tb:tb + 1], None, ALU.mult), inc="V")
        gates_dep = ("V", gts)
        if DEBUG and DBG_CALL[0] == 0:
            em.wait("y", "V", gts)
            em.op("y", lambda h: h.dma_start(out=dbg["mmean"][:, :], in_=mmean[:, :]), dmainc="DBG")
            em.op("y", lambda h: h.dma_start(out=dbg["rstd"][:, :], in_=rstd[:, :]), dmainc="DBG")
            em.op("y", lambda h: h.dma_start(out=dbg["probs"][:, :, :], in_=probs_l[:, :, :]), dmainc="DBG")
            em.op("y", lambda h: h.dma_start(out=dbg["theta"][:, :], in_=theta[:, :]), dmainc="DBG")
            em.op("y", lambda h: h.dma_start(out=dbg["gates"][:, :, :], in_=gates[:, :, :]), dmainc="DBG")
            em.op("y", lambda h: h.dma_start(out=dbg["cnt"][:, :], in_=cntf[:, :]), dmainc="DBG")

        h1_ready = [None] * FCH
        h1_freed = [None] * FCH
        for w in range(NW):
            xn_dep = xn_window(w, stats_dep)
            for e in range(E):
                for qt in range(4):
                    slot = aq[0] % 2
                    aq[0] += 1
                    src = w1s[e, :, qt * 512:(qt + 1) * 512].rearrange(
                        "(dc dp) f -> dp dc f", dp=128)
                    dstv = w1ring[:, slot, :].rearrange("p (dc f) -> p dc f", dc=8)
                    ldep = w1q_load(slot, dstv, src, ("PC", pc_w1[e]) if w == 0 else None)
                    em.waitd("t", ldep)
                    em.waitd("t", xn_dep)
                    w1v = dstv
                    tc = None
                    for fi in range(4):
                        f = qt * 4 + fi
                        b = f % 4
                        pe_bank_acquire(b)
                        if h1_freed[f] is not None:
                            em.waitd("s", h1_freed[f])
                        for dc in range(DCH):
                            tc = em.op("t", lambda h, w1v=w1v, fi=fi, dc=dc, b=b: h.matmul(
                                pb[b][:, :], w1v[:, dc, fi * 128:(fi + 1) * 128],
                                xn_fm[:, dc, :], start=(dc == 0), stop=(dc == DCH - 1)),
                                inc="T" if dc == DCH - 1 else None)
                        bank_fill[b] = ("T", tc)
                        evac_bank_wait("s", b)
                        sc = em.op("s", lambda h, b=b, f=f: h.activation(
                            h1g[:, f, :], pb[b][:, :], AF.Gelu), inc="S")
                        bank_free[b] = ("S", sc)
                        h1_ready[f] = ("S", sc)
                    w1r_free[slot] = ("T", tc)
                if DEBUG and DBG_CALL[0] == 0 and w == 0 and e == 0:
                    em.wait("g", "S", em.cnt["S"])
                    em.op("g", lambda h: h.dma_start(out=dbg["xnfm"][:, :, :], in_=xn_fm[:, :, :]), dmainc="DBG")
                    em.op("g", lambda h: h.dma_start(out=dbg["h1g"][:, :, :], in_=h1g[:, :, :]), dmainc="DBG")
                tc = None
                for f in range(FCH):
                    slot = f % 3
                    em.waitd("y", w2r_free[slot])
                    if w == 0:
                        em.wait("y", "PC", pc_w2[e])
                    lc = em.op("y", lambda h, slot=slot, e=e, f=f: h.dma_start(
                        out=w2ring[:, slot, :], in_=w2s[e, f * 128:(f + 1) * 128, :]),
                        dmainc=w2r_sem[slot])
                    em.wait("t", w2r_sem[slot], lc)
                    em.waitd("t", h1_ready[f])
                    for sub in range(NSUB):
                        for jh in range(2):
                            b = sub * 2 + jh
                            if f == 0:
                                pe_bank_acquire(b)
                            tc = em.op("t", lambda h, f=f, sub=sub, jh=jh, b=b, slot=slot: h.matmul(
                                pb[b][:, :], h1g[:, f, sub * 128:(sub + 1) * 128],
                                w2ring[:, slot, jh * 512:(jh + 1) * 512],
                                start=(f == 0), stop=(f == FCH - 1)),
                                inc="T" if (f == FCH - 1 or (sub == NSUB - 1 and jh == 1)) else None)
                    h1_freed[f] = ("T", tc)
                    w2r_free[slot] = ("T", tc)
                for sub in range(NSUB):
                    for jh in range(2):
                        bank_fill[sub * 2 + jh] = ("T", tc)
                em.waitd("v", gates_dep)
                for sub in range(NSUB):
                    gtb = w * WTB + sub
                    for jh in range(2):
                        b = sub * 2 + jh
                        evac_bank_wait("v", b)
                        ev = em.op("v", lambda h, b=b, gtb=gtb, jh=jh, e=e: h.scalar_tensor_tensor(
                            x[:, gtb, jh * 512:(jh + 1) * 512], pb[b][:, :],
                            gates[:, gtb, e:e + 1], x[:, gtb, jh * 512:(jh + 1) * 512],
                            ALU.mult, ALU.add), inc="V")
                        bank_free[b] = ("V", ev)
        DBG_CALL[0] += 1
        return ("V", em.cnt["V"])

    def attn_call(xdep):
        stats_dep = ln_stats(xdep)
        bias_c = load_bias_pair("bk", "bv", None)

        em.waitd("v", xdep)
        kvz = em.op("v", lambda h: h.memset(
            kv_acc[:, :, :].rearrange("p a c -> p (a c)"), 0.0), inc="V")

        for w in range(NW):
            xn_dep = xn_window(w, stats_dep)
            for proj in ("k", "v"):
                wsrc = wks if proj == "k" else wvs
                btile = 0 if proj == "k" else 1
                dst = k_tm if proj == "k" else v_tm
                cast_dep = attn_cast["wk" if proj == "k" else "wv"]
                for qt in range(4):
                    slot = aq[0] % 2
                    aq[0] += 1
                    src = wsrc[qt * 256:(qt + 1) * 256, :].rearrange(
                        "(dc dp) j -> dp dc j", dp=128)
                    dstv = w1ring[:, slot, 0:2048].rearrange("p (dc j) -> p dc j", dc=2)
                    ldep = w1q_load(slot, dstv, src, ("PC", cast_dep) if w == 0 else None)
                    em.waitd("t", ldep)
                    em.waitd("t", xn_dep)
                    tc = None
                    for dcr in range(2):
                        dc = qt * 2 + dcr
                        for sub in range(NSUB):
                            for jh in range(2):
                                b = sub * 2 + jh
                                if dc == 0:
                                    pe_bank_acquire(b)
                                tc = em.op("t", lambda h, dstv=dstv, dcr=dcr, dc=dc, sub=sub, jh=jh, b=b: h.matmul(
                                    pb[b][:, :], xn_fm[:, dc, sub * 128:(sub + 1) * 128],
                                    dstv[:, dcr, jh * 512:(jh + 1) * 512],
                                    start=(dc == 0), stop=False),
                                    inc="T" if (dcr == 1 and sub == NSUB - 1 and jh == 1) else None)
                    w1r_free[slot] = ("T", tc)
                em.wait("t", "MI", bias_c)
                for sub in range(NSUB):
                    for jh in range(2):
                        b = sub * 2 + jh
                        tc = em.op("t", lambda h, btile=btile, jh=jh, b=b: h.matmul(
                            pb[b][:, :], ones1b[:, :], bias_sb[:, btile, jh * 512:(jh + 1) * 512],
                            start=False, stop=True), inc="T")
                        bank_fill[b] = ("T", tc)
                em.wait("v", "PC", em.cnt["PC"] if w == -1 else pc_done[0])
                for sub in range(NSUB):
                    for jh in range(2):
                        b = sub * 2 + jh
                        evac_bank_wait("v", b)
                        if proj == "k":
                            em.chain("v", lambda h, b=b: h.tensor_scalar(
                                s12[:, 0:512], pb[b][:, :], 0.0, None, ALU.min))
                            vc = em.op("v", lambda h, b=b: h.tensor_scalar(
                                s12[:, 512:1024], pb[b][:, :], 0.0, None, ALU.max), inc="V")
                            em.wait("s", "V", vc)
                            sc = em.op("s", lambda h: h.activation(
                                s12[:, 0:512], s12[:, 0:512], AF.Exp), inc="S")
                            em.wait("v", "S", sc)
                            ev = em.op("v", lambda h, dst=dst, sub=sub, jh=jh: h.tensor_add(
                                dst[:, sub, jh * 512:(jh + 1) * 512],
                                s12[:, 0:512], s12[:, 512:1024]), inc="V")
                        else:
                            ev = em.op("v", lambda h, dst=dst, sub=sub, jh=jh, b=b: h.tensor_copy(
                                dst[:, sub, jh * 512:(jh + 1) * 512], pb[b][:, :]), inc="V")
                        bank_free[b] = ("V", ev)
            kv_dep = ("V", em.cnt["V"])
            for b in (0, 1, 2):
                pe_bank_acquire(b)
            em.waitd("t", kv_dep)
            em.wait("t", "V", kvz)
            em.wait("t", "MI", mi_consts)
            tc = None
            first = True
            for tb in range(WTB):
                for hh in range(H):
                    b = hh // 8
                    reg = pb[b][:, :].rearrange("p (a c) -> p a c", a=8)[0:64, hh % 8, :]
                    em.op("t", lambda h, tb=tb, hh=hh, reg=reg, first=first: h.matmul(
                        reg, k_tm[:, tb, hh * 64:(hh + 1) * 64],
                        v_tm[:, tb, hh * 64:(hh + 1) * 64],
                        start=first and (hh % 8 == 0) and tb == 0,
                        stop=(tb == WTB - 1)), inc=None)
                    ksreg = pb[2][0:64, hh:hh + 1]
                    tc = em.op("t", lambda h, tb=tb, hh=hh, ksreg=ksreg: h.matmul(
                        ksreg, k_tm[:, tb, hh * 64:(hh + 1) * 64], ones_c[:, :],
                        start=(tb == 0 and hh == 0), stop=(tb == WTB - 1)),
                        inc="T" if (tb == WTB - 1 and hh == H - 1) else None)
            for b in (0, 1, 2):
                bank_fill[b] = ("T", tc)
            for b in (0, 1):
                evac_bank_wait("v", b)
                ev = em.op("v", lambda h, b=b: h.tensor_add(
                    kv_acc[:, b * 8:(b + 1) * 8, 0:64],
                    pb[b][:, :].rearrange("p (a c) -> p a c", a=8)[0:64, :, :],
                    kv_acc[:, b * 8:(b + 1) * 8, 0:64]), inc="V")
                bank_free[b] = ("V", ev)
            evac_bank_wait("v", 2)
            ev = em.op("v", lambda h: h.tensor_add(
                kv_acc[:, :, 64:65].rearrange("p a c -> p (a c)"),
                pb[2][0:64, 0:16],
                kv_acc[:, :, 64:65].rearrange("p a c -> p (a c)")), inc="V")
            bank_free[2] = ("V", ev)
        kv_all = em.cnt["V"]

        em.wait("g", "V", kv_all)
        em.wait("g", "MI", em.cnt["MI"])
        d1 = em.op("g", lambda h: h.dma_start(
            out=ar_in[:, :], in_=kv_acc[:, :, :].rearrange("p a c -> p (a c)")), dmainc="MI")
        em.wait("g", "MI", d1)
        em.op("g", lambda h: h.collective_compute(
            "AllReduce", ALU.add, replica_groups=[[0, 1], [2, 3], [4, 5], [6, 7]],
            ins=[ar_in.ap().opt()], outs=[ar_out.ap().opt()]), inc="CC")
        em.wait("g", "CC", em.cnt["CC"])
        d2 = em.op("g", lambda h: h.dma_start(
            out=kv_acc[:, :, :].rearrange("p a c -> p (a c)"), in_=ar_out[:, :]), dmainc="MI")
        kvbf_d = None
        for two in range(2):
            em.wait("g", "MI", em.cnt["MI"])
            kvbf_d = em.op("g", lambda h, two=two: h.dma_start(
                out=kv_bf[two * 64:(two + 1) * 64, :, :],
                in_=ar_out.ap().rearrange("d (hh two c) -> two d hh c", two=2, c=65)[two, :, :, 0:64]), dmainc="MI")
        b = 4
        pe_bank_acquire(b)
        em.wait("t", "MI", d2)
        tc = em.op("t", lambda h, b=b: h.transpose(
            pb[b][0:16, 0:64], kv_acc[:, :, 64:65].rearrange("p a c -> p (a c)"),
            id_f[0:64, 0:64]), inc="T")
        bank_fill[b] = ("T", tc)
        evac_bank_wait("v", b)
        ks_ev = em.op("v", lambda h, b=b: h.tensor_copy(
            s12[0:16, 64:128], pb[b][0:16, 0:64]), inc="V")
        bank_free[b] = ("V", ks_ev)
        em.wait("g", "V", ks_ev)
        em.wait("g", "MI", em.cnt["MI"])
        em.op("g", lambda h: h.dma_start(
            out=kst[:, :], in_=s12[0:16, 64:128]), dmainc="MI")
        em.wait("g", "MI", em.cnt["MI"])
        d3 = em.op("g", lambda h: h.dma_start(
            out=s12[0:1, 0:1024],
            in_=kst.ap().rearrange("h d -> (h d)").rearrange("(o c) -> o c", o=1)), dmainc="MI")
        ks_fill = []
        for half in range(2):
            b = 4 + half
            pe_bank_acquire(b)
            em.wait("t", "MI", d3)
            tc = em.op("t", lambda h, b=b, half=half: h.matmul(
                pb[b][:, :], ones1f[:, :], s12[0:1, half * 512:(half + 1) * 512],
                start=True, stop=True), inc="T")
            bank_fill[b] = ("T", tc)
            evac_bank_wait("v", b)
            ev = em.op("v", lambda h, b=b, half=half: h.tensor_copy(
                ksum_bc[:, half * 512:(half + 1) * 512], pb[b][:, :]), inc="V")
            bank_free[b] = ("V", ev)
        ksbc_c = em.cnt["V"]

        bias2_c = load_bias_pair("bq", "bo", ("T", em.cnt["T"]))

        def emit_wo(wv, af):
            for qt in range(4):
                slot = aq[0] % 2
                aq[0] += 1
                src_ = wos[qt * 256:(qt + 1) * 256, :].rearrange("(dc dp) j -> dp dc j", dp=128)
                dstv = w1ring[:, slot, 0:2048].rearrange("p (dc j) -> p dc j", dc=2)
                ldep = w1q_load(slot, dstv, src_, ("PC", attn_cast["wo"]) if wv == 0 else None)
                em.waitd("t", ldep)
                em.waitd("t", af)
                tc = None
                for dcr in range(2):
                    dc = qt * 2 + dcr
                    for sub in range(NSUB):
                        for jh in range(2):
                            b = sub * 2 + jh
                            if dc == 0:
                                pe_bank_acquire(b)
                            tc = em.op("t", lambda h, dstv=dstv, dcr=dcr, dc=dc, sub=sub, jh=jh, b=b: h.matmul(
                                pb[b][:, :], attn_fm[:, dc, sub * 128:(sub + 1) * 128],
                                dstv[:, dcr, jh * 512:(jh + 1) * 512],
                                start=(dc == 0), stop=False),
                                inc="T" if (dcr == 1 and sub == NSUB - 1 and jh == 1) else None)
                w1r_free[slot] = ("T", tc)
            for sub in range(NSUB):
                for jh in range(2):
                    b = sub * 2 + jh
                    tc = em.op("t", lambda h, jh=jh, b=b: h.matmul(
                        pb[b][:, :], ones1b[:, :], bias_sb[:, 1, jh * 512:(jh + 1) * 512],
                        start=False, stop=True), inc="T")
                    bank_fill[b] = ("T", tc)
            for sub in range(NSUB):
                gtb = wv * WTB + sub
                for jh in range(2):
                    b = sub * 2 + jh
                    evac_bank_wait("v", b)
                    ev = em.op("v", lambda h, b=b, gtb=gtb, jh=jh: h.tensor_add(
                        x[:, gtb, jh * 512:(jh + 1) * 512], pb[b][:, :],
                        x[:, gtb, jh * 512:(jh + 1) * 512]), inc="V")
                    bank_free[b] = ("V", ev)

        pend_wo = [None]

        for w in range(NW):
            xn_dep = xn_window(w, stats_dep)
            for qt in range(4):
                slot = aq[0] % 2
                aq[0] += 1
                src = wqs[qt * 256:(qt + 1) * 256, :].rearrange("(dc dp) j -> dp dc j", dp=128)
                dstv = w1ring[:, slot, 0:2048].rearrange("p (dc j) -> p dc j", dc=2)
                ldep = w1q_load(slot, dstv, src, ("PC", attn_cast["wq"]) if w == 0 else None)
                em.waitd("t", ldep)
                em.waitd("t", xn_dep)
                tc = None
                for dcr in range(2):
                    dc = qt * 2 + dcr
                    for sub in range(NSUB):
                        for jh in range(2):
                            b = sub * 2 + jh
                            if dc == 0:
                                pe_bank_acquire(b)
                            tc = em.op("t", lambda h, dstv=dstv, dcr=dcr, dc=dc, sub=sub, jh=jh, b=b: h.matmul(
                                pb[b][:, :], xn_fm[:, dc, sub * 128:(sub + 1) * 128],
                                dstv[:, dcr, jh * 512:(jh + 1) * 512],
                                start=(dc == 0), stop=False),
                                inc="T" if (dcr == 1 and sub == NSUB - 1 and jh == 1) else None)
                w1r_free[slot] = ("T", tc)
            em.wait("t", "MI", bias2_c)
            for sub in range(NSUB):
                for jh in range(2):
                    b = sub * 2 + jh
                    tc = em.op("t", lambda h, jh=jh, b=b: h.matmul(
                        pb[b][:, :], ones1b[:, :], bias_sb[:, 0, jh * 512:(jh + 1) * 512],
                        start=False, stop=True), inc="T")
                    bank_fill[b] = ("T", tc)
            for sub in range(NSUB):
                for jh in range(2):
                    b = sub * 2 + jh
                    evac_bank_wait("v", b)
                    em.chain("v", lambda h, b=b: h.tensor_scalar(
                        s12[:, 0:512], pb[b][:, :], 0.0, None, ALU.min))
                    vc = em.op("v", lambda h, b=b: h.tensor_scalar(
                        s12[:, 512:1024], pb[b][:, :], 0.0, None, ALU.max), inc="V")
                    em.wait("s", "V", vc)
                    sc = em.op("s", lambda h: h.activation(
                        s12[:, 0:512], s12[:, 0:512], AF.Exp), inc="S")
                    em.wait("v", "S", sc)
                    ev = em.op("v", lambda h, sub=sub, jh=jh: h.tensor_add(
                        q_tm[:, sub, jh * 512:(jh + 1) * 512],
                        s12[:, 0:512], s12[:, 512:1024]), inc="V")
                    bank_free[b] = ("V", ev)
            if pend_wo[0] is not None and pend_wo[0][0] == w - 1:
                emit_wo(*pend_wo[0])
                pend_wo[0] = None
            em.wait("v", "V", ksbc_c)
            qz_c = None
            for tb in range(WTB):
                em.chain("v", lambda h, tb=tb: h.tensor_tensor(
                    s12[:, :].rearrange("p (a c) -> p a c", a=16),
                    q_tm[:, tb, :].rearrange("p (a c) -> p a c", a=16),
                    ksum_bc[:, :].rearrange("p (a c) -> p a c", a=16), ALU.mult))
                em.chain("v", lambda h, tb=tb: h.tensor_reduce(
                    zdot[:, tb * H:(tb + 1) * H],
                    s12[:, :].rearrange("p (a c) -> p a c", a=16),
                    AX.X, ALU.add))
            em.chain("v", lambda h: h.tensor_scalar(
                zdot[:, :], zdot[:, :], 1e-6, None, ALU.add))
            em.chain("v", lambda h: h.reciprocal(zval[:, :], zdot[:, :]))
            for tb in range(WTB):
                zb = zval[:, tb * H:(tb + 1) * H].rearrange(
                    "p (a o) -> p a o", o=1).broadcast_to([128, H, DH])
                if tb == 0:
                    qz_c = em.chain("v", lambda h, tb=tb, zb=zb: h.tensor_tensor(
                        q_tm[:, tb, :].rearrange("p (a c) -> p a c", a=16),
                        q_tm[:, tb, :].rearrange("p (a c) -> p a c", a=16), zb, ALU.mult))
                else:
                    qz_c = em.op("v", lambda h, tb=tb, zb=zb: h.tensor_tensor(
                        q_tm[:, tb, :].rearrange("p (a c) -> p a c", a=16),
                        q_tm[:, tb, :].rearrange("p (a c) -> p a c", a=16), zb, ALU.mult), inc="V")
            em.wait("t", "MI", kvbf_d)
            sc_banks = [2, 3, 4, 5, 6, 7]
            sc_i = [0]
            for jch in range(8):
                ring = jch % 2
                em.wait("t", "V", qz_c)
                b = ring
                pe_bank_acquire(b)
                tc = None
                for tb in range(WTB):
                    tc = em.op("t", lambda h, tb=tb, jch=jch, b=b: h.transpose(
                        pb[b][:, :].bitcast(BF16)[:, 0:512].rearrange("p (a c) -> p a c", a=4)[:, tb, :],
                        q_tm[:, tb, jch * 128:(jch + 1) * 128], id_bf[:, :]),
                        inc="T" if tb == WTB - 1 else None)
                bank_fill[b] = ("T", tc)
                evac_bank_wait("v", b)
                ev = em.op("v", lambda h, b=b, ring=ring: h.tensor_copy(
                    qz_fm[ring][:, :].rearrange("p (a c) -> p a c", a=4),
                    pb[b][:, :].bitcast(BF16)[:, 0:512].rearrange("p (a c) -> p a c", a=4)), inc="V")
                bank_free[b] = ("V", ev)
                for hr in range(2):
                    hh = jch * 2 + hr
                    sb = sc_banks[sc_i[0] % 6]
                    sc_i[0] += 1
                    pe_bank_acquire(sb)
                    em.wait("t", "V", ev)
                    tp = (64, 64) if hr == 1 else None
                    tc = em.op("t", lambda h, hh=hh, hr=hr, ring=ring, sb=sb, tp=tp: h.matmul(
                        pb[sb][hr * 64:(hr + 1) * 64, :],
                        kv_bf[hr * 64:(hr + 1) * 64, hh // 2, :],
                        qz_fm[ring][hr * 64:(hr + 1) * 64, :], start=True, stop=True,
                        tile_position=tp), inc="T")
                    bank_fill[sb] = ("T", tc)
                    evac_bank_wait("v", sb)
                    ev2 = em.op("v", lambda h, hh=hh, hr=hr, jch=jch, sb=sb: h.tensor_copy(
                        attn_fm[hr * 64:(hr + 1) * 64, jch, :],
                        pb[sb][hr * 64:(hr + 1) * 64, :]), inc="V")
                    bank_free[sb] = ("V", ev2)
            af_dep = ("V", em.cnt["V"])
            pend_wo[0] = (w, af_dep)
        emit_wo(*pend_wo[0])
        pend_wo[0] = None
        return ("V", em.cnt["V"])

    dep = ("XL", 16)
    pc_done = [0]

    def _hook():
        precast_rest()
        pc_done[0] = em.cnt["PC"]
    if "ffn1" in STAGES:
        dep = ffn_call(dep, post_ag=_hook)
    else:
        _hook()
    if "attn" in STAGES:
        dep = attn_call(dep)
    if "ffn2" in STAGES:
        dep = ffn_call(dep)

    for w in range(NW):
        em.waitd("y", dep)
        em.op("y", lambda h, w=w: h.dma_start(
            out=out[w * TWIN:(w + 1) * TWIN, :].rearrange("(tb tp) d -> tp tb d", tp=128),
            in_=x[:, w * WTB:(w + 1) * WTB, :]), dmainc="ST")
    em.wait("y", "ST", em.cnt["ST"])
    if DEBUG:
        em.wait("y", "DBG", em.cnt.get("DBG", 0))

    with em._es:
        with nc.Block() as block:
            @block.tensor
            def _(h):
                for fn in em.q["t"]:
                    fn(h)

            @block.scalar
            def _(h):
                for fn in em.q["s"]:
                    fn(h)

            @block.vector
            def _(h):
                for fn in em.q["v"]:
                    fn(h)

            @block.gpsimd
            def _(h):
                for fn in em.q["g"]:
                    fn(h)

            @block.sync
            def _(h):
                for fn in em.q["y"]:
                    fn(h)
    return nc


_NC_CACHE = {}


def kernel(L0_attn_x, L0_attn_wq, L0_attn_bq, L0_attn_wk, L0_attn_bk,
           L0_attn_wv, L0_attn_bv, L0_attn_wo, L0_attn_bo,
           L0_ffn_Wg, L0_ffn_W1, L0_ffn_W2):
    xfull = np.ascontiguousarray(np.asarray(L0_attn_x, dtype=np.float32))
    shared = dict(
        w1=np.ascontiguousarray(np.asarray(L0_ffn_W1, dtype=np.float32)),
        w2=np.ascontiguousarray(np.asarray(L0_ffn_W2, dtype=np.float32)),
        wg=np.ascontiguousarray(np.asarray(L0_ffn_Wg, dtype=np.float32)),
        wq=np.ascontiguousarray(np.asarray(L0_attn_wq, dtype=np.float32)),
        wk=np.ascontiguousarray(np.asarray(L0_attn_wk, dtype=np.float32)),
        wv=np.ascontiguousarray(np.asarray(L0_attn_wv, dtype=np.float32)),
        wo=np.ascontiguousarray(np.asarray(L0_attn_wo, dtype=np.float32)),
        bq=np.ascontiguousarray(np.asarray(L0_attn_bq, dtype=np.float32)).reshape(1, D),
        bk=np.ascontiguousarray(np.asarray(L0_attn_bk, dtype=np.float32)).reshape(1, D),
        bv=np.ascontiguousarray(np.asarray(L0_attn_bv, dtype=np.float32)).reshape(1, D),
        bo=np.ascontiguousarray(np.asarray(L0_attn_bo, dtype=np.float32)).reshape(1, D),
        identb=np.eye(128, dtype=ml_dtypes.bfloat16),
        identf=np.eye(128, dtype=np.float32),
        onesb=np.ones((1, 128), dtype=ml_dtypes.bfloat16),
        onesf=np.ones((1, 128), dtype=np.float32),
        ones128=np.ones((128, 128), dtype=ml_dtypes.bfloat16),
        onescol=np.ones((128, 1), dtype=ml_dtypes.bfloat16),
    )
    in_maps = []
    for c in range(NCORES):
        b, half = c // 2, c % 2
        shard = np.ascontiguousarray(xfull[b, half * NT:(half + 1) * NT, :])
        in_maps.append(dict(xin=shard, **shared))

    key = STAGES
    if key not in _NC_CACHE:
        _NC_CACHE[key] = build_nc()
    nc = _NC_CACHE[key]
    res = run_bass_kernel_spmd(nc, in_maps, core_ids=list(range(NCORES)))
    outp = np.empty((4, 8192, D), dtype=np.float32)
    for c in range(NCORES):
        b, half = c // 2, c % 2
        outp[b, half * NT:(half + 1) * NT, :] = np.asarray(res.results[c]["out"])
    if DEBUG:
        global DEBUG_RES
        DEBUG_RES = [{k: np.asarray(v) for k, v in r.items() if k.startswith("d_")}
                     for r in res.results]
    return outp
